# version 10
# baseline (speedup 1.0000x reference)
import contextlib
import itertools
import numpy as np
import concourse.bass as bass
import concourse.mybir as mybir
from concourse.bass_utils import run_bass_kernel_spmd

F32 = mybir.dt.float32
BF16 = mybir.dt.bfloat16
AF = mybir.ActivationFunctionType
ALU = mybir.AluOpType
AX = mybir.AxisListType

T = 2048
DM = 2048
DI = 4096
NH = 64
HD = 64
NS = 128
CH = 64
NCK = T // CH
HF = 8192
NTT = T // 128
DIP_SSD = 4544
DIP = DIP_SSD + DI

MROUND = 12582912.0
MASKNEG = -28672.0
LN_EPS = 1e-5
RMS_EPS = 1e-5
QMAX = 7.0
QEPS = 1e-8
THRESH = 0.05

DEBUG_TAPS = ()
BUILD_PHASES = 99

ENGN = ("pe", "dv", "ac", "pl", "sp")
_DECN = {"pe": "tensor", "dv": "vector", "ac": "scalar", "pl": "gpsimd", "sp": "sync"}


class _Step:
    __slots__ = ("eng", "emit", "waits", "sem", "val", "isdma", "sedv")


class PB:
    NQSEM = 16

    def __init__(self, nc, E):
        self.nc = nc
        self.eng = {"pe": nc.tensor, "dv": nc.vector, "ac": nc.scalar,
                    "pl": nc.gpsimd, "sp": nc.sync}
        self.sem = {e: E(nc.semaphore(f"s_{e}")) for e in ENGN}
        self.qsem = {e: [E(nc.semaphore(f"q_{e}{i}")) for i in range(self.NQSEM)]
                     for e in ("sp", "pl", "ac")}
        self.cnt = {e: 0 for e in ENGN}
        self.qcnt = {e: 0 for e in ("sp", "pl", "ac")}
        self.steps = {e: [] for e in ENGN}
        self.last = {e: None for e in ENGN}

    def _mkwaits(self, eng, deps):
        waits = {}
        for d in deps:
            if d is None:
                continue
            if (not d.isdma) and d.eng == eng:
                continue
            key = id(d.sem)
            if key not in waits or waits[key][1] < d.val:
                waits[key] = (d.sem, d.val)
        return list(waits.values())

    def comp(self, eng, emit, deps=()):
        st = _Step()
        st.eng = eng
        st.emit = emit
        st.isdma = False
        st.waits = self._mkwaits(eng, deps)
        st.sedv = max((d.val for d in deps
                       if d is not None and (not d.isdma) and d.eng == eng),
                      default=0)
        self.cnt[eng] += 1
        st.sem = self.sem[eng]
        st.val = self.cnt[eng]
        self.steps[eng].append(st)
        self.last[eng] = st
        return st

    def dma(self, eng, out, in_, deps=(), **kw):
        assert eng in ("sp", "pl", "ac")
        st = _Step()
        st.eng = eng
        st.isdma = True
        j = self.qcnt[eng]
        self.qcnt[eng] = j + 1
        qs = self.qsem[eng][j % self.NQSEM]
        uses = j // self.NQSEM
        st.sem = qs
        st.val = 16 * (uses + 1)

        def emit(e, out=out, in_=in_, kw=kw, qs=qs, uses=uses):
            if uses:
                e.wait_ge(qs, 16 * uses)
            e.dma_start(out=out, in_=in_, **kw).then_inc(qs, 16)
        st.emit = emit
        st.waits = self._mkwaits(eng, deps)
        self.steps[eng].append(st)
        self.last[eng] = st
        return st

    def _qsem_state(self):
        ws = []
        for qe in ("sp", "pl", "ac"):
            n = self.qcnt[qe]
            for i in range(self.NQSEM):
                uses = (n - i + self.NQSEM - 1) // self.NQSEM if n > i else 0
                if uses > 0:
                    ws.append((self.qsem[qe][i], 16 * uses))
        return ws

    def barrier(self):
        lasts = [self.last[e] for e in ENGN if self.last[e] is not None]
        qws = self._qsem_state()
        steps = [self.comp(e, lambda _: None, deps=lasts) for e in ENGN]
        for st in steps:
            seen = {id(s) for (s, _) in st.waits}
            for (s, v) in qws:
                if id(s) not in seen:
                    st.waits.append((s, v))
        return steps

    def finalize(self, block):
        finals = []
        for qe in ("sp", "pl", "ac"):
            n = self.qcnt[qe]
            for i in range(self.NQSEM):
                uses = (n - i + self.NQSEM - 1) // self.NQSEM if n > i else 0
                if uses > 0:
                    finals.append((self.qsem[qe][i], 16 * uses))
        for ename in ENGN:
            steps = self.steps[ename]
            esem = self.sem[ename]

            def mk(steps=steps, esem=esem, ename=ename):
                def f(e):
                    wm = {}
                    cover = 0
                    lval = 0
                    for st in steps:
                        for (s, v) in st.waits:
                            k = id(s)
                            if wm.get(k, -1) >= v:
                                continue
                            e.wait_ge(s, v)
                            wm[k] = v
                        if (not st.isdma) and ename != "pe" \
                                and st.sedv > cover:
                            e.drain()
                            cover = lval
                        ret = st.emit(e)
                        if not st.isdma:
                            if ret is not None:
                                ret.then_inc(esem, 1)
                            else:
                                e.drain().then_inc(esem, 1)
                                cover = st.val
                            lval = st.val
                    if ename == "sp":
                        for (s, v) in finals:
                            if wm.get(id(s), -1) < v:
                                e.wait_ge(s, v)
                return f
            getattr(block, _DECN[ename])(mk())


def _dram(nc, dram, name, shape, dtype, kind):
    dram[name] = nc.dram_tensor(name, list(shape), dtype, kind=kind).ap()
    return dram[name]


def build_main(nc: bass.Bass):
    dram = {}
    din = lambda n, s, d=F32: _dram(nc, dram, n, s, d, "ExternalInput")
    dout = lambda n, s, d=F32: _dram(nc, dram, n, s, d, "ExternalOutput")
    dint = lambda n, s, d=F32: _dram(nc, dram, n, s, d, "Internal")

    x_in = din("x", (T, DM))
    winT = din("winT", (DM, DIP))
    woutT = din("woutT", (DI, DM))
    w1T = din("w1T", (DM, HF))
    w2T = din("w2T", (HF, DM))
    b1c_in = din("b1c", (128, HF // 128))
    b2row = din("b2row", (1, DM))
    wrow = din("wrow", (1, DI))
    drow_in = din("drow", (1, DI))
    dtb_row = din("dtb_row", (1, NCK * NH))
    alog_row = din("alog_row", (1, NCK * NH))
    bwe_row = din("bwe_row", (1, NCK * HD))
    bwo_row = din("bwo_row", (1, NCK * HD))
    cwe_row = din("cwe_row", (1, NCK * HD))
    cwo_row = din("cwo_row", (1, NCK * HD))
    ident_in = din("ident", (128, 128), BF16)
    identf_in = din("identf", (128, 128), F32)
    oh_hl_in = din("onehot_hl", (NH, NH * CH))
    oh_l_in = din("onehot_l", (CH, NH * CH), BF16)
    maskB_in = din("maskB", (CH, 128), BF16)
    tril_in = din("tril64", (CH, CH), BF16)
    triui_in = din("triu_inc", (CH, CH))
    triux_in = din("triu_exc32", (NCK, NCK))
    ones_in = din("ones_r", (1, 128))

    y2f_out = dout("y2f", (T, DM))
    lmax_out = dout("lmax", (1, 1))
    rstd_out = dout("rstd_all", (128, NTT))
    s_out = dout("s_raw", (T, DM), BF16)

    xs_hbm = dint("xs_hbm", (T, DI), BF16)
    xnT_hbm = dint("xnT_hbm", (128, 16 * T), BF16)
    bc_hbm = dint("bc_hbm", (T, 448), F32)
    scrA = dint("scrA", (1, 2048), F32)
    scrB = dint("scrB", (NCK, HD), F32)
    scrC = dint("scrC", (CH, CH), F32)
    scrD = dint("scrD", (128, 1), F32)
    scrE = dint("scrE", (1, 8), F32)
    z_hbm = dint("z_hbm", (T, DI), BF16)
    yd_hbm = dint("yd_hbm", (T, DI), BF16)
    yo_hbm = dint("yo_hbm", (T, DI), BF16)
    t1_hbm = dint("t1_hbm", (DM, HF), BF16)
    t2_hbm = dint("t2_hbm", (HF, DM), BF16)
    h_hbm = dint("h_hbm", (HF, T), BF16)
    ynT_hbm = dint("ynT_hbm", (NTT, 128, DI), BF16)

    ctx = contextlib.ExitStack()
    E = ctx.enter_context
    with ctx:
        pb = PB(nc, E)
        taps = []

        def tap(name, ap_, deps):
            if name in DEBUG_TAPS:
                o = dout("tap_" + name, list(ap_.shape), ap_.dtype)
                taps.append(pb.dma("sp", o, ap_, deps=deps))

        identsb = E(nc.sbuf_tensor("identsb", [128, 128], BF16))
        consts = E(nc.sbuf_tensor("consts", [128, 512], F32))
        trilb = E(nc.sbuf_tensor("trilb", [CH, CH], BF16))
        maskBb = E(nc.sbuf_tensor("maskBb", [CH, 128], BF16))
        mcol = E(nc.sbuf_tensor("mcol", [128, 64], F32))
        b2rep = E(nc.sbuf_tensor("b2rep", [128, DM], F32))
        wscr_sc = E(contextlib.ExitStack())
        WE = wscr_sc.enter_context
        wmt = {m: [WE(nc.sbuf_tensor(f"{m}t{i}", [128, 2048], F32))
                   for i in range(3)] for m in ("w1", "w2")}
        wtb = {m: [WE(nc.sbuf_tensor(f"{m}b{i}", [128, 2048], BF16))
                   for i in range(2)] for m in ("w1", "w2")}
        wqs = WE(nc.sbuf_tensor("wqs", [128, 2048], F32))
        wtr = WE(nc.sbuf_tensor("wtr", [128, 2048], F32))
        wprt = WE(nc.sbuf_tensor("wprt", [128, 520], F32))
        wrr = WE(nc.sbuf_tensor("wrr", [1, 512], F32))
        wones = WE(nc.sbuf_tensor("wones", [128, 128], F32))
        xnt_sc = E(contextlib.ExitStack())
        xnT = xnt_sc.enter_context(nc.sbuf_tensor("xnT", [128, 16 * T], BF16))
        block = E(nc.Block())

        c_id = pb.dma("sp", identsb[:], ident_in)
        c_tril = pb.dma("sp", trilb[:], tril_in)
        c_mb = pb.dma("sp", maskBb[:], maskB_in)
        c_tri = pb.dma("sp", consts[0:CH, 64:128], triui_in)
        c_trx = pb.dma("sp", consts[0:NCK, 128:160], triux_in)
        c_on = pb.dma("sp", consts[0:1, 160:288], ones_in)
        c_on32 = pb.dma("sp", consts[32:33, 160:288], ones_in)
        c_on64 = pb.dma("sp", consts[64:65, 160:288], ones_in)
        s_non = pb.comp("dv", lambda v: v.tensor_scalar(
            consts[0:1, 288:416], consts[0:1, 160:288], -1.0, None, ALU.mult),
            deps=[c_on])
        c_eps = pb.comp("dv", lambda v: v.memset(consts[:, 416:417], LN_EPS))
        EPSC = consts[:, 416:417]
        ONES = consts[0:1, 160:288]
        ONES32 = consts[32:33, 160:288]
        ONES64 = consts[64:65, 160:288]
        NEGONES = consts[0:1, 288:416]
        TRIUI = consts[0:CH, 64:128]
        TRIUX = consts[0:NCK, 128:160]
        c_onall = [c_on, c_on32, c_on64]

        WQ = "ac"
        WCFG = {
            "w1": dict(wT=w1T, thbm=t1_hbm, icol=0, acol=2,
                       tc=0, mc0=8, rc1=136, rc2=200, cc0=264, s0=0, e0=0,
                       a0=16, aj=0),
            "w2": dict(wT=w2T, thbm=t2_hbm, icol=1, acol=3,
                       tc=1, mc0=72, rc1=328, rc2=392, cc0=456, s0=8, e0=2,
                       a0=24, aj=1),
        }
        wmx = {"w1": {}, "w2": {}}
        wcons = {"w1": {}, "w2": {}}
        wst = {"w1": {}, "w2": {}}
        wdelta = {}

        WQSEL = ["ac"]

        def wmax_gen(m):
            c = WCFG[m]
            wT = c["wT"]
            nn = wT.shape[1] // 2048
            for i in range(64):
                kt, nt = divmod(i, nn)
                b = i % 3
                ld = pb.dma(WQSEL[0], wmt[m][b][:],
                            wT[128 * kt:128 * (kt + 1), 2048 * nt:2048 * (nt + 1)],
                            deps=[wmx[m].get(i - 3)])
                wmx[m][i] = pb.comp("dv", (lambda v, b=b, co=c["mc0"] + i, m=m:
                    v.reduce_max(wprt[:, co:co + 1], wmt[m][b][:], AX.X,
                                 apply_absolute_value=True)), deps=[ld])
                yield

        def wdfin_gen():
            prev_rd = None
            prev_gm = None
            for m in ("w1", "w2"):
                c = WCFG[m]
                s0 = c["s0"]
                red = pb.comp("dv", (lambda v, c=c: v.reduce_max(
                    wprt[:, c["tc"]:c["tc"] + 1], wprt[:, c["mc0"]:c["mc0"] + 64],
                    AX.X)), deps=[wmx[m][63]])
                d0 = pb.dma(WQ, scrD, wprt[:, c["tc"]:c["tc"] + 1],
                            deps=[red, prev_rd])
                d1 = pb.dma(WQ, wrr[0:1, 128:256], scrD.rearrange("p o -> o p"),
                            deps=[d0, prev_gm])
                prev_rd = d1
                gm = pb.comp("dv", (lambda v, s0=s0: v.reduce_max(
                    wrr[0:1, s0:s0 + 1], wrr[0:1, 128:256], AX.X)), deps=[d1])
                prev_gm = gm
                dl = pb.comp("dv", (lambda v, s0=s0: v.tensor_scalar(
                    wrr[0:1, s0 + 1:s0 + 2], wrr[0:1, s0:s0 + 1], THRESH, None,
                    ALU.mult)), deps=[gm])
                v2 = pb.comp("dv", (lambda v, s0=s0: v.tensor_scalar(
                    wrr[0:1, s0 + 2:s0 + 3], wrr[0:1, s0 + 1:s0 + 2], 2.0, None,
                    ALU.mult)), deps=[dl])
                r2_ = pb.comp("dv", (lambda v, s0=s0: v.reciprocal(
                    wrr[0:1, s0 + 2:s0 + 3], wrr[0:1, s0 + 2:s0 + 3])), deps=[v2])
                ngd = pb.comp("dv", (lambda v, s0=s0: v.tensor_scalar(
                    wrr[0:1, s0 + 3:s0 + 4], wrr[0:1, s0 + 1:s0 + 2], -1.0, None,
                    ALU.mult)), deps=[dl])
                d2 = pb.dma(WQ, scrE[0:1, c["e0"]:c["e0"] + 2],
                            wrr[0:1, s0 + 2:s0 + 4], deps=[r2_, ngd])
                b_iv = pb.dma(WQ, mcol[:, c["icol"]:c["icol"] + 1],
                              scrE[0:1, c["e0"]:c["e0"] + 1].broadcast_to([128, 1]),
                              deps=[d2])
                b_ng = pb.dma(WQ, mcol[:, c["icol"] + 4:c["icol"] + 5],
                              scrE[0:1, c["e0"] + 1:c["e0"] + 2].broadcast_to([128, 1]),
                              deps=[d2])
                wdelta[m] = (b_iv, b_ng)
                yield

        def wbuild_gen(m):
            c = WCFG[m]
            wT = c["wT"]
            thbm = c["thbm"]
            nn = wT.shape[1] // 2048
            jmap = {0: 63, 1: 61, 2: 62}
            for i in range(64):
                kt, nt = divmod(i, nn)
                b = i % 3
                ld = pb.dma(WQSEL[0], wmt[m][b][:],
                            wT[128 * kt:128 * (kt + 1), 2048 * nt:2048 * (nt + 1)],
                            deps=(wcons[m][i - 3] if i >= 3
                                  else [wmx[m][jmap[b]]]))
                u1 = pb.comp("dv", (lambda v, b=b, m=m, ic=c["icol"]:
                    v.tensor_scalar(wqs[:], wmt[m][b][:], mcol[:, ic:ic + 1],
                                    MROUND, ALU.mult, ALU.add)),
                    deps=[ld, wdelta[m][0]])
                u2 = pb.comp("dv", lambda v: v.tensor_scalar(
                    wqs[:], wqs[:], -MROUND, -1.0, ALU.add, ALU.max), deps=[u1])
                tb_ = wtb[m][i % 2]
                u3 = pb.comp("dv", (lambda v, tb_=tb_: v.tensor_scalar_min(
                    tb_[:], wqs[:], 1.0)), deps=[u2, wst[m].get(i - 2)])
                u4 = pb.comp("dv", (lambda v, tb_=tb_, co=c["cc0"] + i:
                    v.reduce_sum(wprt[:, co:co + 1], tb_[:], AX.X,
                                 apply_absolute_value=True)), deps=[u3])
                r1 = pb.comp("ac", (lambda s, b=b, m=m, co=c["rc1"] + i,
                                    ic=c["icol"]: s.activation(
                    wtr[:], wmt[m][b][:], AF.Relu, bias=mcol[:, ic + 4:ic + 5],
                    scale=1.0, accum_out=wprt[:, co:co + 1])),
                    deps=[ld, wdelta[m][1]])
                r2 = pb.comp("ac", (lambda s, b=b, m=m, co=c["rc2"] + i,
                                    ic=c["icol"]: s.activation(
                    wtr[:], wmt[m][b][:], AF.Relu, bias=mcol[:, ic + 4:ic + 5],
                    scale=-1.0, accum_out=wprt[:, co:co + 1])), deps=[r1])
                wcons[m][i] = [u1, r2]
                dst = thbm[128 * kt:128 * (kt + 1), 2048 * nt:2048 * (nt + 1)]
                wst[m][i] = pb.dma("pl", dst, tb_[:], deps=[u3])
                yield

        def _ilv(*gens):
            gens = list(gens)
            while gens:
                for g in gens[:]:
                    try:
                        next(g)
                    except StopIteration:
                        gens.remove(g)
                        continue
                    yield

        wchain = itertools.chain(
            _ilv(wmax_gen("w1"), wmax_gen("w2")), wdfin_gen(),
            _ilv(wbuild_gen("w1"), wbuild_gen("w2")))

        def wpump(n=1, q="ac"):
            WQSEL[0] = q
            for _ in range(n):
                if next(wchain, None) is None:
                    return

        def walpha_emit(psW):
            oc = pb.comp("dv", lambda v: v.memset(wones[:, :], 1.0))
            prev_pe = []
            for m in ("w1", "w2"):
                c = WCFG[m]
                aj, ic = c["aj"], c["icol"]
                lastac = wcons[m][63][1]
                s1 = pb.comp("dv", (lambda v, c=c: v.reduce_sum(
                    wprt[:, 2:3], wprt[:, c["rc1"]:c["rc1"] + 64], AX.X)),
                    deps=[lastac] + prev_pe)
                s2 = pb.comp("dv", (lambda v, c=c: v.reduce_sum(
                    wprt[:, 3:4], wprt[:, c["rc2"]:c["rc2"] + 64], AX.X)),
                    deps=[lastac, s1])
                s3 = pb.comp("dv", (lambda v, c=c: v.reduce_sum(
                    wprt[:, 4:5], wprt[:, c["cc0"]:c["cc0"] + 64], AX.X)),
                    deps=[s2])
                a12 = pb.comp("dv", lambda v: v.tensor_add(
                    wprt[:, 2:3], wprt[:, 2:3], wprt[:, 3:4]), deps=[s1, s2])
                f_r = pb.comp("pe", (lambda t, aj=aj: t.matmul(
                    psW[:, 4 * aj:4 * aj + 1], wones[:, :], wprt[:, 2:3],
                    start=True, stop=True)), deps=[a12, oc])
                f_c = pb.comp("pe", (lambda t, aj=aj: t.matmul(
                    psW[:, 4 * aj + 1:4 * aj + 2], wones[:, :], wprt[:, 4:5],
                    start=True, stop=True)), deps=[s3, oc, f_r])
                prev_pe = [f_r, f_c]
                t_ = pb.comp("dv", (lambda v, aj=aj, ic=ic: v.tensor_scalar(
                    wprt[:, 5:6], psW[:, 4 * aj + 1:4 * aj + 2],
                    mcol[:, ic + 4:ic + 5], None, ALU.mult)),
                    deps=[f_c, wdelta[m][1]])
                ms_ = pb.comp("dv", (lambda v, aj=aj: v.tensor_sub(
                    wprt[:, 5:6], psW[:, 4 * aj:4 * aj + 1], wprt[:, 5:6])),
                    deps=[t_, f_r])
                cn_ = pb.comp("dv", (lambda v, aj=aj: v.tensor_scalar(
                    wprt[:, 6:7], psW[:, 4 * aj + 1:4 * aj + 2],
                    1.0, None, ALU.max)), deps=[f_c])
                rc_ = pb.comp("dv", lambda v: v.reciprocal(
                    wprt[:, 6:7], wprt[:, 6:7]), deps=[cn_])
                al_ = pb.comp("dv", lambda v: v.tensor_mul(
                    wprt[:, 7:8], wprt[:, 5:6], wprt[:, 6:7]), deps=[ms_, rc_])
                cp_ = pb.comp("dv", (lambda v, c=c: v.tensor_copy(
                    mcol[:, c["acol"]:c["acol"] + 1], wprt[:, 7:8])),
                    deps=[al_])

        wpump(2)

        with contextlib.ExitStack() as S:
            xt = [S.enter_context(nc.sbuf_tensor(f"p0x{i}", [128, DM], F32)) for i in range(2)]
            xnb = [S.enter_context(nc.sbuf_tensor(f"p0n{i}", [128, DM], BF16)) for i in range(2)]
            trash = S.enter_context(nc.sbuf_tensor("p0t", [128, DM], BF16))
            cols = S.enter_context(nc.sbuf_tensor("p0c", [128, 16], F32))
            pt = [S.enter_context(nc.psum_tensor(f"p0p{i}", [128, 128], BF16)) for i in range(2)]

            done_tr = {}
            xn_done = [None, None]
            for tt in range(NTT):
                b = tt % 2
                co = 8 * (tt % 2)
                ld = pb.dma("sp", xt[b][:], x_in[128 * tt:128 * (tt + 1), :],
                            deps=[xn_done[b]])
                sq = pb.comp("ac", (lambda s, b=b, co=co: s.activation(
                    trash[:], xt[b][:], AF.Square, accum_out=cols[:, co + 1:co + 2])),
                    deps=[ld])
                sm = pb.comp("dv", (lambda v, b=b, co=co: v.reduce_sum(
                    cols[:, co:co + 1], xt[b][:], AX.X)), deps=[ld])
                nm = pb.comp("dv", (lambda v, co=co: v.tensor_scalar(
                    cols[:, co + 2:co + 3], cols[:, co:co + 1], -1.0 / DM, None, ALU.mult)),
                    deps=[sm])
                ss = pb.comp("dv", (lambda v, co=co: v.tensor_scalar(
                    cols[:, co + 1:co + 2], cols[:, co + 1:co + 2], 1.0 / DM, None, ALU.mult)),
                    deps=[sq])
                nv = pb.comp("dv", (lambda v, co=co: v.scalar_tensor_tensor(
                    cols[:, co + 3:co + 4], cols[:, co + 2:co + 3], cols[:, co + 2:co + 3],
                    cols[:, co + 1:co + 2], ALU.mult, ALU.subtract)), deps=[nm, ss])
                ln = pb.comp("ac", (lambda s, co=co: s.activation(
                    cols[:, co + 3:co + 4], cols[:, co + 3:co + 4], AF.Ln,
                    bias=EPSC, scale=-1.0)), deps=[nv, c_eps])
                rs = pb.comp("ac", (lambda s, co=co: s.activation(
                    cols[:, co + 4:co + 5], cols[:, co + 3:co + 4], AF.Exp, scale=-0.5)),
                    deps=[ln])
                xn_done[b] = pb.comp("dv", (lambda v, b=b, co=co: v.tensor_scalar(
                    xnb[b][:], xt[b][:], cols[:, co + 2:co + 3], cols[:, co + 4:co + 5],
                    ALU.add, ALU.mult)),
                    deps=[rs, nm] + ([done_tr[(tt - 2, 15)]] if tt >= 2 else [c_id]))
                for k in range(16):
                    tr = pb.comp("pe", (lambda t, b=b, k=k: t.transpose(
                        pt[k % 2][:, :], xnb[b][:, 128 * k:128 * (k + 1)], identsb[:, :])),
                        deps=[xn_done[b], done_tr.get((tt, k - 2)),
                              done_tr.get((tt - 1, 14 + k))])
                    done_tr[(tt, k)] = pb.comp("ac", (lambda s, k=k, tt=tt:
                        s.copy(xnT[:, 2048 * k + 128 * tt: 2048 * k + 128 * (tt + 1)],
                               pt[k % 2][:, :])), deps=[tr])
                wpump(2)
            p0_done = pb.barrier()
            tap("xnT", xnT[:, :], p0_done)

        if BUILD_PHASES < 1:
            pb.finalize(block)
            return dram

        with contextlib.ExitStack() as S:
            wsl = [S.enter_context(nc.sbuf_tensor(f"p1w{i}", [128, 16 * 512], BF16))
                   for i in range(2)]
            ost = [S.enter_context(nc.sbuf_tensor(f"p1o{i}", [128, 512], BF16))
                   for i in range(2)]
            obc = [S.enter_context(nc.sbuf_tensor(f"p1b{i}", [128, 448], F32))
                   for i in range(2)]
            psA = [S.enter_context(nc.psum_tensor(f"p1p{i}", [128, 512], F32))
                   for i in range(2)]
            psW = S.enter_context(nc.psum_tensor("p1wps", [128, 8], F32))
            fbounds = []
            off = 0
            while off < DIP:
                wdt = min(512, DIP_SSD - off) if off < DIP_SSD else min(512, DIP - off)
                fbounds.append((off, wdt))
                off += wdt
            wld = [None, None]
            mmd = {}
            evd = {}
            std = {}
            for fc, (off, wdt) in enumerate(fbounds):
                wld[fc % 2] = pb.dma("pl", wsl[fc % 2][:, 0:16 * wdt].rearrange(
                    "p (kt f) -> p kt f", kt=16),
                    winT[:, off:off + wdt].rearrange("(kt p) f -> p kt f", p=128),
                    deps=[mmd.get((fc - 2, NTT - 1))])
                for tt in range(NTT):
                    def mm(t, fc=fc, tt=tt, wdt=wdt):
                        ps = psA[tt % 2]
                        r = None
                        for kt in range(16):
                            r = t.matmul(ps[:, 0:wdt],
                                     xnT[:, 2048 * kt + 128 * tt: 2048 * kt + 128 * (tt + 1)],
                                     wsl[fc % 2][:, wdt * kt: wdt * (kt + 1)],
                                     start=(kt == 0), stop=(kt == 15))
                        return r
                    mmd[(fc, tt)] = pb.comp("pe", mm, deps=[
                        wld[fc % 2], evd.get((fc, tt - 2)), evd.get((fc - 1, NTT - 2 + tt))]
                        + (p0_done if fc == 0 and tt == 0 else []))
                    is_bc = (off == 4096)
                    war = (std.get((fc, tt - 2)) or []) + \
                          (std.get((fc - 1, NTT - 2 + tt)) or [])
                    if not is_bc:
                        evd[(fc, tt)] = pb.comp("ac", (lambda s, tt=tt, wdt=wdt:
                            s.copy(ost[tt % 2][:, 0:wdt], psA[tt % 2][:, 0:wdt])),
                            deps=[mmd[(fc, tt)]] + war)
                        if off + wdt <= DI:
                            dst = xs_hbm[128 * tt:128 * (tt + 1), off:off + wdt]
                        else:
                            dst = z_hbm[128 * tt:128 * (tt + 1),
                                        off - DIP_SSD:off - DIP_SSD + wdt]
                        std[(fc, tt)] = [pb.dma("sp", dst, ost[tt % 2][:, 0:wdt],
                                                deps=[evd[(fc, tt)]])]
                    else:
                        ebc = pb.comp("ac", (lambda s, tt=tt:
                            s.copy(obc[tt % 2][:, 0:448], psA[tt % 2][:, 0:448])),
                            deps=[mmd[(fc, tt)]] + war)
                        evd[(fc, tt)] = ebc
                        st1 = pb.dma("sp", bc_hbm[128 * tt:128 * (tt + 1), :],
                                     obc[tt % 2][:, 0:448], deps=[ebc])
                        std[(fc, tt)] = [st1]
                    wpump(1, "ac" if (16 * fc + tt) % 2 else "sp")
            while next(wchain, None) is not None:
                pass
            walpha_emit(psW)
            p1_done = pb.barrier()
            sp_xnt = pb.dma("sp", xnT_hbm, xnT[:, :], deps=p1_done)
            p1_done = pb.barrier()

        xnt_sc.close()
        wscr_sc.close()

        if BUILD_PHASES < 2:
            pb.finalize(block)
            return dram

        ssd_sc = E(contextlib.ExitStack())
        SE = ssd_sc.enter_context
        acs = SE(nc.sbuf_tensor("acs", [CH, NCK * NH], F32))
        nacsT = SE(nc.sbuf_tensor("nacsT", [CH, NCK * NH], F32))
        expacsD = SE(nc.sbuf_tensor("expacsD", [128, NCK * NH], BF16))
        dsstk = SE(nc.sbuf_tensor("dsstk", [128, NCK * NH], BF16))
        dtlstk = SE(nc.sbuf_tensor("dtlstk", [128, NCK * NH], BF16))
        dbc = SE(nc.sbuf_tensor("dbc", [128, NCK * NH], F32))
        bstack = SE(nc.sbuf_tensor("bstack", [128, NCK * NS], BF16))
        bnT = SE(nc.sbuf_tensor("bnT", [128, T + 64], BF16))
        cnT = SE(nc.sbuf_tensor("cnT", [128, T], BF16))
        dful = SE(nc.sbuf_tensor("dful", [128, DI], BF16))
        oh_l = SE(nc.sbuf_tensor("oh_l", [CH, NH * CH], BF16))
        identf = SE(nc.sbuf_tensor("identf_sb", [CH, CH], F32))
        wrep = SE(nc.sbuf_tensor("wrep", [128, DI], BF16))

        with contextlib.ExitStack() as S:
            tmp = [S.enter_context(nc.sbuf_tensor(f"p2t{i}", [CH, NCK * NH], F32))
                   for i in range(4)]
            sinb = S.enter_context(nc.sbuf_tensor("p2sin", [CH, NCK * HD], F32))
            cosb = S.enter_context(nc.sbuf_tensor("p2cos", [CH, NCK * HD], F32))
            ang = S.enter_context(nc.sbuf_tensor("p2ang", [CH, NCK * HD], F32))
            bcscr = S.enter_context(nc.sbuf_tensor("p2bc", [CH, NCK * NS], F32))
            bn = bstack[0:CH, :]
            cn = S.enter_context(nc.sbuf_tensor("p2cn", [CH, NCK * NS], BF16))
            msb = S.enter_context(nc.sbuf_tensor("p2mb", [CH, NCK], F32))
            msc = S.enter_context(nc.sbuf_tensor("p2mc", [CH, NCK], F32))
            wet = S.enter_context(nc.sbuf_tensor("p2we", [CH, 2048], F32))
            wot = S.enter_context(nc.sbuf_tensor("p2wo", [CH, 2048], F32))
            rows = S.enter_context(nc.sbuf_tensor("p2r", [65, 4096], F32))
            crow = S.enter_context(nc.sbuf_tensor("p2cr", [NCK, 2 * HD], F32))
            shsc = S.enter_context(nc.sbuf_tensor("p2sh", [CH, NCK * NH], BF16))
            ps4 = [S.enter_context(nc.psum_tensor(f"p2p{i}", [CH, 512], F32))
                   for i in range(4)]
            ps4b_ = [S.enter_context(nc.psum_tensor(f"p2q{i}", [CH, 512], F32))
                     for i in range(2)]
            ps4b = [ps4b_[0], ps4b_[1], ps4b_[0], ps4b_[1]]
            ps128 = S.enter_context(nc.psum_tensor("p2d", [128, 512], F32))
            pmisc = S.enter_context(nc.psum_tensor("p2s", [128, 512], F32))
            pst = pmisc[:, 0:128]
            pstb = pmisc[:, 128:192].bitcast(BF16)
            pstc = pmisc[:, 192:256].bitcast(BF16)
            ps32 = pmisc[0:NCK, 256:256 + HD]

            _SLOT = {"A": (0, 0), "B": (0, 2048), "C": (32, 0), "D": (32, 2048),
                     "E": (64, 0), "F": (64, 2048)}

            def RO(s):
                p, c = _SLOT[s]
                return rows[p:p + 1, c:c + 2048]

            def bcload(col0, wdt, dst, deps):
                return pb.dma("sp", dst.rearrange("p (c w) -> p c w", w=wdt),
                              bc_hbm[:, col0:col0 + wdt].rearrange(
                                  "(c p) w -> p c w", p=CH), deps=deps)

            c_ohl = pb.dma("sp", oh_l[:], oh_l_in, deps=p1_done)
            c_idf = pb.dma("sp", identf[:], identf_in[0:CH, 0:CH], deps=p1_done)
            l_dtb = pb.dma("sp", RO("A"), dtb_row)
            l_alog = pb.dma("sp", RO("B"), alog_row)
            l_bwe = pb.dma("sp", RO("E"), bwe_row)
            l_bwo = pb.dma("sp", RO("F"), bwo_row)

            BCH0 = bcscr[0:CH, 0:2048].rearrange("p (c w) -> p c w", w=HD)
            BCH1 = bcscr[0:CH, 2048:4096].rearrange("p (c w) -> p c w", w=HD)
            l_th = bcload(384, HD, bcscr[0:CH, 0:2048], p1_done)
            l_dtr = bcload(256, HD, bcscr[0:CH, 2048:4096], p1_done)

            mm_th = [pb.comp("pe", (lambda t, q=q: t.matmul(
                ps4[q][:, :], TRIUI, BCH0[:, 8 * q:8 * (q + 1), :],
                start=True, stop=True)), deps=[l_th, c_tri]) for q in range(4)]
            ev_ang = [pb.comp("ac", (lambda s, q=q: s.copy(
                ang[:, 512 * q:512 * (q + 1)], ps4[q][:, :])), deps=[mm_th[q]])
                for q in range(4)]
            d_sums0 = pb.dma("pl", scrA, ang[63:64, :], deps=ev_ang)
            d_sums = pb.dma("pl", crow[0:NCK, 0:HD],
                            scrA.rearrange("o (c w) -> (o c) w", w=HD),
                            deps=[d_sums0])
            mm_cs = pb.comp("pe", lambda t: t.matmul(
                ps32[:, :], TRIUX, crow[0:NCK, 0:HD], start=True, stop=True),
                deps=[d_sums, c_trx])
            ev_cs = pb.comp("dv", lambda v: v.tensor_copy(crow[0:NCK, HD:2 * HD], ps32[:, :]),
                            deps=[mm_cs])
            d_flat0 = pb.dma("pl", scrB, crow[0:NCK, HD:2 * HD], deps=[ev_cs])
            d_flat = pb.dma("pl", RO("D"), scrB.rearrange("c w -> () (c w)"),
                            deps=[d_flat0])
            mm_cb = [pb.comp("pe", (lambda t, q=q: t.matmul(
                ps4[q][:, :], ONES32[:, 0:CH], RO("D")[:, 512 * q:512 * (q + 1)],
                start=True, stop=True)), deps=[d_flat, ev_ang[q], c_on32]) for q in range(4)]
            ad_ang = [pb.comp("dv", (lambda v, q=q: v.tensor_add(
                ang[:, 512 * q:512 * (q + 1)], ang[:, 512 * q:512 * (q + 1)],
                ps4[q][:, :])), deps=[mm_cb[q]]) for q in range(4)]

            sc_done = []
            for dst, shift in ((sinb, 0.0), (cosb, np.pi / 2)):
                s1 = pb.comp("dv", (lambda v, sh=shift: v.tensor_scalar(
                    tmp[0][:, 0:2048], ang[:], sh, 1.0 / (2 * np.pi), ALU.add, ALU.mult)),
                    deps=ad_ang + sc_done)
                s2 = pb.comp("dv", lambda v: v.tensor_scalar(
                    tmp[0][:, 0:2048], tmp[0][:, 0:2048], MROUND, -MROUND, ALU.add, ALU.add),
                    deps=[s1])
                s3 = pb.comp("dv", (lambda v, sh=shift: v.tensor_scalar(
                    tmp[1][:, 0:2048], ang[:], sh, None, ALU.add)), deps=[s2])
                s4 = pb.comp("dv", (lambda v, d=dst: v.scalar_tensor_tensor(
                    d[:], tmp[0][:, 0:2048], -2 * np.pi, tmp[1][:, 0:2048],
                    ALU.mult, ALU.add)), deps=[s3])
                s4b = pb.comp("dv", (lambda v, d=dst: v.tensor_scalar(
                    d[:], d[:], -3.1415925, 3.1415925, ALU.max, ALU.min)), deps=[s4])
                s5 = pb.comp("ac", (lambda s, d=dst: s.activation(d[:], d[:], AF.Sin)),
                             deps=[s4b])
                sc_done.append(s5)

            e_al = pb.comp("ac", lambda s: s.activation(RO("B"), RO("B"), AF.Exp),
                           deps=[l_alog])
            mm_dtb = []
            v_v = []
            for q in range(4):
                mm_dtb.append(pb.comp("pe", (lambda t, q=q: t.matmul(
                    ps4b[q][:, :], ONES[:, 0:CH], RO("A")[:, 512 * q:512 * (q + 1)],
                    start=True, stop=True)),
                    deps=[l_dtb, c_on, (v_v[q - 2] if q >= 2 else None)]))
                v_v.append(pb.comp("dv", (lambda v, q=q: v.tensor_add(
                    tmp[0][:, 512 * q:512 * (q + 1)],
                    BCH1[:, 8 * q:8 * (q + 1), :].rearrange("p c w -> p (c w)"),
                    ps4b[q][:, :])), deps=[mm_dtb[q], l_dtr]))
            a_ab = pb.comp("ac", lambda s: s.activation(
                tmp[1][:, 0:2048], tmp[0][:, 0:2048], AF.Abs, scale=-1.0),
                deps=v_v + sc_done)
            a_ex = pb.comp("ac", lambda s: s.activation(
                tmp[1][:, 0:2048], tmp[1][:, 0:2048], AF.Exp, scale=-1.0), deps=[a_ab])
            a_ln = pb.comp("ac", lambda s: s.activation(
                tmp[1][:, 0:2048], tmp[1][:, 0:2048], AF.Ln, bias=1.0), deps=[a_ex])
            v_dt = pb.comp("dv", lambda v: v.scalar_tensor_tensor(
                tmp[0][:, 0:2048], tmp[0][:, 0:2048], 0.0, tmp[1][:, 0:2048],
                ALU.max, ALU.add), deps=[a_ln])
            l_lam = bcload(320, HD, bcscr[0:CH, 0:2048], mm_th)
            a_lam = pb.comp("ac", lambda s: s.activation(
                tmp[2][:, 0:2048], bcscr[0:CH, 0:2048],
                AF.Sigmoid), deps=[l_lam, a_ln])
            mm_na = []
            v_A = []
            for q in range(4):
                mm_na.append(pb.comp("pe", (lambda t, q=q: t.matmul(
                    ps4b[q][:, :], NEGONES[:, 0:CH], RO("B")[:, 512 * q:512 * (q + 1)],
                    start=True, stop=True)),
                    deps=[e_al, s_non, v_v[q], v_v[(q + 2) % 4],
                          (v_A[q - 2] if q >= 2 else None)]))
                v_A.append(pb.comp("dv", (lambda v, q=q: v.tensor_mul(
                    tmp[3][:, 512 * q:512 * (q + 1)], tmp[0][:, 512 * q:512 * (q + 1)],
                    ps4b[q][:, :])), deps=[mm_na[q], v_dt]))
            v_d1 = pb.comp("dv", lambda v: v.tensor_mul(
                tmp[1][:, 0:2048], tmp[0][:, 0:2048], tmp[2][:, 0:2048]),
                deps=[v_dt, a_lam])
            v_d1b = pb.comp("dv", lambda v: v.tensor_copy(dtlstk[0:CH, :], tmp[1][:, 0:2048]),
                            deps=[v_d1])
            v_d2 = pb.comp("dv", lambda v: v.tensor_sub(
                tmp[1][:, 0:2048], tmp[0][:, 0:2048], tmp[1][:, 0:2048]), deps=[v_d1b])
            v_d2b = pb.comp("dv", lambda v: v.tensor_copy(shsc[0:CH, :], tmp[1][:, 0:2048]),
                            deps=[v_d2])
            d_shA = pb.dma("pl", dtlstk[65:128, :], shsc[0:CH - 1, :], deps=[v_d2b])
            d_shB = pb.dma("pl",
                           dtlstk[64:65, :].rearrange("o (c w) -> o c w", w=NH)[:, 1:NCK, :],
                           shsc[CH - 1:CH, :].rearrange("o (c w) -> o c w", w=NH)[:, 0:NCK - 1, :],
                           deps=[v_d2b])
            z_sh = pb.comp("dv", lambda v: v.memset(dtlstk[64:65, 0:NH], 0.0),
                           deps=[v_d2b])

            mm_ac = [pb.comp("pe", (lambda t, q=q: t.matmul(
                ps4[q][:, :], TRIUI, tmp[3][:, 512 * q:512 * (q + 1)],
                start=True, stop=True)), deps=[v_A[q], ad_ang[q]]) for q in range(4)]
            ev_ac = [pb.comp("ac", (lambda s, q=q: s.copy(
                acs[:, 512 * q:512 * (q + 1)], ps4[q][:, :])), deps=[mm_ac[q]])
                for q in range(4)]
            ev_ea = [pb.comp("ac", (lambda s, q=q: s.activation(
                expacsD[0:CH, 512 * q:512 * (q + 1)], ps4[q][:, :], AF.Exp)),
                deps=[mm_ac[q]]) for q in range(4)]
            d_ea = pb.dma("pl", expacsD[64:128, :], expacsD[0:CH, :], deps=ev_ea)
            tr_ac = {}
            ev_na = {}
            for c in range(NCK):
                tr_ac[c] = pb.comp("pe", (lambda t, c=c: t.transpose(
                    pst[0:CH, 0:CH], acs[:, NH * c:NH * (c + 1)], identf[:, :])),
                    deps=[ev_ac[c // 8], ev_na.get(c - 1), c_idf])
                ev_na[c] = pb.comp("dv", (lambda v, c=c: v.tensor_scalar(
                    nacsT[:, NH * c:NH * (c + 1)], pst[0:CH, 0:CH], -1.0, None, ALU.mult)),
                    deps=[tr_ac[c]])
            d_end = pb.dma("pl", RO("A"), acs[63:64, :], deps=ev_ac + [mm_dtb[3]])
            e_dec = pb.comp("ac", lambda s: s.activation(RO("B"), RO("A"), AF.Exp),
                            deps=[d_end, a_lam, mm_na[3]])
            mm_en = [pb.comp("pe", (lambda t, q=q: t.matmul(
                ps4[q][:, :], ONES[:, 0:CH], RO("A")[:, 512 * q:512 * (q + 1)],
                start=True, stop=True)), deps=[d_end, ev_ea[q], ev_ac[q]])
                for q in range(4)]
            v_ds = [pb.comp("dv", (lambda v, q=q: v.tensor_sub(
                tmp[1][:, 512 * q:512 * (q + 1)], ps4[q][:, :],
                acs[:, 512 * q:512 * (q + 1)])), deps=[mm_en[q], v_d2b]) for q in range(4)]
            a_ds = pb.comp("ac", lambda s: s.activation(
                dsstk[0:CH, :], tmp[1][:, 0:2048], AF.Exp), deps=v_ds)
            d_ds2 = pb.dma("pl", dsstk[64:128, :], dsstk[0:CH, :], deps=[a_ds])
            mm_db = {}
            ev_db = {}
            for q in range(4):
                mm_db[q] = pb.comp("pe", (lambda t, q=q: t.matmul(
                    ps128[:, :], ONES[0:1, 0:128], RO("B")[:, 512 * q:512 * (q + 1)],
                    start=True, stop=True)), deps=[e_dec, ev_db.get(q - 1)])
                ev_db[q] = pb.comp("dv", (lambda v, q=q: v.tensor_copy(
                    dbc[:, 512 * q:512 * (q + 1)], ps128[:, :])), deps=[mm_db[q]])

            BCF = bcscr[0:CH, :].rearrange("p (c w) -> p c w", w=NS)

            def rope_build(ms, werow_ap, worow_ap, dst, extra):
                sqd = pb.comp("ac", (lambda s: s.activation(
                    tmp[1][:, 0:2048].rearrange("p (c w) -> p c w", w=HD),
                    BCF[:, :, 0::2], AF.Square)), deps=extra)
                sqd2 = pb.comp("ac", (lambda s: s.activation(
                    tmp[3][:, 0:2048].rearrange("p (c w) -> p c w", w=HD),
                    BCF[:, :, 1::2], AF.Square)), deps=[sqd])
                ad = pb.comp("dv", lambda v: v.tensor_add(
                    tmp[1][:, 0:2048], tmp[1][:, 0:2048], tmp[3][:, 0:2048]), deps=[sqd2])
                rd = pb.comp("dv", (lambda v, m=ms: v.tensor_reduce(
                    m[:, :].rearrange("p c -> p c ()"),
                    tmp[1][:, 0:2048].rearrange("p (c w) -> p c w", w=HD),
                    AX.X, ALU.add)), deps=[ad])
                r1 = pb.comp("ac", (lambda s, m=ms: s.activation(
                    m[:, :], m[:, :], AF.Ln, bias=EPSC[0:CH, :], scale=1.0 / 128)),
                    deps=[rd, c_eps])
                r2 = pb.comp("ac", (lambda s, m=ms: s.activation(
                    m[:, :], m[:, :], AF.Exp, scale=-0.5)), deps=[r1])
                mwA = []
                mwB = []
                ew = []
                ow = []
                for q in range(4):
                    mwA.append(pb.comp("pe", (lambda t, q=q, w=werow_ap: t.matmul(
                        ps4[q][:, :], ONES64[:, 0:CH], w[:, 512 * q:512 * (q + 1)],
                        start=True, stop=True)), deps=[v_ds[q], r2, c_on64] + extra))
                    mwB.append(pb.comp("pe", (lambda t, q=q, w=worow_ap: t.matmul(
                        ps4b[q][:, :], ONES64[:, 0:CH], w[:, 512 * q:512 * (q + 1)],
                        start=True, stop=True)),
                        deps=[v_A[q], v_A[(q + 2) % 4], r2, c_on64,
                              (ow[q - 2] if q >= 2 else None)] + extra))
                    ew.append(pb.comp("dv", (lambda v, q=q: v.tensor_copy(
                        wet[:, 512 * q:512 * (q + 1)], ps4[q][:, :])), deps=[mwA[q]]))
                    ow.append(pb.comp("dv", (lambda v, q=q: v.tensor_copy(
                        wot[:, 512 * q:512 * (q + 1)], ps4b[q][:, :])), deps=[mwB[q]]))
                ms3 = ms[:, :].rearrange("p c -> p c ()").broadcast_to([CH, NCK, HD])
                x1 = pb.comp("dv", (lambda v, m3=ms3: v.tensor_tensor(
                    tmp[1][:, 0:2048].rearrange("p (c w) -> p c w", w=HD),
                    BCF[:, :, 0::2], m3, ALU.mult)), deps=[r2, rd])
                x1w = pb.comp("dv", lambda v: v.tensor_mul(
                    tmp[1][:, 0:2048], tmp[1][:, 0:2048], wet[:, :]), deps=[x1] + ew)
                x2 = pb.comp("dv", (lambda v, m3=ms3: v.tensor_tensor(
                    tmp[3][:, 0:2048].rearrange("p (c w) -> p c w", w=HD),
                    BCF[:, :, 1::2], m3, ALU.mult)), deps=[r2, ad])
                x2w = pb.comp("dv", lambda v: v.tensor_mul(
                    tmp[3][:, 0:2048], tmp[3][:, 0:2048], wot[:, :]), deps=[x2] + ow)
                dst3 = dst[0:CH, :].rearrange("p (c w) -> p c w", w=NS)
                t1 = pb.comp("dv", lambda v: v.tensor_mul(
                    tmp[0][:, 0:2048], cosb[:], tmp[1][:, 0:2048]),
                    deps=[x1w] + sc_done + [v_dt, v_d1])
                tA = pb.comp("dv", lambda v: v.tensor_mul(
                    tmp[2][:, 0:2048], sinb[:], tmp[3][:, 0:2048]),
                    deps=[x2w] + sc_done + [a_lam, v_d1])
                evn = pb.comp("dv", (lambda v, d=dst3: v.tensor_tensor(
                    d[:, :, 0:NS:2], tmp[0][:, 0:2048].rearrange("p (c w) -> p c w", w=HD),
                    tmp[2][:, 0:2048].rearrange("p (c w) -> p c w", w=HD), ALU.subtract)),
                    deps=[t1, tA])
                t3 = pb.comp("dv", lambda v: v.tensor_mul(
                    tmp[0][:, 0:2048], sinb[:], tmp[1][:, 0:2048]), deps=[evn])
                t4 = pb.comp("dv", lambda v: v.tensor_mul(
                    tmp[2][:, 0:2048], cosb[:], tmp[3][:, 0:2048]), deps=[evn])
                odd = pb.comp("dv", (lambda v, d=dst3: v.tensor_tensor(
                    d[:, :, 1:NS:2], tmp[0][:, 0:2048].rearrange("p (c w) -> p c w", w=HD),
                    tmp[2][:, 0:2048].rearrange("p (c w) -> p c w", w=HD), ALU.add)),
                    deps=[t3, t4])
                return odd

            l_B = bcload(0, NS, bcscr[0:CH, :],
                         [a_lam, v_ds[3], mm_ac[3], v_v[3], mm_th[3]])
            bn_done = rope_build(msb, RO("E"), RO("F"), bn,
                                 [l_B, v_ds[3], a_ds, z_sh, d_shA, d_shB, l_bwe,
                                  l_bwo, v_d2b, ev_na[NCK - 1]])
            l_cwe = pb.dma("sp", RO("E"), cwe_row, deps=[bn_done])
            l_cwo = pb.dma("sp", RO("F"), cwo_row, deps=[bn_done])
            l_C = bcload(128, NS, bcscr[0:CH, :], [bn_done])
            cn_done = rope_build(msc, RO("E"), RO("F"), cn,
                                 [l_C, bn_done, l_cwe, l_cwo])

            d_bs1 = pb.dma("pl", bstack[65:128, :], bn[0:CH - 1, :], deps=[bn_done])
            d_bs2 = pb.dma("pl",
                           bstack[64:65, :].rearrange("o (c w) -> o c w", w=NS)[:, 1:NCK, :],
                           bn[CH - 1:CH, :].rearrange("o (c w) -> o c w", w=NS)[:, 0:NCK - 1, :],
                           deps=[bn_done])
            z_bs = pb.comp("dv", lambda v: v.memset(bstack[64:65, 0:NS], 0.0),
                           deps=[bn_done])
            z_bnT = pb.comp("dv", lambda v: v.memset(bnT[:, 0:1], 0.0), deps=[c_id])
            tr_b = {}
            for c in range(NCK):
                trb = pb.comp("pe", (lambda t, c=c: t.transpose(
                    pstb[:, 0:CH], bn[0:CH, NS * c:NS * (c + 1)], identsb[0:CH, 0:CH])),
                    deps=[bn_done, tr_b.get(c - 1)])
                tr_b[c] = pb.comp("ac", (lambda s, c=c: s.copy(
                    bnT[:, 1 + CH * c:1 + CH * (c + 1)], pstb[:, 0:CH])), deps=[trb])
            tr_c = {}
            for c in range(NCK):
                trc = pb.comp("pe", (lambda t, c=c: t.transpose(
                    pstc[:, 0:CH], cn[0:CH, NS * c:NS * (c + 1)], identsb[0:CH, 0:CH])),
                    deps=[cn_done, tr_c.get(c - 1), tr_b[NCK - 1]])
                tr_c[c] = pb.comp("ac", (lambda s, c=c: s.copy(
                    cnT[:, CH * c:CH * (c + 1)], pstc[:, 0:CH])), deps=[trc])

            l_dr = pb.dma("sp", RO("D"), drow_in[:, 0:2048], deps=[d_flat, mm_cb[3]])
            l_dr2 = pb.dma("sp", RO("C"), drow_in[:, 2048:4096],
                           deps=[e_dec, mm_na[3], mm_db[3]])
            mm_df = {}
            ev_df = {}
            for q in range(8):
                src = (RO("D") if q < 4 else RO("C"))
                mm_df[q] = pb.comp("pe", (lambda t, q=q, src=src: t.matmul(
                    ps128[:, :], ONES32[:, 0:128], src[:, 512 * (q % 4):512 * (q % 4 + 1)],
                    start=True, stop=True)),
                    deps=[l_dr, l_dr2, ev_db[3], ev_df.get(q - 1), c_on32])
                ev_df[q] = pb.comp("dv", (lambda v, q=q: v.tensor_copy(
                    dful[:, 512 * q:512 * (q + 1)], ps128[:, :])), deps=[mm_df[q]])
            l_b2 = pb.dma("sp", RO("A"), b2row, deps=[e_dec, mm_en[3], mm_db[3]])
            l_wr = pb.dma("sp", RO("B"), wrow[:, 0:2048], deps=[mm_db[3]])
            l_wr2 = pb.dma("sp", RO("E"), wrow[:, 2048:4096], deps=[cn_done])
            mm_b2 = {}
            ev_b2 = {}
            for q in range(4):
                mm_b2[q] = pb.comp("pe", (lambda t, q=q: t.matmul(
                    ps128[:, :], ONES[0:1, 0:128], RO("A")[:, 512 * q:512 * (q + 1)],
                    start=True, stop=True)), deps=[l_b2, ev_df[7], mm_b2.get(q - 1),
                                                   ev_b2.get(q - 1)])
                ev_b2[q] = pb.comp("dv", (lambda v, q=q: v.tensor_copy(
                    b2rep[:, 512 * q:512 * (q + 1)], ps128[:, :])), deps=[mm_b2[q]])
            mm_wr = {}
            ev_wr = {}
            for q in range(8):
                src = (RO("B") if q < 4 else RO("E"))
                lhs1 = (ONES if q < 4 else ONES64)
                mm_wr[q] = pb.comp("pe", (lambda t, q=q, src=src, lhs1=lhs1: t.matmul(
                    ps128[:, :], lhs1[:, 0:128], src[:, 512 * (q % 4):512 * (q % 4 + 1)],
                    start=True, stop=True)),
                    deps=[l_wr, l_wr2, ev_b2[3], ev_wr.get(q - 1), c_on64])
                ev_wr[q] = pb.comp("dv", (lambda v, q=q: v.tensor_copy(
                    wrep[:, 512 * q:512 * (q + 1)], ps128[:, :])), deps=[mm_wr[q]])

            p2a_done = pb.barrier()
            for nm, ap_ in (("acs", acs[:, :]), ("ang", ang[:, :]), ("bn", bn[:, :]),
                            ("cn", cn[:, :]),
                            ("dsstk", dsstk[:, :]), ("dtlstk", dtlstk[:, :]),
                            ("dbc", dbc[:, :]),
                            ("bstack", bstack[:, :]), ("bnT", bnT[:, 0:T + 1]),
                            ("cnT", cnT[:, :]),
                            ("expacsD", expacsD[:, :]), ("nacsT", nacsT[:, :]),
                            ("dful", dful[:, :]),
                            ("wrep", wrep[:, :]), ("b2rep", b2rep[:, :])):
                tap(nm, ap_, p2a_done)
            tap("dt", tmp[0][:, :], p2a_done)

        if BUILD_PHASES < 3:
            pb.finalize(block)
            return dram

        with contextlib.ExitStack() as S:
            xsal = [S.enter_context(nc.sbuf_tensor(f"sxa{i}", [128, DI], BF16))
                    for i in range(2)]
            statebf = S.enter_context(nc.sbuf_tensor("sstbf", [128, DI], BF16))
            mpre = S.enter_context(nc.sbuf_tensor("smp", [128, 2 * 512], BF16))
            statbA = [S.enter_context(nc.sbuf_tensor(f"sstb{i}", [128, DI], BF16))
                      for i in range(2)]
            gmst = S.enter_context(nc.sbuf_tensor("sgm", [128, 2 * CH], BF16))
            gmtmp = S.enter_context(nc.sbuf_tensor("sgt", [CH, CH], BF16))
            lhsA = S.enter_context(nc.sbuf_tensor("sla", [128, 128], F32))
            rhsA = S.enter_context(nc.sbuf_tensor("sra", [128, NH * CH], F32))
            state = S.enter_context(nc.sbuf_tensor("sste", [128, DI], F32))
            ych = [S.enter_context(nc.sbuf_tensor(f"syc{i}", [CH, 2 * DI], BF16))
                   for i in range(2)]
            ps_d = [S.enter_context(nc.psum_tensor(f"qd{i}", [128, 512], F32))
                    for i in range(2)]
            ps_g = S.enter_context(nc.psum_tensor("qg", [CH, 128], F32))
            ps_yd = [S.enter_context(nc.psum_tensor(f"qyd{i}", [CH, 512], F32))
                     for i in range(2)]
            ps_yo = S.enter_context(nc.psum_tensor("qyo", [CH, 512], F32))
            ps_st = [S.enter_context(nc.psum_tensor(f"qst{i}", [128, 512], F32))
                     for i in range(2)]

            l_oh = pb.dma("sp", rhsA[0:NH, :], oh_hl_in, deps=p2a_done)
            onesm1 = pb.comp("dv", lambda v: v.memset(lhsA[64:65, :], -1.0),
                             deps=p2a_done)
            zro_st = [pb.comp("dv", (lambda v, q=q: v.memset(
                state[:, 512 * q:512 * (q + 1)], 0.0)), deps=p2a_done) for q in range(8)]
            zro_st += [pb.comp("dv", lambda v: v.memset(statebf[:, :], 0.0),
                               deps=p2a_done)]

            P = {}

            def prep_head(c):
                cp = c % 2
                buf = c % 2
                tok0 = CH * c
                lda = pb.dma("sp", xsal[buf][0:CH, :], xs_hbm[tok0:tok0 + CH, :],
                             deps=[P.get(("smm", c - 2, 7))])
                if c > 0:
                    lds = pb.dma("sp", xsal[buf][64:128, :],
                                 xs_hbm[tok0 - 1:tok0 + CH - 1, :],
                                 deps=[P.get(("smm", c - 2, 7))])
                    shz = None
                else:
                    lds = pb.dma("sp", xsal[0][65:128, :], xs_hbm[0:CH - 1, :],
                                 deps=p2a_done)
                    shz = pb.comp("dv", lambda v: v.memset(xsal[0][64:65, :], 0.0),
                                  deps=p2a_done)
                dtls = dtlstk[:, NH * c:NH * (c + 1)].rearrange(
                    "p w -> p () w").broadcast_to([128, HD, NH])
                P[("xd", c)] = pb.comp("dv", (lambda v, b=buf, d=dtls: v.tensor_tensor(
                    xsal[b][:].rearrange("p (a w) -> p a w", w=NH),
                    xsal[b][:].rearrange("p (a w) -> p a w", w=NH), d, ALU.mult)),
                    deps=[lda, lds, shz])
                g_mm1 = pb.comp("pe", (lambda t, tok0=tok0: t.matmul(
                    ps_g[:, 0:CH], bnT[:, 1 + tok0:1 + tok0 + CH],
                    cnT[:, tok0:tok0 + CH], start=True, stop=False)),
                    deps=[P.get(("gm1", c - 1)), P.get(("gm2t", c - 1)), l_oh, onesm1])
                g_mm2 = pb.comp("pe", (lambda t, tok0=tok0: t.matmul(
                    ps_g[:, CH:128], bnT[:, tok0:tok0 + CH],
                    cnT[:, tok0:tok0 + CH], start=False, stop=True)), deps=[g_mm1])
                gm1 = pb.comp("dv", (lambda v, cp=cp: v.tensor_tensor(
                    gmst[0:CH, CH * cp:CH * (cp + 1)], ps_g[:, 0:CH], trilb[:, :],
                    ALU.mult)), deps=[g_mm2, c_tril, P.get(("stb", c - 2, 7))])
                gm2t = pb.comp("dv", lambda v: v.tensor_tensor(
                    gmtmp[:, :], ps_g[:, CH:128], trilb[:, :], ALU.mult),
                    deps=[g_mm2, P.get(("gmdn", c - 1))])
                gmdn = pb.dma("pl", gmst[64:128, CH * cp:CH * (cp + 1)], gmtmp[:, :],
                              deps=[gm2t, P.get(("stb", c - 2, 7))])
                P[("gm1", c)], P[("gm2t", c)], P[("gmdn", c)] = gm1, gm2t, gmdn
                la1 = pb.comp("dv", (lambda v, c=c: v.tensor_copy(
                    lhsA[0:CH, 0:CH], nacsT[:, NH * c:NH * (c + 1)])),
                    deps=[P.get(("dmm", c - 1, 7)), onesm1])
                la2 = pb.comp("dv", (lambda v, c=c: v.tensor_copy(
                    lhsA[0:CH, CH:128], nacsT[:, NH * c:NH * (c + 1)])), deps=[la1])
                flt0 = pb.dma("pl", scrC, nacsT[:, NH * c:NH * (c + 1)],
                              deps=[P.get(("flt", c - 1))])
                flt = pb.dma("pl", rhsA[64:65, :],
                             scrC.rearrange("h s -> () (h s)"),
                             deps=[flt0, P.get(("dmm", c - 1, 7))])
                P[("flt", c)] = flt
                P[("la", c)] = la2

            def prep_g(c, g):
                cp = c % 2
                pd = ps_d[g % 2]
                war_mp = P.get(("mp", c, g - 2)) or P.get(("mp", c - 1, g + 6))
                dmm = pb.comp("pe", (lambda t, g=g, pd=pd: (
                    t.matmul(pd[:, :], lhsA[0:65, :],
                             rhsA[0:65, 512 * g:512 * (g + 1)], start=True, stop=False),
                    t.matmul(pd[:, :], maskBb[:, :],
                             oh_l[:, 512 * g:512 * (g + 1)], start=False, stop=True))[-1]),
                    deps=[P[("la", c)], P[("flt", c)], l_oh, c_mb, c_ohl, war_mp])
                P[("dmm", c, g)] = dmm
                war_stb = P.get(("stb", c, g - 2)) or P.get(("stb", c - 1, g + 6))
                mp = pb.comp("ac", (lambda s, g=g, pd=pd: s.activation(
                    mpre[:, 512 * (g % 2):512 * (g % 2 + 1)], pd[:, :], AF.Exp)),
                    deps=[dmm, war_stb])
                P[("mp", c, g)] = mp

                def stbf(v, g=g, cp=cp):
                    r = None
                    for i in range(8):
                        r = v.tensor_tensor(
                            statbA[cp][:, 512 * g + 64 * i:512 * g + 64 * (i + 1)],
                            mpre[:, 512 * (g % 2) + 64 * i:
                                 512 * (g % 2) + 64 * (i + 1)],
                            gmst[:, CH * cp:CH * (cp + 1)], ALU.mult)
                    return r
                P[("stb", c, g)] = pb.comp("dv", stbf, deps=[
                    mp, P[("gmdn", c)], P[("gm1", c)], P.get(("ym", c - 2, g))])

            def comp_g(c, g):
                buf = c % 2
                cp = c % 2
                tok0 = CH * c

                def ydmm(t, g=g, b=buf, cp=cp):
                    r = None
                    for i in range(8):
                        h = 8 * g + i
                        mv = xsal[b][:].rearrange("p (a w) -> p a w", w=NH)[:, :, h]
                        r = t.matmul(ps_yd[g % 2][:, 64 * i:64 * (i + 1)],
                                     statbA[cp][:, 512 * g + 64 * i:
                                                512 * g + 64 * (i + 1)], mv,
                                     start=(i == 0), stop=(i == 7))
                    return r
                ym = pb.comp("pe", ydmm, deps=[
                    P[("stb", c, g)], P[("xd", c)],
                    P.get(("yde", c, g - 2)) or P.get(("yde", c - 1, g + 6))])
                P[("ym", c, g)] = ym

                def yomm(t, g=g, tok0=tok0):
                    r = None
                    for i in range(8):
                        h = 8 * g + i
                        mv = statebf[:, :].rearrange("p (a w) -> p a w", w=NH)[:, :, h]
                        r = t.matmul(ps_yo[:, 64 * i:64 * (i + 1)],
                                     cnT[:, tok0:tok0 + CH], mv,
                                     start=(i == 0), stop=(i == 7))
                    return r
                yo = pb.comp("pe", yomm, deps=[
                    ym, P.get(("yoe", c, g - 1)) or P.get(("yoe", c - 1, 7)),
                    P.get(("sbf", c - 1))] + ([] if c > 0 else zro_st))
                P[("yo", c, g)] = yo

                war_ych = P.get(("ychst", buf)) or []
                yde = pb.comp("ac", (lambda s, g=g, b=buf: s.copy(
                    ych[b][0:CH, 0:DI].rearrange(
                        "p (a w) -> p a w", w=NH)[:, :, 8 * g:8 * (g + 1)],
                    ps_yd[g % 2][:, :].rearrange("p (i w) -> p w i", i=8))),
                    deps=[ym] + war_ych)
                P[("yde", c, g)] = yde
                yoe = pb.comp("ac", (lambda s, g=g, b=buf: s.copy(
                    ych[b][0:CH, DI:2 * DI].rearrange(
                        "p (a w) -> p a w", w=NH)[:, :, 8 * g:8 * (g + 1)],
                    ps_yo[:, :].rearrange("p (i w) -> p w i", i=8))),
                    deps=[yo] + war_ych)
                P[("yoe", c, g)] = yoe

            def comp_tail(c):
                buf = c % 2
                dss = dsstk[:, NH * c:NH * (c + 1)].rearrange(
                    "p w -> p () w").broadcast_to([128, HD, NH])
                dx = pb.comp("dv", (lambda v, b=buf, d=dss: v.tensor_tensor(
                    xsal[b][:].rearrange("p (a w) -> p a w", w=NH),
                    xsal[b][:].rearrange("p (a w) -> p a w", w=NH), d, ALU.mult)),
                    deps=[P[("ym", c, 7)], P[("xd", c)]])
                P[("dx", c)] = dx
                for g in range(8):
                    pq = ps_st[g % 2]
                    war_upd = P.get(("upd", c, g - 2)) or P.get(("upd", c - 1, g + 6))
                    smm = pb.comp("pe", (lambda t, g=g, pq=pq, b=buf, c=c: t.matmul(
                        pq[:, :], bstack[:, NS * c:NS * (c + 1)],
                        xsal[b][:].rearrange("p (a w) -> p a w", w=NH)[:, :, 8 * g:8 * (g + 1)],
                        start=True, stop=True)),
                        deps=[dx, war_upd, P[("yo", c, 7)]])
                    P[("smm", c, g)] = smm

                    def updf(v, g=g, pq=pq, c=c):
                        r = None
                        for i in range(8):
                            h = 8 * g + i
                            sl = state[:, :].rearrange("p (a w) -> p a w", w=NH)[:, :, h]
                            pv = pq[:, :].rearrange("p (a w) -> p a w", w=8)[:, :, i]
                            r = v.scalar_tensor_tensor(
                                sl, sl, dbc[:, NH * c + h:NH * c + h + 1], pv,
                                ALU.mult, ALU.add)
                        return r
                    P[("upd", c, g)] = pb.comp("dv", updf, deps=[smm]
                                               + ([] if c > 0 else zro_st))
                sbf = pb.comp("dv", lambda v: v.tensor_copy(statebf[:, :], state[:, :]),
                              deps=[P[("upd", c, 7)], P[("yo", c, 7)]])
                P[("sbf", c)] = sbf
                evs = [P[("yde", c, g)] for g in range(8)] + \
                      [P[("yoe", c, g)] for g in range(8)]
                tok0 = CH * c
                sd = pb.dma("sp", yd_hbm[tok0:tok0 + CH, :], ych[buf][0:CH, 0:DI],
                            deps=evs)
                so = pb.dma("sp", yo_hbm[tok0:tok0 + CH, :], ych[buf][0:CH, DI:2 * DI],
                            deps=evs)
                P[("ychst", buf)] = [sd, so]

            prep_head(0)
            for g in range(8):
                prep_g(0, g)
            for c in range(NCK):
                if c + 1 < NCK:
                    prep_head(c + 1)
                for g in range(8):
                    if c + 1 < NCK:
                        prep_g(c + 1, g)
                    comp_g(c, g)
                comp_tail(c)
            p2b_done = pb.barrier()

        if BUILD_PHASES < 4:
            pb.finalize(block)
            return dram

        with contextlib.ExitStack() as S:
            ydt = [S.enter_context(nc.sbuf_tensor(f"g0_{i}", [128, DI], BF16)) for i in range(2)]
            yot = [S.enter_context(nc.sbuf_tensor(f"g1_{i}", [128, DI], BF16)) for i in range(2)]
            xst = [S.enter_context(nc.sbuf_tensor(f"g2_{i}", [128, DI], BF16)) for i in range(2)]
            zt = [S.enter_context(nc.sbuf_tensor(f"g3_{i}", [128, DI], BF16)) for i in range(2)]
            ybuf = S.enter_context(nc.sbuf_tensor("g4", [128, DI], BF16))
            zsb = S.enter_context(nc.sbuf_tensor("g5", [128, DI], BF16))
            ynb = S.enter_context(nc.sbuf_tensor("g6", [128, DI], BF16))
            ynstg = [S.enter_context(nc.sbuf_tensor(f"g7_{i}", [128, DI], BF16)) for i in range(2)]
            msc_ = S.enter_context(nc.sbuf_tensor("g8", [128, NTT], F32))
            ptr = [S.enter_context(nc.psum_tensor(f"g9_{i}", [128, 128], BF16)) for i in range(2)]

            stg_done = [None, None]
            cons = [None, None]
            prev_sq = None
            prev_tr_last = None
            sq2 = None
            for tt in range(NTT):
                b = tt % 2
                r0 = 128 * tt
                l1 = pb.dma("sp", ydt[b][:], yd_hbm[r0:r0 + 128, :],
                            deps=[cons[b]] + (p2b_done if tt < 2 else []))
                l2 = pb.dma("sp", yot[b][:], yo_hbm[r0:r0 + 128, :], deps=[cons[b]])
                l3 = pb.dma("sp", xst[b][:], xs_hbm[r0:r0 + 128, :], deps=[cons[b]])
                l4 = pb.dma("pl", zt[b][:], z_hbm[r0:r0 + 128, :], deps=[cons[b]])
                e0 = expacsD[0:CH, NH * 2 * tt:NH * (2 * tt + 1)].rearrange(
                    "p w -> p () w").broadcast_to([CH, HD, NH])
                e1 = expacsD[64:128, NH * (2 * tt + 1):NH * (2 * tt + 2)].rearrange(
                    "p w -> p () w").broadcast_to([CH, HD, NH])
                ya = pb.comp("dv", (lambda v, b=b, e=e0: v.tensor_tensor(
                    ybuf[0:CH, :].rearrange("p (a w) -> p a w", w=NH),
                    yot[b][0:CH, :].rearrange("p (a w) -> p a w", w=NH), e, ALU.mult)),
                    deps=[l2, stg_done[b], prev_sq])
                yb = pb.comp("dv", (lambda v, b=b, e=e1: v.tensor_tensor(
                    ybuf[64:128, :].rearrange("p (a w) -> p a w", w=NH),
                    yot[b][64:128, :].rearrange("p (a w) -> p a w", w=NH), e, ALU.mult)),
                    deps=[l2, stg_done[b], prev_sq])
                yc = pb.comp("dv", (lambda v, b=b: v.tensor_add(
                    ybuf[:, :], ybuf[:, :], ydt[b][:, :])), deps=[ya, yb, l1])
                yd2 = pb.comp("dv", (lambda v, b=b: v.tensor_mul(
                    zsb[:, :], xst[b][:, :], dful[:, :])), deps=[l3, yc, prev_sq])
                ye = pb.comp("dv", lambda v: v.tensor_add(ybuf[:, :], ybuf[:, :],
                                                          zsb[:, :]), deps=[yd2])
                zs = pb.comp("ac", (lambda s, b=b: s.activation(
                    zsb[:, :], zt[b][:, :], AF.Silu)), deps=[l4, ye])
                g2 = pb.comp("dv", lambda v: v.tensor_mul(ybuf[:, :], ybuf[:, :],
                                                          zsb[:, :]), deps=[zs])
                cons[b] = g2
                sq2 = pb.comp("ac", (lambda s, tt=tt: s.activation(
                    zsb[:, :], ybuf[:, :], AF.Square,
                    accum_out=msc_[:, tt:tt + 1])), deps=[g2])
                prev_sq = sq2
                yn = pb.comp("dv", lambda v: v.tensor_mul(ynb[:, :], ybuf[:, :],
                                                          wrep[:, :]),
                             deps=[sq2, prev_tr_last])
                trs = []
                for k in range(32):
                    tr = pb.comp("pe", (lambda t, k=k: t.transpose(
                        ptr[k % 2][:, :], ynb[:, 128 * k:128 * (k + 1)], identsb[:, :])),
                        deps=[yn, trs[k - 2] if k >= 2 else stg_done[b]])
                    ev = pb.comp("dv", (lambda v, k=k, b=b: v.tensor_copy(
                        ynstg[b][:, 128 * k:128 * (k + 1)], ptr[k % 2][:, :])), deps=[tr])
                    trs.append(ev)
                prev_tr_last = trs[-1]
                stg_done[b] = pb.dma("sp", ynT_hbm[tt, :, :], ynstg[b][:, :],
                                     deps=trs)
            r1 = pb.comp("dv", lambda v: v.tensor_scalar(
                msc_[:, :], msc_[:, :], 1.0 / DI, RMS_EPS,
                ALU.mult, ALU.add), deps=[sq2, stg_done[0], stg_done[1]])
            r2 = pb.comp("dv", lambda v: v.reciprocal(msc_[:, :], msc_[:, :]), deps=[r1])
            r3 = pb.comp("ac", lambda s: s.activation(msc_[:, :], msc_[:, :], AF.Sqrt),
                         deps=[r2])
            st_r = pb.dma("sp", rstd_out, msc_[:, :], deps=[r3])
            p3a_done = pb.barrier()
        ssd_sc.close()

        if BUILD_PHASES < 5:
            pb.finalize(block)
            return dram

        if BUILD_PHASES < 6:
            pb.finalize(block)
            return dram

        with contextlib.ExitStack() as S:
            xnT2 = S.enter_context(nc.sbuf_tensor("xnT2", [128, 16 * T], BF16))
            t1b = [S.enter_context(nc.sbuf_tensor(f"f1t{i}", [128, 16 * 1024], BF16))
                   for i in range(2)]
            hst = [S.enter_context(nc.sbuf_tensor(f"f1h{i}", [128, 2048], BF16))
                   for i in range(2)]
            b1sb = S.enter_context(nc.sbuf_tensor("f1b", [128, HF // 128], F32))
            psF = [S.enter_context(nc.psum_tensor(f"f1p{i}", [128, 512], F32))
                   for i in range(8)]
            l_b1 = pb.dma("sp", b1sb[:], b1c_in, deps=p3a_done)
            l_xnt = pb.dma("sp", xnT2[:, :], xnT_hbm, deps=p3a_done)
            ldf = {}
            mmf = {}
            evf = {}
            stf = {}
            for g in range(HF // 1024):
                ldf[g] = pb.dma("sp", t1b[g % 2][:].rearrange(
                    "p (k q) -> p k q", k=16),
                    t1_hbm[:, 1024 * g:1024 * (g + 1)].rearrange(
                        "(k p) q -> p k q", p=128),
                    deps=[mmf.get((g - 2, 7))] + (p3a_done if g < 2 else []))
                for hi in range(8):
                    hft = 8 * g + hi

                    def mmall(t, g=g, hi=hi):
                        r = None
                        for kt in range(16):
                            for q in range(4):
                                pf = psF[(hi % 2) * 4 + q]
                                r = t.matmul(
                                    pf[:, :],
                                    t1b[g % 2][:, 1024 * kt + 128 * hi:
                                               1024 * kt + 128 * (hi + 1)],
                                    xnT2[:, 2048 * kt + 512 * q:
                                         2048 * kt + 512 * (q + 1)],
                                    start=(kt == 0), stop=(kt == 15))
                        return r
                    mmf[(g, hi)] = pb.comp("pe", mmall, deps=[
                        ldf[g], l_xnt,
                        evf.get((g, hi - 2)), evf.get((g - 1, 6 + hi))])

                    def ev(s, g=g, hi=hi, hft=hft):
                        r = None
                        for q in range(4):
                            pf = psF[(hi % 2) * 4 + q]
                            r = s.activation(hst[hi % 2][:, 512 * q:512 * (q + 1)],
                                             pf[:, :], AF.Gelu,
                                             bias=b1sb[:, hft:hft + 1],
                                             scale=mcol[:, 2:3])
                        return r
                    evf[(g, hi)] = pb.comp("ac", ev, deps=[
                        mmf[(g, hi)], l_b1,
                        stf.get((g, hi - 2)), stf.get((g - 1, 6 + hi))])
                    stf[(g, hi)] = pb.dma("sp", h_hbm[128 * hft:128 * (hft + 1), :],
                                          hst[hi % 2][:], deps=[evf[(g, hi)]])
            f1_done = pb.barrier()

        if BUILD_PHASES < 7:
            pb.finalize(block)
            return dram

        with contextlib.ExitStack() as S:
            wo = S.enter_context(nc.sbuf_tensor("p3w", [128, 32 * 2048], BF16))
            ynb_ = [S.enter_context(nc.sbuf_tensor(f"p3y{i}", [128, 32 * 128], BF16))
                    for i in range(2)]
            sst = [S.enter_context(nc.sbuf_tensor(f"p3s{i}", [128, 2048], BF16))
                   for i in range(2)]
            psS = [S.enter_context(nc.psum_tensor(f"p3p{i}", [128, 512], F32))
                   for i in range(8)]
            l_wo = [pb.dma("pl",
                           wo[:, 8 * 2048 * j:8 * 2048 * (j + 1)].rearrange(
                               "p (kt f) -> p kt f", kt=8),
                           woutT[1024 * j:1024 * (j + 1), :].rearrange(
                               "(kt p) f -> p kt f", p=128), deps=f1_done)
                    for j in range(4)]
            ldy = {}
            mms = {}
            evs = {}
            sts = {}
            for tt in range(NTT):
                ldy[tt] = pb.dma("sp", ynb_[tt % 2][:], ynT_hbm[tt, :, :],
                                 deps=[mms.get(tt - 2)] + (f1_done if tt < 2 else []))

                def mall(t, tt=tt):
                    r = None
                    for kt in range(32):
                        for q in range(4):
                            pf = psS[(tt % 2) * 4 + q]
                            r = t.matmul(pf[:, :], ynb_[tt % 2][:, 128 * kt:128 * (kt + 1)],
                                     wo[:, 2048 * kt + 512 * q: 2048 * kt + 512 * (q + 1)],
                                     start=(kt == 0), stop=(kt == 31))
                    return r
                mms[tt] = pb.comp("pe", mall, deps=l_wo + [ldy[tt], evs.get(tt - 2)])

                def evq(s, tt=tt):
                    r = None
                    for q in range(4):
                        pf = psS[(tt % 2) * 4 + q]
                        r = s.copy(sst[tt % 2][:, 512 * q:512 * (q + 1)], pf[:, :])
                    return r
                evs[tt] = pb.comp("ac", evq, deps=[mms[tt], sts.get(tt - 2)])
                sts[tt] = pb.dma("sp", s_out[128 * tt:128 * (tt + 1), :],
                                 sst[tt % 2][:], deps=[evs[tt]])
            p3b_done = pb.barrier()

        if BUILD_PHASES < 8:
            pb.finalize(block)
            return dram

        with contextlib.ExitStack() as S:
            t2b = [S.enter_context(nc.sbuf_tensor("f2t0", [128, 8 * 2048], BF16))]
            hti = [S.enter_context(nc.sbuf_tensor(f"f2h{i}", [128, 8 * 512], BF16))
                   for i in range(2)]
            y2a = S.enter_context(nc.sbuf_tensor("f2y", [128, 16 * 2048], F32))
            y2s = [S.enter_context(nc.sbuf_tensor(f"f2s{i}", [128, 2048], F32))
                   for i in range(2)]
            lmx = S.enter_context(nc.sbuf_tensor("f2m", [128, NTT + 2], F32))
            lrow = S.enter_context(nc.sbuf_tensor("f2r", [1, 128], F32))
            psY = [S.enter_context(nc.psum_tensor(f"f2p{i}", [128, 512], F32))
                   for i in range(8)]
            NBLK = 8
            NKB = HF // 128 // NBLK
            lb = None
            lh = {}
            mmy = {}
            evy = {}
            for blk in range(NBLK):
                lb = pb.dma("sp", t2b[0][:].rearrange(
                    "p (kt f) -> p kt f", kt=NKB),
                    t2_hbm[1024 * blk:1024 * (blk + 1), :].rearrange(
                        "(kt p) f -> p kt f", p=128),
                    deps=[mmy.get((blk - 1, NTT - 1))] + (p3b_done if blk < 1 else []))
                for tt in range(NTT):
                    ts = tt // 4
                    tq = tt % 4
                    if tq == 0:
                        lh[(blk, ts)] = pb.dma("pl" if ts % 2 else "sp",
                            hti[ts % 2][:].rearrange("p (k q) -> p k q", k=NKB),
                            h_hbm[1024 * blk:1024 * (blk + 1),
                                  512 * ts:512 * (ts + 1)].rearrange(
                                "(k p) q -> p k q", p=128),
                            deps=[mmy.get((blk, 4 * ts - 5)),
                                  mmy.get((blk - 1, 4 * ts + 11))]
                            + (p3b_done if blk == 0 and ts < 2 else []))

                    def mall(t, blk=blk, tt=tt, ts=ts, tq=tq):
                        r = None
                        for kt in range(NKB):
                            for q in range(4):
                                pf = psY[(tt % 2) * 4 + q]
                                r = t.matmul(pf[:, :],
                                         hti[ts % 2][:, 512 * kt + 128 * tq:
                                                      512 * kt + 128 * (tq + 1)],
                                         t2b[0][:, 2048 * kt + 512 * q:
                                                2048 * kt + 512 * (q + 1)],
                                         start=(kt == 0), stop=(kt == NKB - 1))
                        return r
                    mmy[(blk, tt)] = pb.comp("pe", mall, deps=[
                        lb, lh[(blk, ts)],
                        evy.get((blk, tt - 2)), evy.get((blk - 1, NTT - 2 + tt))])

                    def evq(v, blk=blk, tt=tt):
                        r = None
                        for q in range(4):
                            pf = psY[(tt % 2) * 4 + q]
                            dst = y2a[:, 2048 * tt + 512 * q: 2048 * tt + 512 * (q + 1)]
                            if blk == 0:
                                r = v.tensor_copy(dst, pf[:, :])
                            else:
                                r = v.tensor_add(dst, dst, pf[:, :])
                        return r
                    evy[(blk, tt)] = pb.comp("dv", evq, deps=[mmy[(blk, tt)],
                                                              evy.get((blk - 1, tt))])
            fin = {}
            for tt in range(NTT):
                f1_ = pb.comp("dv", (lambda v, tt=tt: v.scalar_tensor_tensor(
                    y2s[tt % 2][:], y2a[:, 2048 * tt:2048 * (tt + 1)], mcol[:, 3:4],
                    b2rep[:, :], ALU.mult, ALU.add)),
                    deps=[evy[(NBLK - 1, tt)], fin.get(tt - 2)])
                f2_ = pb.comp("dv", (lambda v, tt=tt: v.reduce_max(
                    lmx[:, tt:tt + 1], y2s[tt % 2][:], AX.X,
                    apply_absolute_value=True)), deps=[f1_])
                fin[tt] = pb.dma("sp", y2f_out[128 * tt:128 * (tt + 1), :],
                                 y2s[tt % 2][:], deps=[f2_])
            mred = pb.comp("dv", lambda v: v.reduce_max(
                lmx[:, NTT:NTT + 1], lmx[:, 0:NTT], AX.X), deps=[fin[NTT - 1],
                                                                 fin[NTT - 2]])
            d_lm0 = pb.dma("pl", scrD, lmx[:, NTT:NTT + 1], deps=[mred])
            d_lm = pb.dma("pl", lrow[0:1, 0:128],
                          scrD.rearrange("p o -> o p"), deps=[d_lm0])
            gmx = pb.comp("dv", lambda v: v.reduce_max(lmx[0:1, NTT + 1:NTT + 2],
                                                       lrow[0:1, 0:128], AX.X),
                          deps=[d_lm])
            st_lm = pb.dma("sp", lmax_out, lmx[0:1, NTT + 1:NTT + 2], deps=[gmx])
            pb.barrier()

        pb.finalize(block)
    return dram


def build_final(nc: bass.Bass):
    dram = {}
    din = lambda n, s, d=F32: _dram(nc, dram, n, s, d, "ExternalInput")
    dout = lambda n, s, d=F32: _dram(nc, dram, n, s, d, "ExternalOutput")
    x_in = din("x", (T, DM))
    y2f = din("y2f", (T, DM))
    s_raw = din("s_raw", (T, DM), BF16)
    rstd = din("rstd_all", (128, NTT))
    maxv = din("maxv", (1, 8))
    out = dout("out", (T, DM))

    ctx = contextlib.ExitStack()
    E = ctx.enter_context
    with ctx:
        pb = PB(nc, E)
        yt = [E(nc.sbuf_tensor(f"by{i}", [128, DM], F32)) for i in range(2)]
        st_ = [E(nc.sbuf_tensor(f"bs{i}", [128, DM], BF16)) for i in range(2)]
        xt = [E(nc.sbuf_tensor(f"bx{i}", [128, DM], F32)) for i in range(2)]
        ot = [E(nc.sbuf_tensor(f"bo{i}", [128, DM], F32)) for i in range(2)]
        gt = E(nc.sbuf_tensor("bg", [128, DM], BF16))
        rst = E(nc.sbuf_tensor("br", [128, NTT], F32))
        mrow = E(nc.sbuf_tensor("bm", [1, 8], F32))
        scol = E(nc.sbuf_tensor("bc", [128, 4], F32))
        pq = E(nc.psum_tensor("bp", [128, 2], F32))
        block = E(nc.Block())
        ones = E(nc.sbuf_tensor("bones", [1, 128], F32))

        oc = pb.comp("dv", lambda v: v.memset(ones[:, :], 1.0))
        l_m = pb.dma("sp", mrow[:, :], maxv)
        l_r = pb.dma("sp", rst[:, :], rstd)
        g1 = pb.comp("dv", lambda v: v.reduce_max(scol[0:1, 0:1], mrow[:, :], AX.X),
                     deps=[l_m])
        g2 = pb.comp("dv", lambda v: v.tensor_scalar(
            scol[0:1, 1:2], scol[0:1, 0:1], 1.0 / QMAX, QEPS, ALU.mult, ALU.add),
            deps=[g1])
        g3 = pb.comp("dv", lambda v: v.reciprocal(scol[0:1, 2:3], scol[0:1, 1:2]),
                     deps=[g2])
        m1 = pb.comp("pe", lambda t: t.matmul(pq[:, 0:1], ones[0:1, 0:128],
                                              scol[0:1, 1:2], start=True, stop=True),
                     deps=[g2, oc])
        m2 = pb.comp("pe", lambda t: t.matmul(pq[:, 1:2], ones[0:1, 0:128],
                                              scol[0:1, 2:3], start=True, stop=True),
                     deps=[g3, m1])
        e1 = pb.comp("dv", lambda v: v.tensor_copy(scol[:, 0:1], pq[:, 0:1]),
                     deps=[m1])
        e2 = pb.comp("dv", lambda v: v.tensor_copy(scol[:, 1:2], pq[:, 1:2]),
                     deps=[m2, e1])
        done = {}
        for tt in range(NTT):
            b = tt % 2
            r0 = 128 * tt
            l1 = pb.dma("sp", yt[b][:], y2f[r0:r0 + 128, :], deps=[done.get(tt - 2)])
            l2 = pb.dma("sp", st_[b][:], s_raw[r0:r0 + 128, :], deps=[done.get(tt - 2)])
            l3 = pb.dma("pl", xt[b][:], x_in[r0:r0 + 128, :], deps=[done.get(tt - 2)])
            q1 = pb.comp("dv", (lambda v, b=b: v.tensor_scalar(
                yt[b][:], yt[b][:], scol[:, 1:2], MROUND, ALU.mult, ALU.add)),
                deps=[l1, e2])
            q2 = pb.comp("dv", (lambda v, b=b: v.tensor_scalar(
                yt[b][:], yt[b][:], -MROUND, -QMAX, ALU.add, ALU.max)), deps=[q1])
            q3 = pb.comp("dv", (lambda v, b=b: v.tensor_scalar_min(
                yt[b][:], yt[b][:], QMAX)), deps=[q2])
            sg = pb.comp("ac", (lambda s, b=b: s.activation(
                gt[:], yt[b][:], AF.Silu, scale=scol[:, 0:1])), deps=[q3, e1,
                                                                     done.get(tt - 1)])
            s2 = pb.comp("dv", (lambda v, b=b, tt=tt: v.tensor_scalar(
                st_[b][:], st_[b][:], rst[:, tt:tt + 1], None, ALU.mult)),
                deps=[l2, l_r])
            o1 = pb.comp("dv", (lambda v, b=b: v.tensor_mul(
                ot[b][:], st_[b][:], gt[:])), deps=[sg, s2])
            o2 = pb.comp("dv", (lambda v, b=b: v.tensor_add(
                ot[b][:], ot[b][:], xt[b][:])), deps=[o1, l3])
            done[tt] = pb.dma("sp", out[r0:r0 + 128, :], ot[b][:], deps=[o2])
        pb.barrier()
        pb.finalize(block)
    return dram


def _perm_ph():
    j = np.arange(DI)
    return (j % NH) * HD + j // NH


def _consts():
    import ml_dtypes
    bf = ml_dtypes.bfloat16
    ident = np.eye(128, dtype=np.float32).astype(bf)
    hh, jj = np.meshgrid(np.arange(NH), np.arange(NH * CH), indexing="ij")
    onehot_hl = (jj // CH == hh).astype(np.float32)
    ll = np.meshgrid(np.arange(CH), np.arange(NH * CH), indexing="ij")
    onehot_l = (ll[1] % CH == ll[0]).astype(np.float32).astype(bf)
    lp, sd = np.meshgrid(np.arange(CH), np.arange(128), indexing="ij")
    maskB = np.where(lp < (sd % 64), MASKNEG, 0.0).astype(np.float32).astype(bf)
    s_, l_ = np.meshgrid(np.arange(CH), np.arange(CH), indexing="ij")
    tril64 = (l_ >= s_).astype(np.float32).astype(bf)
    k_, t_ = np.meshgrid(np.arange(CH), np.arange(CH), indexing="ij")
    triu_inc = (k_ <= t_).astype(np.float32)
    k2, c2 = np.meshgrid(np.arange(NCK), np.arange(NCK), indexing="ij")
    triu_exc = (k2 < c2).astype(np.float32)
    ones_r = np.ones((1, 128), dtype=np.float32)
    return dict(ident=ident, identf=np.eye(128, dtype=np.float32),
                onehot_hl=onehot_hl, onehot_l=onehot_l, maskB=maskB,
                tril64=tril64, triu_inc=triu_inc, triu_exc32=triu_exc, ones_r=ones_r)


def make_inputs_a(core, inputs):
    perm = _perm_ph()
    W_in = inputs["L0_attn_W_in"]
    WinT = np.ascontiguousarray(np.asarray(W_in).T)
    xs_cols = WinT[:, 4096 + perm]
    bc_cols = WinT[:, 8192:8640]
    z_cols = WinT[:, perm]
    winT = np.ascontiguousarray(np.concatenate([xs_cols, bc_cols, z_cols], axis=1))
    WoutT = np.ascontiguousarray(np.asarray(inputs["L0_attn_W_out"]).T)
    woutT = np.ascontiguousarray(WoutT[perm, :])
    w1T = np.ascontiguousarray(np.asarray(inputs["L0_ffn_W1"]).T)
    w2T = np.ascontiguousarray(np.asarray(inputs["L0_ffn_W2"]).T)
    b1 = np.asarray(inputs["L0_ffn_b1"])
    b1c = np.ascontiguousarray(b1.reshape(HF // 128, 128).T)
    b2row = np.asarray(inputs["L0_ffn_b2"]).reshape(1, DM)
    wrow = np.asarray(inputs["L0_attn_out_norm_w"])[perm].reshape(1, DI)
    drow = np.tile(np.asarray(inputs["L0_attn_D"]), HD).reshape(1, DI)
    dtb = np.tile(np.asarray(inputs["L0_attn_dt_bias"]), NCK).reshape(1, NCK * NH)
    alog = np.tile(np.asarray(inputs["L0_attn_A_log"]), NCK).reshape(1, NCK * NH)
    bw = np.asarray(inputs["L0_attn_B_norm_w"])
    cw = np.asarray(inputs["L0_attn_C_norm_w"])
    bwe = np.tile(bw[0::2], NCK).reshape(1, NCK * HD)
    bwo = np.tile(bw[1::2], NCK).reshape(1, NCK * HD)
    cwe = np.tile(cw[0::2], NCK).reshape(1, NCK * HD)
    cwo = np.tile(cw[1::2], NCK).reshape(1, NCK * HD)
    im = dict(x=np.ascontiguousarray(np.asarray(inputs["L0_attn_u"])[core]),
              winT=winT, woutT=woutT, w1T=w1T, w2T=w2T, b1c=b1c, b2row=b2row,
              wrow=wrow, drow=drow, dtb_row=dtb, alog_row=alog,
              bwe_row=bwe, bwo_row=bwo, cwe_row=cwe, cwo_row=cwo)
    im.update(_consts())
    return {k: np.ascontiguousarray(v) for k, v in im.items()}


def kernel(**inputs):
    ncA = bass.Bass("TRN2", target_bir_lowering=False, debug=False)
    build_main(ncA)
    in_maps = [make_inputs_a(c, inputs) for c in range(8)]
    resA = run_bass_kernel_spmd(ncA, in_maps, core_ids=list(range(8)))

    maxv = np.concatenate([np.asarray(resA.results[c]["lmax"]).reshape(1, 1)
                           for c in range(8)], axis=1)
    ncB = bass.Bass("TRN2", target_bir_lowering=False, debug=False)
    build_final(ncB)
    in_maps_b = []
    for c in range(8):
        in_maps_b.append(dict(
            x=np.ascontiguousarray(np.asarray(inputs["L0_attn_u"])[c]),
            y2f=np.asarray(resA.results[c]["y2f"]),
            s_raw=np.asarray(resA.results[c]["s_raw"]),
            rstd_all=np.asarray(resA.results[c]["rstd_all"]),
            maxv=maxv))
    resB = run_bass_kernel_spmd(ncB, in_maps_b, core_ids=list(range(8)))
    out = np.stack([np.asarray(resB.results[c]["out"]) for c in range(8)], axis=0)
    return out.astype(np.float32)


# version 11
# speedup vs baseline: 1.0085x; 1.0085x over previous
import contextlib
import itertools
import numpy as np
import concourse.bass as bass
import concourse.mybir as mybir
from concourse.bass_utils import run_bass_kernel_spmd

F32 = mybir.dt.float32
BF16 = mybir.dt.bfloat16
AF = mybir.ActivationFunctionType
ALU = mybir.AluOpType
AX = mybir.AxisListType

T = 2048
DM = 2048
DI = 4096
NH = 64
HD = 64
NS = 128
CH = 64
NCK = T // CH
HF = 8192
NTT = T // 128
DIP_SSD = 4544
DIP = DIP_SSD + DI

MROUND = 12582912.0
MASKNEG = -28672.0
LN_EPS = 1e-5
RMS_EPS = 1e-5
QMAX = 7.0
QEPS = 1e-8
THRESH = 0.05

DEBUG_TAPS = ()
BUILD_PHASES = 99

ENGN = ("pe", "dv", "ac", "pl", "sp")
_DECN = {"pe": "tensor", "dv": "vector", "ac": "scalar", "pl": "gpsimd", "sp": "sync"}


class _Step:
    __slots__ = ("eng", "emit", "waits", "sem", "val", "isdma", "sedv")


class PB:
    NQSEM = 16

    def __init__(self, nc, E):
        self.nc = nc
        self.eng = {"pe": nc.tensor, "dv": nc.vector, "ac": nc.scalar,
                    "pl": nc.gpsimd, "sp": nc.sync}
        self.sem = {e: E(nc.semaphore(f"s_{e}")) for e in ENGN}
        self.qsem = {e: [E(nc.semaphore(f"q_{e}{i}")) for i in range(self.NQSEM)]
                     for e in ("sp", "pl", "ac")}
        self.cnt = {e: 0 for e in ENGN}
        self.qcnt = {e: 0 for e in ("sp", "pl", "ac")}
        self.steps = {e: [] for e in ENGN}
        self.last = {e: None for e in ENGN}

    def _mkwaits(self, eng, deps):
        waits = {}
        for d in deps:
            if d is None:
                continue
            if (not d.isdma) and d.eng == eng:
                continue
            key = id(d.sem)
            if key not in waits or waits[key][1] < d.val:
                waits[key] = (d.sem, d.val)
        return list(waits.values())

    def comp(self, eng, emit, deps=()):
        st = _Step()
        st.eng = eng
        st.emit = emit
        st.isdma = False
        st.waits = self._mkwaits(eng, deps)
        st.sedv = max((d.val for d in deps
                       if d is not None and (not d.isdma) and d.eng == eng),
                      default=0)
        self.cnt[eng] += 1
        st.sem = self.sem[eng]
        st.val = self.cnt[eng]
        self.steps[eng].append(st)
        self.last[eng] = st
        return st

    def dma(self, eng, out, in_, deps=(), **kw):
        assert eng in ("sp", "pl", "ac")
        st = _Step()
        st.eng = eng
        st.isdma = True
        j = self.qcnt[eng]
        self.qcnt[eng] = j + 1
        qs = self.qsem[eng][j % self.NQSEM]
        uses = j // self.NQSEM
        st.sem = qs
        st.val = 16 * (uses + 1)

        def emit(e, out=out, in_=in_, kw=kw, qs=qs, uses=uses):
            if uses:
                e.wait_ge(qs, 16 * uses)
            e.dma_start(out=out, in_=in_, **kw).then_inc(qs, 16)
        st.emit = emit
        st.waits = self._mkwaits(eng, deps)
        self.steps[eng].append(st)
        self.last[eng] = st
        return st

    def _qsem_state(self):
        ws = []
        for qe in ("sp", "pl", "ac"):
            n = self.qcnt[qe]
            for i in range(self.NQSEM):
                uses = (n - i + self.NQSEM - 1) // self.NQSEM if n > i else 0
                if uses > 0:
                    ws.append((self.qsem[qe][i], 16 * uses))
        return ws

    def barrier(self):
        lasts = [self.last[e] for e in ENGN if self.last[e] is not None]
        qws = self._qsem_state()
        steps = [self.comp(e, lambda _: None, deps=lasts) for e in ENGN]
        for st in steps:
            seen = {id(s) for (s, _) in st.waits}
            for (s, v) in qws:
                if id(s) not in seen:
                    st.waits.append((s, v))
        return steps

    def finalize(self, block):
        finals = []
        for qe in ("sp", "pl", "ac"):
            n = self.qcnt[qe]
            for i in range(self.NQSEM):
                uses = (n - i + self.NQSEM - 1) // self.NQSEM if n > i else 0
                if uses > 0:
                    finals.append((self.qsem[qe][i], 16 * uses))
        for ename in ENGN:
            steps = self.steps[ename]
            esem = self.sem[ename]

            def mk(steps=steps, esem=esem, ename=ename):
                def f(e):
                    wm = {}
                    cover = 0
                    lval = 0
                    for st in steps:
                        for (s, v) in st.waits:
                            k = id(s)
                            if wm.get(k, -1) >= v:
                                continue
                            e.wait_ge(s, v)
                            wm[k] = v
                        if (not st.isdma) and ename != "pe" \
                                and st.sedv > cover:
                            e.drain()
                            cover = lval
                        ret = st.emit(e)
                        if not st.isdma:
                            if ret is not None:
                                ret.then_inc(esem, 1)
                            else:
                                e.drain().then_inc(esem, 1)
                                cover = st.val
                            lval = st.val
                    if ename == "sp":
                        for (s, v) in finals:
                            if wm.get(id(s), -1) < v:
                                e.wait_ge(s, v)
                return f
            getattr(block, _DECN[ename])(mk())


def _dram(nc, dram, name, shape, dtype, kind):
    dram[name] = nc.dram_tensor(name, list(shape), dtype, kind=kind).ap()
    return dram[name]


def build_main(nc: bass.Bass):
    dram = {}
    din = lambda n, s, d=F32: _dram(nc, dram, n, s, d, "ExternalInput")
    dout = lambda n, s, d=F32: _dram(nc, dram, n, s, d, "ExternalOutput")
    dint = lambda n, s, d=F32: _dram(nc, dram, n, s, d, "Internal")

    x_in = din("x", (T, DM))
    winT = din("winT", (DM, DIP))
    woutT = din("woutT", (DI, DM))
    w1T = din("w1T", (DM, HF))
    w2T = din("w2T", (HF, DM))
    b1c_in = din("b1c", (128, HF // 128))
    b2row = din("b2row", (1, DM))
    wrow = din("wrow", (1, DI))
    drow_in = din("drow", (1, DI))
    dtb_row = din("dtb_row", (1, NCK * NH))
    alog_row = din("alog_row", (1, NCK * NH))
    bwe_row = din("bwe_row", (1, NCK * HD))
    bwo_row = din("bwo_row", (1, NCK * HD))
    cwe_row = din("cwe_row", (1, NCK * HD))
    cwo_row = din("cwo_row", (1, NCK * HD))
    ident_in = din("ident", (128, 128), BF16)
    identf_in = din("identf", (128, 128), F32)
    oh_hl_in = din("onehot_hl", (NH, NH * CH))
    oh_l_in = din("onehot_l", (CH, NH * CH), BF16)
    maskB_in = din("maskB", (CH, 128), BF16)
    tril_in = din("tril64", (CH, CH), BF16)
    triui_in = din("triu_inc", (CH, CH))
    triux_in = din("triu_exc32", (NCK, NCK))
    ones_in = din("ones_r", (1, 128))

    y2f_out = dout("y2f", (T, DM))
    lmax_out = dout("lmax", (1, 1))
    rstd_out = dout("rstd_all", (128, NTT))
    s_out = dout("s_raw", (T, DM), BF16)

    xs_hbm = dint("xs_hbm", (T, DI), BF16)
    xnT_hbm = dint("xnT_hbm", (128, 16 * T), BF16)
    bc_hbm = dint("bc_hbm", (T, 448), F32)
    scrA = dint("scrA", (1, 2048), F32)
    scrB = dint("scrB", (NCK, HD), F32)
    scrC = dint("scrC", (CH, CH), F32)
    scrD = dint("scrD", (128, 1), F32)
    scrE = dint("scrE", (1, 8), F32)
    z_hbm = dint("z_hbm", (T, DI), BF16)
    yd_hbm = dint("yd_hbm", (T, DI), BF16)
    yo_hbm = dint("yo_hbm", (T, DI), BF16)
    t1_hbm = dint("t1_hbm", (DM, HF), BF16)
    t2_hbm = dint("t2_hbm", (HF, DM), BF16)
    h_hbm = dint("h_hbm", (HF, T), BF16)
    ynT_hbm = dint("ynT_hbm", (NTT, 128, DI), BF16)

    ctx = contextlib.ExitStack()
    E = ctx.enter_context
    with ctx:
        pb = PB(nc, E)
        taps = []

        def tap(name, ap_, deps):
            if name in DEBUG_TAPS:
                o = dout("tap_" + name, list(ap_.shape), ap_.dtype)
                taps.append(pb.dma("sp", o, ap_, deps=deps))

        identsb = E(nc.sbuf_tensor("identsb", [128, 128], BF16))
        consts = E(nc.sbuf_tensor("consts", [128, 512], F32))
        trilb = E(nc.sbuf_tensor("trilb", [CH, CH], BF16))
        maskBb = E(nc.sbuf_tensor("maskBb", [CH, 128], BF16))
        mcol = E(nc.sbuf_tensor("mcol", [128, 64], F32))
        b2rep = E(nc.sbuf_tensor("b2rep", [128, DM], F32))
        wscr_sc = E(contextlib.ExitStack())
        WE = wscr_sc.enter_context
        wmt = {m: [WE(nc.sbuf_tensor(f"{m}t{i}", [128, 2048], F32))
                   for i in range(2)] for m in ("w1", "w2")}
        wtb = {m: [WE(nc.sbuf_tensor(f"{m}b{i}", [128, 2048], BF16))
                   for i in range(1)] for m in ("w1", "w2")}
        wqs = WE(nc.sbuf_tensor("wqs", [128, 2048], F32))
        wtr = WE(nc.sbuf_tensor("wtr", [128, 2048], F32))
        wprt = WE(nc.sbuf_tensor("wprt", [128, 520], F32))
        wrr = WE(nc.sbuf_tensor("wrr", [1, 512], F32))
        wones = WE(nc.sbuf_tensor("wones", [128, 128], F32))
        xnt_sc = E(contextlib.ExitStack())
        xnT = xnt_sc.enter_context(nc.sbuf_tensor("xnT", [128, 16 * T], BF16))
        block = E(nc.Block())

        c_id = pb.dma("sp", identsb[:], ident_in)
        c_tril = pb.dma("sp", trilb[:], tril_in)
        c_mb = pb.dma("sp", maskBb[:], maskB_in)
        c_tri = pb.dma("sp", consts[0:CH, 64:128], triui_in)
        c_trx = pb.dma("sp", consts[0:NCK, 128:160], triux_in)
        c_on = pb.dma("sp", consts[0:1, 160:288], ones_in)
        c_on32 = pb.dma("sp", consts[32:33, 160:288], ones_in)
        c_on64 = pb.dma("sp", consts[64:65, 160:288], ones_in)
        s_non = pb.comp("dv", lambda v: v.tensor_scalar(
            consts[0:1, 288:416], consts[0:1, 160:288], -1.0, None, ALU.mult),
            deps=[c_on])
        c_eps = pb.comp("dv", lambda v: v.memset(consts[:, 416:417], LN_EPS))
        EPSC = consts[:, 416:417]
        ONES = consts[0:1, 160:288]
        ONES32 = consts[32:33, 160:288]
        ONES64 = consts[64:65, 160:288]
        NEGONES = consts[0:1, 288:416]
        TRIUI = consts[0:CH, 64:128]
        TRIUX = consts[0:NCK, 128:160]
        c_onall = [c_on, c_on32, c_on64]

        WQ = "ac"
        WCFG = {
            "w1": dict(wT=w1T, thbm=t1_hbm, icol=0, acol=2,
                       tc=0, mc0=8, rc1=136, rc2=200, cc0=264, s0=0, e0=0,
                       a0=16, aj=0),
            "w2": dict(wT=w2T, thbm=t2_hbm, icol=1, acol=3,
                       tc=1, mc0=72, rc1=328, rc2=392, cc0=456, s0=8, e0=2,
                       a0=24, aj=1),
        }
        wmx = {"w1": {}, "w2": {}}
        wcons = {"w1": {}, "w2": {}}
        wst = {"w1": {}, "w2": {}}
        wdelta = {}

        WQSEL = ["ac"]

        def wmax_gen(m):
            c = WCFG[m]
            wT = c["wT"]
            nn = wT.shape[1] // 2048
            for i in range(64):
                kt, nt = divmod(i, nn)
                b = i % 2
                ld = pb.dma(WQSEL[0], wmt[m][b][:],
                            wT[128 * kt:128 * (kt + 1), 2048 * nt:2048 * (nt + 1)],
                            deps=[wmx[m].get(i - 2)])
                wmx[m][i] = pb.comp("dv", (lambda v, b=b, co=c["mc0"] + i, m=m:
                    v.reduce_max(wprt[:, co:co + 1], wmt[m][b][:], AX.X,
                                 apply_absolute_value=True)), deps=[ld])
                yield

        def wdfin_gen():
            prev_rd = None
            prev_gm = None
            for m in ("w1", "w2"):
                c = WCFG[m]
                s0 = c["s0"]
                red = pb.comp("dv", (lambda v, c=c: v.reduce_max(
                    wprt[:, c["tc"]:c["tc"] + 1], wprt[:, c["mc0"]:c["mc0"] + 64],
                    AX.X)), deps=[wmx[m][63]])
                d0 = pb.dma(WQ, scrD, wprt[:, c["tc"]:c["tc"] + 1],
                            deps=[red, prev_rd])
                d1 = pb.dma(WQ, wrr[0:1, 128:256], scrD.rearrange("p o -> o p"),
                            deps=[d0, prev_gm])
                prev_rd = d1
                gm = pb.comp("dv", (lambda v, s0=s0: v.reduce_max(
                    wrr[0:1, s0:s0 + 1], wrr[0:1, 128:256], AX.X)), deps=[d1])
                prev_gm = gm
                dl = pb.comp("dv", (lambda v, s0=s0: v.tensor_scalar(
                    wrr[0:1, s0 + 1:s0 + 2], wrr[0:1, s0:s0 + 1], THRESH, None,
                    ALU.mult)), deps=[gm])
                v2 = pb.comp("dv", (lambda v, s0=s0: v.tensor_scalar(
                    wrr[0:1, s0 + 2:s0 + 3], wrr[0:1, s0 + 1:s0 + 2], 2.0, None,
                    ALU.mult)), deps=[dl])
                r2_ = pb.comp("dv", (lambda v, s0=s0: v.reciprocal(
                    wrr[0:1, s0 + 2:s0 + 3], wrr[0:1, s0 + 2:s0 + 3])), deps=[v2])
                ngd = pb.comp("dv", (lambda v, s0=s0: v.tensor_scalar(
                    wrr[0:1, s0 + 3:s0 + 4], wrr[0:1, s0 + 1:s0 + 2], -1.0, None,
                    ALU.mult)), deps=[dl])
                d2 = pb.dma(WQ, scrE[0:1, c["e0"]:c["e0"] + 2],
                            wrr[0:1, s0 + 2:s0 + 4], deps=[r2_, ngd])
                b_iv = pb.dma(WQ, mcol[:, c["icol"]:c["icol"] + 1],
                              scrE[0:1, c["e0"]:c["e0"] + 1].broadcast_to([128, 1]),
                              deps=[d2])
                b_ng = pb.dma(WQ, mcol[:, c["icol"] + 4:c["icol"] + 5],
                              scrE[0:1, c["e0"] + 1:c["e0"] + 2].broadcast_to([128, 1]),
                              deps=[d2])
                wdelta[m] = (b_iv, b_ng)
                yield

        def wbuild_gen(m):
            c = WCFG[m]
            wT = c["wT"]
            thbm = c["thbm"]
            nn = wT.shape[1] // 2048
            jmap = {0: 62, 1: 63}
            for i in range(64):
                kt, nt = divmod(i, nn)
                b = i % 2
                ld = pb.dma(WQSEL[0], wmt[m][b][:],
                            wT[128 * kt:128 * (kt + 1), 2048 * nt:2048 * (nt + 1)],
                            deps=(wcons[m][i - 2] if i >= 2
                                  else [wmx[m][jmap[b]]]))
                u1 = pb.comp("dv", (lambda v, b=b, m=m, ic=c["icol"]:
                    v.tensor_scalar(wqs[:], wmt[m][b][:], mcol[:, ic:ic + 1],
                                    MROUND, ALU.mult, ALU.add)),
                    deps=[ld, wdelta[m][0]])
                u2 = pb.comp("dv", lambda v: v.tensor_scalar(
                    wqs[:], wqs[:], -MROUND, -1.0, ALU.add, ALU.max), deps=[u1])
                tb_ = wtb[m][0]
                u3 = pb.comp("dv", (lambda v, tb_=tb_: v.tensor_scalar_min(
                    tb_[:], wqs[:], 1.0)), deps=[u2, wst[m].get(i - 1)])
                u4 = pb.comp("dv", (lambda v, tb_=tb_, co=c["cc0"] + i:
                    v.reduce_sum(wprt[:, co:co + 1], tb_[:], AX.X,
                                 apply_absolute_value=True)), deps=[u3])
                r1 = pb.comp("ac", (lambda s, b=b, m=m, co=c["rc1"] + i,
                                    ic=c["icol"]: s.activation(
                    wtr[:], wmt[m][b][:], AF.Relu, bias=mcol[:, ic + 4:ic + 5],
                    scale=1.0, accum_out=wprt[:, co:co + 1])),
                    deps=[ld, wdelta[m][1]])
                r2 = pb.comp("ac", (lambda s, b=b, m=m, co=c["rc2"] + i,
                                    ic=c["icol"]: s.activation(
                    wtr[:], wmt[m][b][:], AF.Relu, bias=mcol[:, ic + 4:ic + 5],
                    scale=-1.0, accum_out=wprt[:, co:co + 1])), deps=[r1])
                wcons[m][i] = [u1, r2]
                dst = thbm[128 * kt:128 * (kt + 1), 2048 * nt:2048 * (nt + 1)]
                wst[m][i] = pb.dma("pl", dst, tb_[:], deps=[u3])
                yield

        def _ilv(*gens):
            gens = list(gens)
            while gens:
                for g in gens[:]:
                    try:
                        next(g)
                    except StopIteration:
                        gens.remove(g)
                        continue
                    yield

        wchain = itertools.chain(
            _ilv(wmax_gen("w1"), wmax_gen("w2")), wdfin_gen(),
            _ilv(wbuild_gen("w1"), wbuild_gen("w2")))

        def wpump(n=1, q="ac"):
            WQSEL[0] = q
            for _ in range(n):
                if next(wchain, None) is None:
                    return

        def walpha_emit(psW):
            oc = pb.comp("dv", lambda v: v.memset(wones[:, :], 1.0))
            prev_pe = []
            for m in ("w1", "w2"):
                c = WCFG[m]
                aj, ic = c["aj"], c["icol"]
                lastac = wcons[m][63][1]
                s1 = pb.comp("dv", (lambda v, c=c: v.reduce_sum(
                    wprt[:, 2:3], wprt[:, c["rc1"]:c["rc1"] + 64], AX.X)),
                    deps=[lastac] + prev_pe)
                s2 = pb.comp("dv", (lambda v, c=c: v.reduce_sum(
                    wprt[:, 3:4], wprt[:, c["rc2"]:c["rc2"] + 64], AX.X)),
                    deps=[lastac, s1])
                s3 = pb.comp("dv", (lambda v, c=c: v.reduce_sum(
                    wprt[:, 4:5], wprt[:, c["cc0"]:c["cc0"] + 64], AX.X)),
                    deps=[s2])
                a12 = pb.comp("dv", lambda v: v.tensor_add(
                    wprt[:, 2:3], wprt[:, 2:3], wprt[:, 3:4]), deps=[s1, s2])
                f_r = pb.comp("pe", (lambda t, aj=aj: t.matmul(
                    psW[:, 4 * aj:4 * aj + 1], wones[:, :], wprt[:, 2:3],
                    start=True, stop=True)), deps=[a12, oc])
                f_c = pb.comp("pe", (lambda t, aj=aj: t.matmul(
                    psW[:, 4 * aj + 1:4 * aj + 2], wones[:, :], wprt[:, 4:5],
                    start=True, stop=True)), deps=[s3, oc, f_r])
                prev_pe = [f_r, f_c]
                t_ = pb.comp("dv", (lambda v, aj=aj, ic=ic: v.tensor_scalar(
                    wprt[:, 5:6], psW[:, 4 * aj + 1:4 * aj + 2],
                    mcol[:, ic + 4:ic + 5], None, ALU.mult)),
                    deps=[f_c, wdelta[m][1]])
                ms_ = pb.comp("dv", (lambda v, aj=aj: v.tensor_sub(
                    wprt[:, 5:6], psW[:, 4 * aj:4 * aj + 1], wprt[:, 5:6])),
                    deps=[t_, f_r])
                cn_ = pb.comp("dv", (lambda v, aj=aj: v.tensor_scalar(
                    wprt[:, 6:7], psW[:, 4 * aj + 1:4 * aj + 2],
                    1.0, None, ALU.max)), deps=[f_c])
                rc_ = pb.comp("dv", lambda v: v.reciprocal(
                    wprt[:, 6:7], wprt[:, 6:7]), deps=[cn_])
                al_ = pb.comp("dv", lambda v: v.tensor_mul(
                    wprt[:, 7:8], wprt[:, 5:6], wprt[:, 6:7]), deps=[ms_, rc_])
                cp_ = pb.comp("dv", (lambda v, c=c: v.tensor_copy(
                    mcol[:, c["acol"]:c["acol"] + 1], wprt[:, 7:8])),
                    deps=[al_])

        wpump(2)

        wsl_sc = E(contextlib.ExitStack())
        wsl = [wsl_sc.enter_context(nc.sbuf_tensor(f"p1w{i}", [128, 16 * 512], BF16))
               for i in range(2)]
        fbounds = []
        off = 0
        while off < DIP:
            wdt = min(512, DIP_SSD - off) if off < DIP_SSD else min(512, DIP - off)
            fbounds.append((off, wdt))
            off += wdt
        wld = [None, None]
        for fc in range(2):
            off, wdt = fbounds[fc]
            wld[fc] = pb.dma("pl", wsl[fc][:, 0:16 * wdt].rearrange(
                "p (kt f) -> p kt f", kt=16),
                winT[:, off:off + wdt].rearrange("(kt p) f -> p kt f", p=128))

        with contextlib.ExitStack() as S:
            xt = [S.enter_context(nc.sbuf_tensor(f"p0x{i}", [128, DM], F32)) for i in range(2)]
            xnb = [S.enter_context(nc.sbuf_tensor(f"p0n{i}", [128, DM], BF16)) for i in range(2)]
            cols = S.enter_context(nc.sbuf_tensor("p0c", [128, 16], F32))
            trash = wtr
            pt = [S.enter_context(nc.psum_tensor(f"p0p{i}", [128, 128], BF16)) for i in range(2)]

            done_tr = {}
            xn_done = [None, None]
            for tt in range(NTT):
                b = tt % 2
                co = 8 * (tt % 2)
                ld = pb.dma("sp", xt[b][:], x_in[128 * tt:128 * (tt + 1), :],
                            deps=[xn_done[b]])
                sq = pb.comp("ac", (lambda s, b=b, co=co: s.activation(
                    trash[:], xt[b][:], AF.Square, accum_out=cols[:, co + 1:co + 2])),
                    deps=[ld])
                sm = pb.comp("dv", (lambda v, b=b, co=co: v.reduce_sum(
                    cols[:, co:co + 1], xt[b][:], AX.X)), deps=[ld])
                nm = pb.comp("dv", (lambda v, co=co: v.tensor_scalar(
                    cols[:, co + 2:co + 3], cols[:, co:co + 1], -1.0 / DM, None, ALU.mult)),
                    deps=[sm])
                ss = pb.comp("dv", (lambda v, co=co: v.tensor_scalar(
                    cols[:, co + 1:co + 2], cols[:, co + 1:co + 2], 1.0 / DM, None, ALU.mult)),
                    deps=[sq])
                nv = pb.comp("dv", (lambda v, co=co: v.scalar_tensor_tensor(
                    cols[:, co + 3:co + 4], cols[:, co + 2:co + 3], cols[:, co + 2:co + 3],
                    cols[:, co + 1:co + 2], ALU.mult, ALU.subtract)), deps=[nm, ss])
                ln = pb.comp("ac", (lambda s, co=co: s.activation(
                    cols[:, co + 3:co + 4], cols[:, co + 3:co + 4], AF.Ln,
                    bias=EPSC, scale=-1.0)), deps=[nv, c_eps])
                rs = pb.comp("ac", (lambda s, co=co: s.activation(
                    cols[:, co + 4:co + 5], cols[:, co + 3:co + 4], AF.Exp, scale=-0.5)),
                    deps=[ln])
                xn_done[b] = pb.comp("dv", (lambda v, b=b, co=co: v.tensor_scalar(
                    xnb[b][:], xt[b][:], cols[:, co + 2:co + 3], cols[:, co + 4:co + 5],
                    ALU.add, ALU.mult)),
                    deps=[rs, nm] + ([done_tr[(tt - 2, 15)]] if tt >= 2 else [c_id]))
                for k in range(16):
                    tr = pb.comp("pe", (lambda t, b=b, k=k: t.transpose(
                        pt[k % 2][:, :], xnb[b][:, 128 * k:128 * (k + 1)], identsb[:, :])),
                        deps=[xn_done[b], done_tr.get((tt, k - 2)),
                              done_tr.get((tt - 1, 14 + k))])
                    done_tr[(tt, k)] = pb.comp("ac", (lambda s, k=k, tt=tt:
                        s.copy(xnT[:, 2048 * k + 128 * tt: 2048 * k + 128 * (tt + 1)],
                               pt[k % 2][:, :])), deps=[tr])
                wpump(1, "pl")
                wpump(1, "ac")
            p0_done = pb.barrier()
            tap("xnT", xnT[:, :], p0_done)

        if BUILD_PHASES < 1:
            pb.finalize(block)
            return dram

        with contextlib.ExitStack() as S:
            ost = [S.enter_context(nc.sbuf_tensor(f"p1o{i}", [128, 512], BF16))
                   for i in range(2)]
            obc = [S.enter_context(nc.sbuf_tensor(f"p1b{i}", [128, 448], F32))
                   for i in range(2)]
            psA = [S.enter_context(nc.psum_tensor(f"p1p{i}", [128, 512], F32))
                   for i in range(2)]
            psW = S.enter_context(nc.psum_tensor("p1wps", [128, 8], F32))
            mmd = {}
            evd = {}
            std = {}
            for fc, (off, wdt) in enumerate(fbounds):
                if fc >= 2:
                    wld[fc % 2] = pb.dma("pl", wsl[fc % 2][:, 0:16 * wdt].rearrange(
                        "p (kt f) -> p kt f", kt=16),
                        winT[:, off:off + wdt].rearrange("(kt p) f -> p kt f", p=128),
                        deps=[mmd.get((fc - 2, NTT - 1))])
                for tt in range(NTT):
                    def mm(t, fc=fc, tt=tt, wdt=wdt):
                        ps = psA[tt % 2]
                        r = None
                        for kt in range(16):
                            r = t.matmul(ps[:, 0:wdt],
                                     xnT[:, 2048 * kt + 128 * tt: 2048 * kt + 128 * (tt + 1)],
                                     wsl[fc % 2][:, wdt * kt: wdt * (kt + 1)],
                                     start=(kt == 0), stop=(kt == 15))
                        return r
                    mmd[(fc, tt)] = pb.comp("pe", mm, deps=[
                        wld[fc % 2], evd.get((fc, tt - 2)), evd.get((fc - 1, NTT - 2 + tt))]
                        + (p0_done if fc == 0 and tt == 0 else []))
                    is_bc = (off == 4096)
                    war = (std.get((fc, tt - 2)) or []) + \
                          (std.get((fc - 1, NTT - 2 + tt)) or [])
                    if not is_bc:
                        evd[(fc, tt)] = pb.comp("ac", (lambda s, tt=tt, wdt=wdt:
                            s.copy(ost[tt % 2][:, 0:wdt], psA[tt % 2][:, 0:wdt])),
                            deps=[mmd[(fc, tt)]] + war)
                        if off + wdt <= DI:
                            dst = xs_hbm[128 * tt:128 * (tt + 1), off:off + wdt]
                        else:
                            dst = z_hbm[128 * tt:128 * (tt + 1),
                                        off - DIP_SSD:off - DIP_SSD + wdt]
                        std[(fc, tt)] = [pb.dma("sp", dst, ost[tt % 2][:, 0:wdt],
                                                deps=[evd[(fc, tt)]])]
                    else:
                        ebc = pb.comp("ac", (lambda s, tt=tt:
                            s.copy(obc[tt % 2][:, 0:448], psA[tt % 2][:, 0:448])),
                            deps=[mmd[(fc, tt)]] + war)
                        evd[(fc, tt)] = ebc
                        st1 = pb.dma("sp", bc_hbm[128 * tt:128 * (tt + 1), :],
                                     obc[tt % 2][:, 0:448], deps=[ebc])
                        std[(fc, tt)] = [st1]
                    wpump(1, "sp" if (16 * fc + tt) % 3 == 0 else "ac")
            while next(wchain, None) is not None:
                pass
            walpha_emit(psW)
            p1_done = pb.barrier()
            sp_xnt = pb.dma("sp", xnT_hbm, xnT[:, :], deps=p1_done)
            p1_done = pb.barrier()

        wsl_sc.close()
        xnt_sc.close()
        wscr_sc.close()

        if BUILD_PHASES < 2:
            pb.finalize(block)
            return dram

        ssd_sc = E(contextlib.ExitStack())
        SE = ssd_sc.enter_context
        acs = SE(nc.sbuf_tensor("acs", [CH, NCK * NH], F32))
        nacsT = SE(nc.sbuf_tensor("nacsT", [CH, NCK * NH], F32))
        expacsD = SE(nc.sbuf_tensor("expacsD", [128, NCK * NH], BF16))
        dsstk = SE(nc.sbuf_tensor("dsstk", [128, NCK * NH], BF16))
        dtlstk = SE(nc.sbuf_tensor("dtlstk", [128, NCK * NH], BF16))
        dbc = SE(nc.sbuf_tensor("dbc", [128, NCK * NH], F32))
        bstack = SE(nc.sbuf_tensor("bstack", [128, NCK * NS], BF16))
        bnT = SE(nc.sbuf_tensor("bnT", [128, T + 64], BF16))
        cnT = SE(nc.sbuf_tensor("cnT", [128, T], BF16))
        dful = SE(nc.sbuf_tensor("dful", [128, DI], BF16))
        oh_l = SE(nc.sbuf_tensor("oh_l", [CH, NH * CH], BF16))
        identf = SE(nc.sbuf_tensor("identf_sb", [CH, CH], F32))
        wrep = SE(nc.sbuf_tensor("wrep", [128, DI], BF16))

        with contextlib.ExitStack() as S:
            tmp = [S.enter_context(nc.sbuf_tensor(f"p2t{i}", [CH, NCK * NH], F32))
                   for i in range(4)]
            sinb = S.enter_context(nc.sbuf_tensor("p2sin", [CH, NCK * HD], F32))
            cosb = S.enter_context(nc.sbuf_tensor("p2cos", [CH, NCK * HD], F32))
            ang = S.enter_context(nc.sbuf_tensor("p2ang", [CH, NCK * HD], F32))
            bcscr = S.enter_context(nc.sbuf_tensor("p2bc", [CH, NCK * NS], F32))
            bn = bstack[0:CH, :]
            cn = S.enter_context(nc.sbuf_tensor("p2cn", [CH, NCK * NS], BF16))
            msb = S.enter_context(nc.sbuf_tensor("p2mb", [CH, NCK], F32))
            msc = S.enter_context(nc.sbuf_tensor("p2mc", [CH, NCK], F32))
            wet = S.enter_context(nc.sbuf_tensor("p2we", [CH, 2048], F32))
            wot = S.enter_context(nc.sbuf_tensor("p2wo", [CH, 2048], F32))
            rows = S.enter_context(nc.sbuf_tensor("p2r", [65, 4096], F32))
            crow = S.enter_context(nc.sbuf_tensor("p2cr", [NCK, 2 * HD], F32))
            shsc = S.enter_context(nc.sbuf_tensor("p2sh", [CH, NCK * NH], BF16))
            ps4 = [S.enter_context(nc.psum_tensor(f"p2p{i}", [CH, 512], F32))
                   for i in range(4)]
            ps4b_ = [S.enter_context(nc.psum_tensor(f"p2q{i}", [CH, 512], F32))
                     for i in range(2)]
            ps4b = [ps4b_[0], ps4b_[1], ps4b_[0], ps4b_[1]]
            ps128 = S.enter_context(nc.psum_tensor("p2d", [128, 512], F32))
            pmisc = S.enter_context(nc.psum_tensor("p2s", [128, 512], F32))
            pst = pmisc[:, 0:128]
            pstb = pmisc[:, 128:192].bitcast(BF16)
            pstc = pmisc[:, 192:256].bitcast(BF16)
            ps32 = pmisc[0:NCK, 256:256 + HD]

            _SLOT = {"A": (0, 0), "B": (0, 2048), "C": (32, 0), "D": (32, 2048),
                     "E": (64, 0), "F": (64, 2048)}

            def RO(s):
                p, c = _SLOT[s]
                return rows[p:p + 1, c:c + 2048]

            def bcload(col0, wdt, dst, deps):
                return pb.dma("sp", dst.rearrange("p (c w) -> p c w", w=wdt),
                              bc_hbm[:, col0:col0 + wdt].rearrange(
                                  "(c p) w -> p c w", p=CH), deps=deps)

            c_ohl = pb.dma("sp", oh_l[:], oh_l_in, deps=p1_done)
            c_idf = pb.dma("sp", identf[:], identf_in[0:CH, 0:CH], deps=p1_done)
            l_dtb = pb.dma("sp", RO("A"), dtb_row)
            l_alog = pb.dma("sp", RO("B"), alog_row)
            l_bwe = pb.dma("sp", RO("E"), bwe_row)
            l_bwo = pb.dma("sp", RO("F"), bwo_row)

            BCH0 = bcscr[0:CH, 0:2048].rearrange("p (c w) -> p c w", w=HD)
            BCH1 = bcscr[0:CH, 2048:4096].rearrange("p (c w) -> p c w", w=HD)
            l_th = bcload(384, HD, bcscr[0:CH, 0:2048], p1_done)
            l_dtr = bcload(256, HD, bcscr[0:CH, 2048:4096], p1_done)

            mm_th = [pb.comp("pe", (lambda t, q=q: t.matmul(
                ps4[q][:, :], TRIUI, BCH0[:, 8 * q:8 * (q + 1), :],
                start=True, stop=True)), deps=[l_th, c_tri]) for q in range(4)]
            ev_ang = [pb.comp("ac", (lambda s, q=q: s.copy(
                ang[:, 512 * q:512 * (q + 1)], ps4[q][:, :])), deps=[mm_th[q]])
                for q in range(4)]
            d_sums0 = pb.dma("pl", scrA, ang[63:64, :], deps=ev_ang)
            d_sums = pb.dma("pl", crow[0:NCK, 0:HD],
                            scrA.rearrange("o (c w) -> (o c) w", w=HD),
                            deps=[d_sums0])
            mm_cs = pb.comp("pe", lambda t: t.matmul(
                ps32[:, :], TRIUX, crow[0:NCK, 0:HD], start=True, stop=True),
                deps=[d_sums, c_trx])
            ev_cs = pb.comp("dv", lambda v: v.tensor_copy(crow[0:NCK, HD:2 * HD], ps32[:, :]),
                            deps=[mm_cs])
            d_flat0 = pb.dma("pl", scrB, crow[0:NCK, HD:2 * HD], deps=[ev_cs])
            d_flat = pb.dma("pl", RO("D"), scrB.rearrange("c w -> () (c w)"),
                            deps=[d_flat0])
            mm_cb = [pb.comp("pe", (lambda t, q=q: t.matmul(
                ps4[q][:, :], ONES32[:, 0:CH], RO("D")[:, 512 * q:512 * (q + 1)],
                start=True, stop=True)), deps=[d_flat, ev_ang[q], c_on32]) for q in range(4)]
            ad_ang = [pb.comp("dv", (lambda v, q=q: v.tensor_add(
                ang[:, 512 * q:512 * (q + 1)], ang[:, 512 * q:512 * (q + 1)],
                ps4[q][:, :])), deps=[mm_cb[q]]) for q in range(4)]

            sc_done = []
            for dst, shift in ((sinb, 0.0), (cosb, np.pi / 2)):
                s1 = pb.comp("dv", (lambda v, sh=shift: v.tensor_scalar(
                    tmp[0][:, 0:2048], ang[:], sh, 1.0 / (2 * np.pi), ALU.add, ALU.mult)),
                    deps=ad_ang + sc_done)
                s2 = pb.comp("dv", lambda v: v.tensor_scalar(
                    tmp[0][:, 0:2048], tmp[0][:, 0:2048], MROUND, -MROUND, ALU.add, ALU.add),
                    deps=[s1])
                s3 = pb.comp("dv", (lambda v, sh=shift: v.tensor_scalar(
                    tmp[1][:, 0:2048], ang[:], sh, None, ALU.add)), deps=[s2])
                s4 = pb.comp("dv", (lambda v, d=dst: v.scalar_tensor_tensor(
                    d[:], tmp[0][:, 0:2048], -2 * np.pi, tmp[1][:, 0:2048],
                    ALU.mult, ALU.add)), deps=[s3])
                s4b = pb.comp("dv", (lambda v, d=dst: v.tensor_scalar(
                    d[:], d[:], -3.1415925, 3.1415925, ALU.max, ALU.min)), deps=[s4])
                s5 = pb.comp("ac", (lambda s, d=dst: s.activation(d[:], d[:], AF.Sin)),
                             deps=[s4b])
                sc_done.append(s5)

            e_al = pb.comp("ac", lambda s: s.activation(RO("B"), RO("B"), AF.Exp),
                           deps=[l_alog])
            mm_dtb = []
            v_v = []
            for q in range(4):
                mm_dtb.append(pb.comp("pe", (lambda t, q=q: t.matmul(
                    ps4b[q][:, :], ONES[:, 0:CH], RO("A")[:, 512 * q:512 * (q + 1)],
                    start=True, stop=True)),
                    deps=[l_dtb, c_on, (v_v[q - 2] if q >= 2 else None)]))
                v_v.append(pb.comp("dv", (lambda v, q=q: v.tensor_add(
                    tmp[0][:, 512 * q:512 * (q + 1)],
                    BCH1[:, 8 * q:8 * (q + 1), :].rearrange("p c w -> p (c w)"),
                    ps4b[q][:, :])), deps=[mm_dtb[q], l_dtr]))
            a_ab = pb.comp("ac", lambda s: s.activation(
                tmp[1][:, 0:2048], tmp[0][:, 0:2048], AF.Abs, scale=-1.0),
                deps=v_v + sc_done)
            a_ex = pb.comp("ac", lambda s: s.activation(
                tmp[1][:, 0:2048], tmp[1][:, 0:2048], AF.Exp, scale=-1.0), deps=[a_ab])
            a_ln = pb.comp("ac", lambda s: s.activation(
                tmp[1][:, 0:2048], tmp[1][:, 0:2048], AF.Ln, bias=1.0), deps=[a_ex])
            v_dt = pb.comp("dv", lambda v: v.scalar_tensor_tensor(
                tmp[0][:, 0:2048], tmp[0][:, 0:2048], 0.0, tmp[1][:, 0:2048],
                ALU.max, ALU.add), deps=[a_ln])
            l_lam = bcload(320, HD, bcscr[0:CH, 0:2048], mm_th)
            a_lam = pb.comp("ac", lambda s: s.activation(
                tmp[2][:, 0:2048], bcscr[0:CH, 0:2048],
                AF.Sigmoid), deps=[l_lam, a_ln])
            mm_na = []
            v_A = []
            for q in range(4):
                mm_na.append(pb.comp("pe", (lambda t, q=q: t.matmul(
                    ps4b[q][:, :], NEGONES[:, 0:CH], RO("B")[:, 512 * q:512 * (q + 1)],
                    start=True, stop=True)),
                    deps=[e_al, s_non, v_v[q], v_v[(q + 2) % 4],
                          (v_A[q - 2] if q >= 2 else None)]))
                v_A.append(pb.comp("dv", (lambda v, q=q: v.tensor_mul(
                    tmp[3][:, 512 * q:512 * (q + 1)], tmp[0][:, 512 * q:512 * (q + 1)],
                    ps4b[q][:, :])), deps=[mm_na[q], v_dt]))
            v_d1 = pb.comp("dv", lambda v: v.tensor_mul(
                tmp[1][:, 0:2048], tmp[0][:, 0:2048], tmp[2][:, 0:2048]),
                deps=[v_dt, a_lam])
            v_d1b = pb.comp("dv", lambda v: v.tensor_copy(dtlstk[0:CH, :], tmp[1][:, 0:2048]),
                            deps=[v_d1])
            v_d2 = pb.comp("dv", lambda v: v.tensor_sub(
                tmp[1][:, 0:2048], tmp[0][:, 0:2048], tmp[1][:, 0:2048]), deps=[v_d1b])
            v_d2b = pb.comp("dv", lambda v: v.tensor_copy(shsc[0:CH, :], tmp[1][:, 0:2048]),
                            deps=[v_d2])
            d_shA = pb.dma("pl", dtlstk[65:128, :], shsc[0:CH - 1, :], deps=[v_d2b])
            d_shB = pb.dma("pl",
                           dtlstk[64:65, :].rearrange("o (c w) -> o c w", w=NH)[:, 1:NCK, :],
                           shsc[CH - 1:CH, :].rearrange("o (c w) -> o c w", w=NH)[:, 0:NCK - 1, :],
                           deps=[v_d2b])
            z_sh = pb.comp("dv", lambda v: v.memset(dtlstk[64:65, 0:NH], 0.0),
                           deps=[v_d2b])

            mm_ac = [pb.comp("pe", (lambda t, q=q: t.matmul(
                ps4[q][:, :], TRIUI, tmp[3][:, 512 * q:512 * (q + 1)],
                start=True, stop=True)), deps=[v_A[q], ad_ang[q]]) for q in range(4)]
            ev_ac = [pb.comp("ac", (lambda s, q=q: s.copy(
                acs[:, 512 * q:512 * (q + 1)], ps4[q][:, :])), deps=[mm_ac[q]])
                for q in range(4)]
            ev_ea = [pb.comp("ac", (lambda s, q=q: s.activation(
                expacsD[0:CH, 512 * q:512 * (q + 1)], ps4[q][:, :], AF.Exp)),
                deps=[mm_ac[q]]) for q in range(4)]
            d_ea = pb.dma("pl", expacsD[64:128, :], expacsD[0:CH, :], deps=ev_ea)
            tr_ac = {}
            ev_na = {}
            for c in range(NCK):
                tr_ac[c] = pb.comp("pe", (lambda t, c=c: t.transpose(
                    pst[0:CH, 0:CH], acs[:, NH * c:NH * (c + 1)], identf[:, :])),
                    deps=[ev_ac[c // 8], ev_na.get(c - 1), c_idf])
                ev_na[c] = pb.comp("dv", (lambda v, c=c: v.tensor_scalar(
                    nacsT[:, NH * c:NH * (c + 1)], pst[0:CH, 0:CH], -1.0, None, ALU.mult)),
                    deps=[tr_ac[c]])
            d_end = pb.dma("pl", RO("A"), acs[63:64, :], deps=ev_ac + [mm_dtb[3]])
            e_dec = pb.comp("ac", lambda s: s.activation(RO("B"), RO("A"), AF.Exp),
                            deps=[d_end, a_lam, mm_na[3]])
            mm_en = [pb.comp("pe", (lambda t, q=q: t.matmul(
                ps4[q][:, :], ONES[:, 0:CH], RO("A")[:, 512 * q:512 * (q + 1)],
                start=True, stop=True)), deps=[d_end, ev_ea[q], ev_ac[q]])
                for q in range(4)]
            v_ds = [pb.comp("dv", (lambda v, q=q: v.tensor_sub(
                tmp[1][:, 512 * q:512 * (q + 1)], ps4[q][:, :],
                acs[:, 512 * q:512 * (q + 1)])), deps=[mm_en[q], v_d2b]) for q in range(4)]
            a_ds = pb.comp("ac", lambda s: s.activation(
                dsstk[0:CH, :], tmp[1][:, 0:2048], AF.Exp), deps=v_ds)
            d_ds2 = pb.dma("pl", dsstk[64:128, :], dsstk[0:CH, :], deps=[a_ds])
            mm_db = {}
            ev_db = {}
            for q in range(4):
                mm_db[q] = pb.comp("pe", (lambda t, q=q: t.matmul(
                    ps128[:, :], ONES[0:1, 0:128], RO("B")[:, 512 * q:512 * (q + 1)],
                    start=True, stop=True)), deps=[e_dec, ev_db.get(q - 1)])
                ev_db[q] = pb.comp("dv", (lambda v, q=q: v.tensor_copy(
                    dbc[:, 512 * q:512 * (q + 1)], ps128[:, :])), deps=[mm_db[q]])

            BCF = bcscr[0:CH, :].rearrange("p (c w) -> p c w", w=NS)

            def rope_build(ms, werow_ap, worow_ap, dst, extra):
                sqd = pb.comp("ac", (lambda s: s.activation(
                    tmp[1][:, 0:2048].rearrange("p (c w) -> p c w", w=HD),
                    BCF[:, :, 0::2], AF.Square)), deps=extra)
                sqd2 = pb.comp("ac", (lambda s: s.activation(
                    tmp[3][:, 0:2048].rearrange("p (c w) -> p c w", w=HD),
                    BCF[:, :, 1::2], AF.Square)), deps=[sqd])
                ad = pb.comp("dv", lambda v: v.tensor_add(
                    tmp[1][:, 0:2048], tmp[1][:, 0:2048], tmp[3][:, 0:2048]), deps=[sqd2])
                rd = pb.comp("dv", (lambda v, m=ms: v.tensor_reduce(
                    m[:, :].rearrange("p c -> p c ()"),
                    tmp[1][:, 0:2048].rearrange("p (c w) -> p c w", w=HD),
                    AX.X, ALU.add)), deps=[ad])
                r1 = pb.comp("ac", (lambda s, m=ms: s.activation(
                    m[:, :], m[:, :], AF.Ln, bias=EPSC[0:CH, :], scale=1.0 / 128)),
                    deps=[rd, c_eps])
                r2 = pb.comp("ac", (lambda s, m=ms: s.activation(
                    m[:, :], m[:, :], AF.Exp, scale=-0.5)), deps=[r1])
                mwA = []
                mwB = []
                ew = []
                ow = []
                for q in range(4):
                    mwA.append(pb.comp("pe", (lambda t, q=q, w=werow_ap: t.matmul(
                        ps4[q][:, :], ONES64[:, 0:CH], w[:, 512 * q:512 * (q + 1)],
                        start=True, stop=True)), deps=[v_ds[q], r2, c_on64] + extra))
                    mwB.append(pb.comp("pe", (lambda t, q=q, w=worow_ap: t.matmul(
                        ps4b[q][:, :], ONES64[:, 0:CH], w[:, 512 * q:512 * (q + 1)],
                        start=True, stop=True)),
                        deps=[v_A[q], v_A[(q + 2) % 4], r2, c_on64,
                              (ow[q - 2] if q >= 2 else None)] + extra))
                    ew.append(pb.comp("dv", (lambda v, q=q: v.tensor_copy(
                        wet[:, 512 * q:512 * (q + 1)], ps4[q][:, :])), deps=[mwA[q]]))
                    ow.append(pb.comp("dv", (lambda v, q=q: v.tensor_copy(
                        wot[:, 512 * q:512 * (q + 1)], ps4b[q][:, :])), deps=[mwB[q]]))
                ms3 = ms[:, :].rearrange("p c -> p c ()").broadcast_to([CH, NCK, HD])
                x1 = pb.comp("dv", (lambda v, m3=ms3: v.tensor_tensor(
                    tmp[1][:, 0:2048].rearrange("p (c w) -> p c w", w=HD),
                    BCF[:, :, 0::2], m3, ALU.mult)), deps=[r2, rd])
                x1w = pb.comp("dv", lambda v: v.tensor_mul(
                    tmp[1][:, 0:2048], tmp[1][:, 0:2048], wet[:, :]), deps=[x1] + ew)
                x2 = pb.comp("dv", (lambda v, m3=ms3: v.tensor_tensor(
                    tmp[3][:, 0:2048].rearrange("p (c w) -> p c w", w=HD),
                    BCF[:, :, 1::2], m3, ALU.mult)), deps=[r2, ad])
                x2w = pb.comp("dv", lambda v: v.tensor_mul(
                    tmp[3][:, 0:2048], tmp[3][:, 0:2048], wot[:, :]), deps=[x2] + ow)
                dst3 = dst[0:CH, :].rearrange("p (c w) -> p c w", w=NS)
                t1 = pb.comp("dv", lambda v: v.tensor_mul(
                    tmp[0][:, 0:2048], cosb[:], tmp[1][:, 0:2048]),
                    deps=[x1w] + sc_done + [v_dt, v_d1])
                tA = pb.comp("dv", lambda v: v.tensor_mul(
                    tmp[2][:, 0:2048], sinb[:], tmp[3][:, 0:2048]),
                    deps=[x2w] + sc_done + [a_lam, v_d1])
                evn = pb.comp("dv", (lambda v, d=dst3: v.tensor_tensor(
                    d[:, :, 0:NS:2], tmp[0][:, 0:2048].rearrange("p (c w) -> p c w", w=HD),
                    tmp[2][:, 0:2048].rearrange("p (c w) -> p c w", w=HD), ALU.subtract)),
                    deps=[t1, tA])
                t3 = pb.comp("dv", lambda v: v.tensor_mul(
                    tmp[0][:, 0:2048], sinb[:], tmp[1][:, 0:2048]), deps=[evn])
                t4 = pb.comp("dv", lambda v: v.tensor_mul(
                    tmp[2][:, 0:2048], cosb[:], tmp[3][:, 0:2048]), deps=[evn])
                odd = pb.comp("dv", (lambda v, d=dst3: v.tensor_tensor(
                    d[:, :, 1:NS:2], tmp[0][:, 0:2048].rearrange("p (c w) -> p c w", w=HD),
                    tmp[2][:, 0:2048].rearrange("p (c w) -> p c w", w=HD), ALU.add)),
                    deps=[t3, t4])
                return odd

            l_B = bcload(0, NS, bcscr[0:CH, :],
                         [a_lam, v_ds[3], mm_ac[3], v_v[3], mm_th[3]])
            bn_done = rope_build(msb, RO("E"), RO("F"), bn,
                                 [l_B, v_ds[3], a_ds, z_sh, d_shA, d_shB, l_bwe,
                                  l_bwo, v_d2b, ev_na[NCK - 1]])
            l_cwe = pb.dma("sp", RO("E"), cwe_row, deps=[bn_done])
            l_cwo = pb.dma("sp", RO("F"), cwo_row, deps=[bn_done])
            l_C = bcload(128, NS, bcscr[0:CH, :], [bn_done])
            cn_done = rope_build(msc, RO("E"), RO("F"), cn,
                                 [l_C, bn_done, l_cwe, l_cwo])

            d_bs1 = pb.dma("pl", bstack[65:128, :], bn[0:CH - 1, :], deps=[bn_done])
            d_bs2 = pb.dma("pl",
                           bstack[64:65, :].rearrange("o (c w) -> o c w", w=NS)[:, 1:NCK, :],
                           bn[CH - 1:CH, :].rearrange("o (c w) -> o c w", w=NS)[:, 0:NCK - 1, :],
                           deps=[bn_done])
            z_bs = pb.comp("dv", lambda v: v.memset(bstack[64:65, 0:NS], 0.0),
                           deps=[bn_done])
            z_bnT = pb.comp("dv", lambda v: v.memset(bnT[:, 0:1], 0.0), deps=[c_id])
            tr_b = {}
            for c in range(NCK):
                trb = pb.comp("pe", (lambda t, c=c: t.transpose(
                    pstb[:, 0:CH], bn[0:CH, NS * c:NS * (c + 1)], identsb[0:CH, 0:CH])),
                    deps=[bn_done, tr_b.get(c - 1)])
                tr_b[c] = pb.comp("ac", (lambda s, c=c: s.copy(
                    bnT[:, 1 + CH * c:1 + CH * (c + 1)], pstb[:, 0:CH])), deps=[trb])
            tr_c = {}
            for c in range(NCK):
                trc = pb.comp("pe", (lambda t, c=c: t.transpose(
                    pstc[:, 0:CH], cn[0:CH, NS * c:NS * (c + 1)], identsb[0:CH, 0:CH])),
                    deps=[cn_done, tr_c.get(c - 1), tr_b[NCK - 1]])
                tr_c[c] = pb.comp("ac", (lambda s, c=c: s.copy(
                    cnT[:, CH * c:CH * (c + 1)], pstc[:, 0:CH])), deps=[trc])

            l_dr = pb.dma("sp", RO("D"), drow_in[:, 0:2048], deps=[d_flat, mm_cb[3]])
            l_dr2 = pb.dma("sp", RO("C"), drow_in[:, 2048:4096],
                           deps=[e_dec, mm_na[3], mm_db[3]])
            mm_df = {}
            ev_df = {}
            for q in range(8):
                src = (RO("D") if q < 4 else RO("C"))
                mm_df[q] = pb.comp("pe", (lambda t, q=q, src=src: t.matmul(
                    ps128[:, :], ONES32[:, 0:128], src[:, 512 * (q % 4):512 * (q % 4 + 1)],
                    start=True, stop=True)),
                    deps=[l_dr, l_dr2, ev_db[3], ev_df.get(q - 1), c_on32])
                ev_df[q] = pb.comp("dv", (lambda v, q=q: v.tensor_copy(
                    dful[:, 512 * q:512 * (q + 1)], ps128[:, :])), deps=[mm_df[q]])
            l_b2 = pb.dma("sp", RO("A"), b2row, deps=[e_dec, mm_en[3], mm_db[3]])
            l_wr = pb.dma("sp", RO("B"), wrow[:, 0:2048], deps=[mm_db[3]])
            l_wr2 = pb.dma("sp", RO("E"), wrow[:, 2048:4096], deps=[cn_done])
            mm_b2 = {}
            ev_b2 = {}
            for q in range(4):
                mm_b2[q] = pb.comp("pe", (lambda t, q=q: t.matmul(
                    ps128[:, :], ONES[0:1, 0:128], RO("A")[:, 512 * q:512 * (q + 1)],
                    start=True, stop=True)), deps=[l_b2, ev_df[7], mm_b2.get(q - 1),
                                                   ev_b2.get(q - 1)])
                ev_b2[q] = pb.comp("dv", (lambda v, q=q: v.tensor_copy(
                    b2rep[:, 512 * q:512 * (q + 1)], ps128[:, :])), deps=[mm_b2[q]])
            mm_wr = {}
            ev_wr = {}
            for q in range(8):
                src = (RO("B") if q < 4 else RO("E"))
                lhs1 = (ONES if q < 4 else ONES64)
                mm_wr[q] = pb.comp("pe", (lambda t, q=q, src=src, lhs1=lhs1: t.matmul(
                    ps128[:, :], lhs1[:, 0:128], src[:, 512 * (q % 4):512 * (q % 4 + 1)],
                    start=True, stop=True)),
                    deps=[l_wr, l_wr2, ev_b2[3], ev_wr.get(q - 1), c_on64])
                ev_wr[q] = pb.comp("dv", (lambda v, q=q: v.tensor_copy(
                    wrep[:, 512 * q:512 * (q + 1)], ps128[:, :])), deps=[mm_wr[q]])

            p2a_done = pb.barrier()
            for nm, ap_ in (("acs", acs[:, :]), ("ang", ang[:, :]), ("bn", bn[:, :]),
                            ("cn", cn[:, :]),
                            ("dsstk", dsstk[:, :]), ("dtlstk", dtlstk[:, :]),
                            ("dbc", dbc[:, :]),
                            ("bstack", bstack[:, :]), ("bnT", bnT[:, 0:T + 1]),
                            ("cnT", cnT[:, :]),
                            ("expacsD", expacsD[:, :]), ("nacsT", nacsT[:, :]),
                            ("dful", dful[:, :]),
                            ("wrep", wrep[:, :]), ("b2rep", b2rep[:, :])):
                tap(nm, ap_, p2a_done)
            tap("dt", tmp[0][:, :], p2a_done)

        if BUILD_PHASES < 3:
            pb.finalize(block)
            return dram

        with contextlib.ExitStack() as S:
            xsal = [S.enter_context(nc.sbuf_tensor(f"sxa{i}", [128, DI], BF16))
                    for i in range(2)]
            statebf = S.enter_context(nc.sbuf_tensor("sstbf", [128, DI], BF16))
            mpre = S.enter_context(nc.sbuf_tensor("smp", [128, 2 * 512], BF16))
            statbA = [S.enter_context(nc.sbuf_tensor(f"sstb{i}", [128, DI], BF16))
                      for i in range(2)]
            gmst = S.enter_context(nc.sbuf_tensor("sgm", [128, 2 * CH], BF16))
            gmtmp = S.enter_context(nc.sbuf_tensor("sgt", [CH, CH], BF16))
            lhsA = S.enter_context(nc.sbuf_tensor("sla", [128, 128], F32))
            rhsA = S.enter_context(nc.sbuf_tensor("sra", [128, NH * CH], F32))
            state = S.enter_context(nc.sbuf_tensor("sste", [128, DI], F32))
            ych = [S.enter_context(nc.sbuf_tensor(f"syc{i}", [CH, 2 * DI], BF16))
                   for i in range(2)]
            ps_d = [S.enter_context(nc.psum_tensor(f"qd{i}", [128, 512], F32))
                    for i in range(2)]
            ps_g = S.enter_context(nc.psum_tensor("qg", [CH, 128], F32))
            ps_yd = [S.enter_context(nc.psum_tensor(f"qyd{i}", [CH, 512], F32))
                     for i in range(2)]
            ps_yo = S.enter_context(nc.psum_tensor("qyo", [CH, 512], F32))
            ps_st = [S.enter_context(nc.psum_tensor(f"qst{i}", [128, 512], F32))
                     for i in range(2)]

            l_oh = pb.dma("sp", rhsA[0:NH, :], oh_hl_in, deps=p2a_done)
            onesm1 = pb.comp("dv", lambda v: v.memset(lhsA[64:65, :], -1.0),
                             deps=p2a_done)
            zro_st = [pb.comp("dv", (lambda v, q=q: v.memset(
                state[:, 512 * q:512 * (q + 1)], 0.0)), deps=p2a_done) for q in range(8)]
            zro_st += [pb.comp("dv", lambda v: v.memset(statebf[:, :], 0.0),
                               deps=p2a_done)]

            P = {}

            def prep_head(c):
                cp = c % 2
                buf = c % 2
                tok0 = CH * c
                lda = pb.dma("sp", xsal[buf][0:CH, :], xs_hbm[tok0:tok0 + CH, :],
                             deps=[P.get(("smm", c - 2, 7))])
                if c > 0:
                    lds = pb.dma("sp", xsal[buf][64:128, :],
                                 xs_hbm[tok0 - 1:tok0 + CH - 1, :],
                                 deps=[P.get(("smm", c - 2, 7))])
                    shz = None
                else:
                    lds = pb.dma("sp", xsal[0][65:128, :], xs_hbm[0:CH - 1, :],
                                 deps=p2a_done)
                    shz = pb.comp("dv", lambda v: v.memset(xsal[0][64:65, :], 0.0),
                                  deps=p2a_done)
                dtls = dtlstk[:, NH * c:NH * (c + 1)].rearrange(
                    "p w -> p () w").broadcast_to([128, HD, NH])
                P[("xd", c)] = pb.comp("dv", (lambda v, b=buf, d=dtls: v.tensor_tensor(
                    xsal[b][:].rearrange("p (a w) -> p a w", w=NH),
                    xsal[b][:].rearrange("p (a w) -> p a w", w=NH), d, ALU.mult)),
                    deps=[lda, lds, shz])
                g_mm1 = pb.comp("pe", (lambda t, tok0=tok0: t.matmul(
                    ps_g[:, 0:CH], bnT[:, 1 + tok0:1 + tok0 + CH],
                    cnT[:, tok0:tok0 + CH], start=True, stop=False)),
                    deps=[P.get(("gm1", c - 1)), P.get(("gm2t", c - 1)), l_oh, onesm1])
                g_mm2 = pb.comp("pe", (lambda t, tok0=tok0: t.matmul(
                    ps_g[:, CH:128], bnT[:, tok0:tok0 + CH],
                    cnT[:, tok0:tok0 + CH], start=False, stop=True)), deps=[g_mm1])
                gm1 = pb.comp("dv", (lambda v, cp=cp: v.tensor_tensor(
                    gmst[0:CH, CH * cp:CH * (cp + 1)], ps_g[:, 0:CH], trilb[:, :],
                    ALU.mult)), deps=[g_mm2, c_tril, P.get(("stb", c - 2, 7))])
                gm2t = pb.comp("dv", lambda v: v.tensor_tensor(
                    gmtmp[:, :], ps_g[:, CH:128], trilb[:, :], ALU.mult),
                    deps=[g_mm2, P.get(("gmdn", c - 1))])
                gmdn = pb.dma("pl", gmst[64:128, CH * cp:CH * (cp + 1)], gmtmp[:, :],
                              deps=[gm2t, P.get(("stb", c - 2, 7))])
                P[("gm1", c)], P[("gm2t", c)], P[("gmdn", c)] = gm1, gm2t, gmdn
                la1 = pb.comp("dv", (lambda v, c=c: v.tensor_copy(
                    lhsA[0:CH, 0:CH], nacsT[:, NH * c:NH * (c + 1)])),
                    deps=[P.get(("dmm", c - 1, 7)), onesm1])
                la2 = pb.comp("dv", (lambda v, c=c: v.tensor_copy(
                    lhsA[0:CH, CH:128], nacsT[:, NH * c:NH * (c + 1)])), deps=[la1])
                flt0 = pb.dma("pl", scrC, nacsT[:, NH * c:NH * (c + 1)],
                              deps=[P.get(("flt", c - 1))])
                flt = pb.dma("pl", rhsA[64:65, :],
                             scrC.rearrange("h s -> () (h s)"),
                             deps=[flt0, P.get(("dmm", c - 1, 7))])
                P[("flt", c)] = flt
                P[("la", c)] = la2

            def prep_g(c, g):
                cp = c % 2
                pd = ps_d[g % 2]
                war_mp = P.get(("mp", c, g - 2)) or P.get(("mp", c - 1, g + 6))
                dmm = pb.comp("pe", (lambda t, g=g, pd=pd: (
                    t.matmul(pd[:, :], lhsA[0:65, :],
                             rhsA[0:65, 512 * g:512 * (g + 1)], start=True, stop=False),
                    t.matmul(pd[:, :], maskBb[:, :],
                             oh_l[:, 512 * g:512 * (g + 1)], start=False, stop=True))[-1]),
                    deps=[P[("la", c)], P[("flt", c)], l_oh, c_mb, c_ohl, war_mp])
                P[("dmm", c, g)] = dmm
                war_stb = P.get(("stb", c, g - 2)) or P.get(("stb", c - 1, g + 6))
                mp = pb.comp("ac", (lambda s, g=g, pd=pd: s.activation(
                    mpre[:, 512 * (g % 2):512 * (g % 2 + 1)], pd[:, :], AF.Exp)),
                    deps=[dmm, war_stb])
                P[("mp", c, g)] = mp

                def stbf(v, g=g, cp=cp):
                    r = None
                    for i in range(8):
                        r = v.tensor_tensor(
                            statbA[cp][:, 512 * g + 64 * i:512 * g + 64 * (i + 1)],
                            mpre[:, 512 * (g % 2) + 64 * i:
                                 512 * (g % 2) + 64 * (i + 1)],
                            gmst[:, CH * cp:CH * (cp + 1)], ALU.mult)
                    return r
                P[("stb", c, g)] = pb.comp("dv", stbf, deps=[
                    mp, P[("gmdn", c)], P[("gm1", c)], P.get(("ym", c - 2, g))])

            def comp_g(c, g):
                buf = c % 2
                cp = c % 2
                tok0 = CH * c

                def ydmm(t, g=g, b=buf, cp=cp):
                    r = None
                    for i in range(8):
                        h = 8 * g + i
                        mv = xsal[b][:].rearrange("p (a w) -> p a w", w=NH)[:, :, h]
                        r = t.matmul(ps_yd[g % 2][:, 64 * i:64 * (i + 1)],
                                     statbA[cp][:, 512 * g + 64 * i:
                                                512 * g + 64 * (i + 1)], mv,
                                     start=(i == 0), stop=(i == 7))
                    return r
                ym = pb.comp("pe", ydmm, deps=[
                    P[("stb", c, g)], P[("xd", c)],
                    P.get(("yde", c, g - 2)) or P.get(("yde", c - 1, g + 6))])
                P[("ym", c, g)] = ym

                def yomm(t, g=g, tok0=tok0):
                    r = None
                    for i in range(8):
                        h = 8 * g + i
                        mv = statebf[:, :].rearrange("p (a w) -> p a w", w=NH)[:, :, h]
                        r = t.matmul(ps_yo[:, 64 * i:64 * (i + 1)],
                                     cnT[:, tok0:tok0 + CH], mv,
                                     start=(i == 0), stop=(i == 7))
                    return r
                yo = pb.comp("pe", yomm, deps=[
                    ym, P.get(("yoe", c, g - 1)) or P.get(("yoe", c - 1, 7)),
                    P.get(("sbf", c - 1))] + ([] if c > 0 else zro_st))
                P[("yo", c, g)] = yo

                war_ych = P.get(("ychst", buf)) or []
                yde = pb.comp("ac", (lambda s, g=g, b=buf: s.copy(
                    ych[b][0:CH, 0:DI].rearrange(
                        "p (a w) -> p a w", w=NH)[:, :, 8 * g:8 * (g + 1)],
                    ps_yd[g % 2][:, :].rearrange("p (i w) -> p w i", i=8))),
                    deps=[ym] + war_ych)
                P[("yde", c, g)] = yde
                yoe = pb.comp("ac", (lambda s, g=g, b=buf: s.copy(
                    ych[b][0:CH, DI:2 * DI].rearrange(
                        "p (a w) -> p a w", w=NH)[:, :, 8 * g:8 * (g + 1)],
                    ps_yo[:, :].rearrange("p (i w) -> p w i", i=8))),
                    deps=[yo] + war_ych)
                P[("yoe", c, g)] = yoe

            def comp_tail(c):
                buf = c % 2
                dss = dsstk[:, NH * c:NH * (c + 1)].rearrange(
                    "p w -> p () w").broadcast_to([128, HD, NH])
                dx = pb.comp("dv", (lambda v, b=buf, d=dss: v.tensor_tensor(
                    xsal[b][:].rearrange("p (a w) -> p a w", w=NH),
                    xsal[b][:].rearrange("p (a w) -> p a w", w=NH), d, ALU.mult)),
                    deps=[P[("ym", c, 7)], P[("xd", c)]])
                P[("dx", c)] = dx
                for g in range(8):
                    pq = ps_st[g % 2]
                    war_upd = P.get(("upd", c, g - 2)) or P.get(("upd", c - 1, g + 6))
                    smm = pb.comp("pe", (lambda t, g=g, pq=pq, b=buf, c=c: t.matmul(
                        pq[:, :], bstack[:, NS * c:NS * (c + 1)],
                        xsal[b][:].rearrange("p (a w) -> p a w", w=NH)[:, :, 8 * g:8 * (g + 1)],
                        start=True, stop=True)),
                        deps=[dx, war_upd, P[("yo", c, 7)]])
                    P[("smm", c, g)] = smm

                    def updf(v, g=g, pq=pq, c=c):
                        r = None
                        for i in range(8):
                            h = 8 * g + i
                            sl = state[:, :].rearrange("p (a w) -> p a w", w=NH)[:, :, h]
                            pv = pq[:, :].rearrange("p (a w) -> p a w", w=8)[:, :, i]
                            r = v.scalar_tensor_tensor(
                                sl, sl, dbc[:, NH * c + h:NH * c + h + 1], pv,
                                ALU.mult, ALU.add)
                        return r
                    P[("upd", c, g)] = pb.comp("dv", updf, deps=[smm]
                                               + ([] if c > 0 else zro_st))
                sbf = pb.comp("dv", lambda v: v.tensor_copy(statebf[:, :], state[:, :]),
                              deps=[P[("upd", c, 7)], P[("yo", c, 7)]])
                P[("sbf", c)] = sbf
                evs = [P[("yde", c, g)] for g in range(8)] + \
                      [P[("yoe", c, g)] for g in range(8)]
                tok0 = CH * c
                sd = pb.dma("sp", yd_hbm[tok0:tok0 + CH, :], ych[buf][0:CH, 0:DI],
                            deps=evs)
                so = pb.dma("sp", yo_hbm[tok0:tok0 + CH, :], ych[buf][0:CH, DI:2 * DI],
                            deps=evs)
                P[("ychst", buf)] = [sd, so]

            prep_head(0)
            for g in range(8):
                prep_g(0, g)
            for c in range(NCK):
                if c + 1 < NCK:
                    prep_head(c + 1)
                for g in range(8):
                    if c + 1 < NCK:
                        prep_g(c + 1, g)
                    comp_g(c, g)
                comp_tail(c)
            p2b_done = pb.barrier()

        if BUILD_PHASES < 4:
            pb.finalize(block)
            return dram

        with contextlib.ExitStack() as S:
            ydt = [S.enter_context(nc.sbuf_tensor(f"g0_{i}", [128, DI], BF16)) for i in range(2)]
            yot = [S.enter_context(nc.sbuf_tensor(f"g1_{i}", [128, DI], BF16)) for i in range(2)]
            xst = [S.enter_context(nc.sbuf_tensor(f"g2_{i}", [128, DI], BF16)) for i in range(2)]
            zt = [S.enter_context(nc.sbuf_tensor(f"g3_{i}", [128, DI], BF16)) for i in range(2)]
            ybuf = S.enter_context(nc.sbuf_tensor("g4", [128, DI], BF16))
            zsb = S.enter_context(nc.sbuf_tensor("g5", [128, DI], BF16))
            ynb = S.enter_context(nc.sbuf_tensor("g6", [128, DI], BF16))
            ynstg = [S.enter_context(nc.sbuf_tensor(f"g7_{i}", [128, DI], BF16)) for i in range(2)]
            msc_ = S.enter_context(nc.sbuf_tensor("g8", [128, NTT], F32))
            ptr = [S.enter_context(nc.psum_tensor(f"g9_{i}", [128, 128], BF16)) for i in range(2)]

            stg_done = [None, None]
            cons = [None, None]
            prev_sq = None
            prev_tr_last = None
            sq2 = None
            for tt in range(NTT):
                b = tt % 2
                r0 = 128 * tt
                l1 = pb.dma("sp", ydt[b][:], yd_hbm[r0:r0 + 128, :],
                            deps=[cons[b]] + (p2b_done if tt < 2 else []))
                l2 = pb.dma("sp", yot[b][:], yo_hbm[r0:r0 + 128, :], deps=[cons[b]])
                l3 = pb.dma("sp", xst[b][:], xs_hbm[r0:r0 + 128, :], deps=[cons[b]])
                l4 = pb.dma("pl", zt[b][:], z_hbm[r0:r0 + 128, :], deps=[cons[b]])
                e0 = expacsD[0:CH, NH * 2 * tt:NH * (2 * tt + 1)].rearrange(
                    "p w -> p () w").broadcast_to([CH, HD, NH])
                e1 = expacsD[64:128, NH * (2 * tt + 1):NH * (2 * tt + 2)].rearrange(
                    "p w -> p () w").broadcast_to([CH, HD, NH])
                ya = pb.comp("dv", (lambda v, b=b, e=e0: v.tensor_tensor(
                    ybuf[0:CH, :].rearrange("p (a w) -> p a w", w=NH),
                    yot[b][0:CH, :].rearrange("p (a w) -> p a w", w=NH), e, ALU.mult)),
                    deps=[l2, stg_done[b], prev_sq])
                yb = pb.comp("dv", (lambda v, b=b, e=e1: v.tensor_tensor(
                    ybuf[64:128, :].rearrange("p (a w) -> p a w", w=NH),
                    yot[b][64:128, :].rearrange("p (a w) -> p a w", w=NH), e, ALU.mult)),
                    deps=[l2, stg_done[b], prev_sq])
                yc = pb.comp("dv", (lambda v, b=b: v.tensor_add(
                    ybuf[:, :], ybuf[:, :], ydt[b][:, :])), deps=[ya, yb, l1])
                yd2 = pb.comp("dv", (lambda v, b=b: v.tensor_mul(
                    zsb[:, :], xst[b][:, :], dful[:, :])), deps=[l3, yc, prev_sq])
                ye = pb.comp("dv", lambda v: v.tensor_add(ybuf[:, :], ybuf[:, :],
                                                          zsb[:, :]), deps=[yd2])
                zs = pb.comp("ac", (lambda s, b=b: s.activation(
                    zsb[:, :], zt[b][:, :], AF.Silu)), deps=[l4, ye])
                g2 = pb.comp("dv", lambda v: v.tensor_mul(ybuf[:, :], ybuf[:, :],
                                                          zsb[:, :]), deps=[zs])
                cons[b] = g2
                sq2 = pb.comp("ac", (lambda s, tt=tt: s.activation(
                    zsb[:, :], ybuf[:, :], AF.Square,
                    accum_out=msc_[:, tt:tt + 1])), deps=[g2])
                prev_sq = sq2
                yn = pb.comp("dv", lambda v: v.tensor_mul(ynb[:, :], ybuf[:, :],
                                                          wrep[:, :]),
                             deps=[sq2, prev_tr_last])
                trs = []
                for k in range(32):
                    tr = pb.comp("pe", (lambda t, k=k: t.transpose(
                        ptr[k % 2][:, :], ynb[:, 128 * k:128 * (k + 1)], identsb[:, :])),
                        deps=[yn, trs[k - 2] if k >= 2 else stg_done[b]])
                    ev = pb.comp("dv", (lambda v, k=k, b=b: v.tensor_copy(
                        ynstg[b][:, 128 * k:128 * (k + 1)], ptr[k % 2][:, :])), deps=[tr])
                    trs.append(ev)
                prev_tr_last = trs[-1]
                stg_done[b] = pb.dma("sp", ynT_hbm[tt, :, :], ynstg[b][:, :],
                                     deps=trs)
            r1 = pb.comp("dv", lambda v: v.tensor_scalar(
                msc_[:, :], msc_[:, :], 1.0 / DI, RMS_EPS,
                ALU.mult, ALU.add), deps=[sq2, stg_done[0], stg_done[1]])
            r2 = pb.comp("dv", lambda v: v.reciprocal(msc_[:, :], msc_[:, :]), deps=[r1])
            r3 = pb.comp("ac", lambda s: s.activation(msc_[:, :], msc_[:, :], AF.Sqrt),
                         deps=[r2])
            st_r = pb.dma("sp", rstd_out, msc_[:, :], deps=[r3])
            p3a_done = pb.barrier()
        ssd_sc.close()

        if BUILD_PHASES < 5:
            pb.finalize(block)
            return dram

        if BUILD_PHASES < 6:
            pb.finalize(block)
            return dram

        with contextlib.ExitStack() as S:
            xnT2 = S.enter_context(nc.sbuf_tensor("xnT2", [128, 16 * T], BF16))
            t1b = [S.enter_context(nc.sbuf_tensor(f"f1t{i}", [128, 16 * 1024], BF16))
                   for i in range(2)]
            hst = [S.enter_context(nc.sbuf_tensor(f"f1h{i}", [128, 2048], BF16))
                   for i in range(2)]
            b1sb = S.enter_context(nc.sbuf_tensor("f1b", [128, HF // 128], F32))
            psF = [S.enter_context(nc.psum_tensor(f"f1p{i}", [128, 512], F32))
                   for i in range(8)]
            l_b1 = pb.dma("sp", b1sb[:], b1c_in, deps=p3a_done)
            l_xnt = pb.dma("sp", xnT2[:, :], xnT_hbm, deps=p3a_done)
            ldf = {}
            mmf = {}
            evf = {}
            stf = {}
            for g in range(HF // 1024):
                ldf[g] = pb.dma("sp", t1b[g % 2][:].rearrange(
                    "p (k q) -> p k q", k=16),
                    t1_hbm[:, 1024 * g:1024 * (g + 1)].rearrange(
                        "(k p) q -> p k q", p=128),
                    deps=[mmf.get((g - 2, 7))] + (p3a_done if g < 2 else []))
                for hi in range(8):
                    hft = 8 * g + hi

                    def mmall(t, g=g, hi=hi):
                        r = None
                        for kt in range(16):
                            for q in range(4):
                                pf = psF[(hi % 2) * 4 + q]
                                r = t.matmul(
                                    pf[:, :],
                                    t1b[g % 2][:, 1024 * kt + 128 * hi:
                                               1024 * kt + 128 * (hi + 1)],
                                    xnT2[:, 2048 * kt + 512 * q:
                                         2048 * kt + 512 * (q + 1)],
                                    start=(kt == 0), stop=(kt == 15))
                        return r
                    mmf[(g, hi)] = pb.comp("pe", mmall, deps=[
                        ldf[g], l_xnt,
                        evf.get((g, hi - 2)), evf.get((g - 1, 6 + hi))])

                    def ev(s, g=g, hi=hi, hft=hft):
                        r = None
                        for q in range(4):
                            pf = psF[(hi % 2) * 4 + q]
                            r = s.activation(hst[hi % 2][:, 512 * q:512 * (q + 1)],
                                             pf[:, :], AF.Gelu,
                                             bias=b1sb[:, hft:hft + 1],
                                             scale=mcol[:, 2:3])
                        return r
                    evf[(g, hi)] = pb.comp("ac", ev, deps=[
                        mmf[(g, hi)], l_b1,
                        stf.get((g, hi - 2)), stf.get((g - 1, 6 + hi))])
                    stf[(g, hi)] = pb.dma("sp", h_hbm[128 * hft:128 * (hft + 1), :],
                                          hst[hi % 2][:], deps=[evf[(g, hi)]])
            f1_done = pb.barrier()

        if BUILD_PHASES < 7:
            pb.finalize(block)
            return dram

        with contextlib.ExitStack() as S:
            wo = S.enter_context(nc.sbuf_tensor("p3w", [128, 32 * 2048], BF16))
            ynb_ = [S.enter_context(nc.sbuf_tensor(f"p3y{i}", [128, 32 * 128], BF16))
                    for i in range(2)]
            sst = [S.enter_context(nc.sbuf_tensor(f"p3s{i}", [128, 2048], BF16))
                   for i in range(2)]
            psS = [S.enter_context(nc.psum_tensor(f"p3p{i}", [128, 512], F32))
                   for i in range(8)]
            l_wo = [pb.dma("pl",
                           wo[:, 8 * 2048 * j:8 * 2048 * (j + 1)].rearrange(
                               "p (kt f) -> p kt f", kt=8),
                           woutT[1024 * j:1024 * (j + 1), :].rearrange(
                               "(kt p) f -> p kt f", p=128), deps=f1_done)
                    for j in range(4)]
            ldy = {}
            mms = {}
            evs = {}
            sts = {}
            for tt in range(NTT):
                ldy[tt] = pb.dma("sp", ynb_[tt % 2][:], ynT_hbm[tt, :, :],
                                 deps=[mms.get(tt - 2)] + (f1_done if tt < 2 else []))

                def mall(t, tt=tt):
                    r = None
                    for kt in range(32):
                        for q in range(4):
                            pf = psS[(tt % 2) * 4 + q]
                            r = t.matmul(pf[:, :], ynb_[tt % 2][:, 128 * kt:128 * (kt + 1)],
                                     wo[:, 2048 * kt + 512 * q: 2048 * kt + 512 * (q + 1)],
                                     start=(kt == 0), stop=(kt == 31))
                    return r
                mms[tt] = pb.comp("pe", mall, deps=l_wo + [ldy[tt], evs.get(tt - 2)])

                def evq(s, tt=tt):
                    r = None
                    for q in range(4):
                        pf = psS[(tt % 2) * 4 + q]
                        r = s.copy(sst[tt % 2][:, 512 * q:512 * (q + 1)], pf[:, :])
                    return r
                evs[tt] = pb.comp("ac", evq, deps=[mms[tt], sts.get(tt - 2)])
                sts[tt] = pb.dma("sp", s_out[128 * tt:128 * (tt + 1), :],
                                 sst[tt % 2][:], deps=[evs[tt]])
            p3b_done = pb.barrier()

        if BUILD_PHASES < 8:
            pb.finalize(block)
            return dram

        with contextlib.ExitStack() as S:
            t2b = [S.enter_context(nc.sbuf_tensor("f2t0", [128, 8 * 2048], BF16))]
            hti = [S.enter_context(nc.sbuf_tensor(f"f2h{i}", [128, 8 * 512], BF16))
                   for i in range(2)]
            y2a = S.enter_context(nc.sbuf_tensor("f2y", [128, 16 * 2048], F32))
            y2s = [S.enter_context(nc.sbuf_tensor(f"f2s{i}", [128, 2048], F32))
                   for i in range(2)]
            lmx = S.enter_context(nc.sbuf_tensor("f2m", [128, NTT + 2], F32))
            lrow = S.enter_context(nc.sbuf_tensor("f2r", [1, 128], F32))
            psY = [S.enter_context(nc.psum_tensor(f"f2p{i}", [128, 512], F32))
                   for i in range(8)]
            NBLK = 8
            NKB = HF // 128 // NBLK
            lb = None
            lh = {}
            mmy = {}
            evy = {}
            for blk in range(NBLK):
                lb = pb.dma("sp", t2b[0][:].rearrange(
                    "p (kt f) -> p kt f", kt=NKB),
                    t2_hbm[1024 * blk:1024 * (blk + 1), :].rearrange(
                        "(kt p) f -> p kt f", p=128),
                    deps=[mmy.get((blk - 1, NTT - 1))] + (p3b_done if blk < 1 else []))
                for tt in range(NTT):
                    ts = tt // 4
                    tq = tt % 4
                    if tq == 0:
                        lh[(blk, ts)] = pb.dma("pl" if ts % 2 else "sp",
                            hti[ts % 2][:].rearrange("p (k q) -> p k q", k=NKB),
                            h_hbm[1024 * blk:1024 * (blk + 1),
                                  512 * ts:512 * (ts + 1)].rearrange(
                                "(k p) q -> p k q", p=128),
                            deps=[mmy.get((blk, 4 * ts - 5)),
                                  mmy.get((blk - 1, 4 * ts + 11))]
                            + (p3b_done if blk == 0 and ts < 2 else []))

                    def mall(t, blk=blk, tt=tt, ts=ts, tq=tq):
                        r = None
                        for kt in range(NKB):
                            for q in range(4):
                                pf = psY[(tt % 2) * 4 + q]
                                r = t.matmul(pf[:, :],
                                         hti[ts % 2][:, 512 * kt + 128 * tq:
                                                      512 * kt + 128 * (tq + 1)],
                                         t2b[0][:, 2048 * kt + 512 * q:
                                                2048 * kt + 512 * (q + 1)],
                                         start=(kt == 0), stop=(kt == NKB - 1))
                        return r
                    mmy[(blk, tt)] = pb.comp("pe", mall, deps=[
                        lb, lh[(blk, ts)],
                        evy.get((blk, tt - 2)), evy.get((blk - 1, NTT - 2 + tt))])

                    def evq(v, blk=blk, tt=tt):
                        r = None
                        for q in range(4):
                            pf = psY[(tt % 2) * 4 + q]
                            dst = y2a[:, 2048 * tt + 512 * q: 2048 * tt + 512 * (q + 1)]
                            if blk == 0:
                                r = v.tensor_copy(dst, pf[:, :])
                            else:
                                r = v.tensor_add(dst, dst, pf[:, :])
                        return r
                    evy[(blk, tt)] = pb.comp("dv", evq, deps=[mmy[(blk, tt)],
                                                              evy.get((blk - 1, tt))])
            fin = {}
            for tt in range(NTT):
                f1_ = pb.comp("dv", (lambda v, tt=tt: v.scalar_tensor_tensor(
                    y2s[tt % 2][:], y2a[:, 2048 * tt:2048 * (tt + 1)], mcol[:, 3:4],
                    b2rep[:, :], ALU.mult, ALU.add)),
                    deps=[evy[(NBLK - 1, tt)], fin.get(tt - 2)])
                f2_ = pb.comp("dv", (lambda v, tt=tt: v.reduce_max(
                    lmx[:, tt:tt + 1], y2s[tt % 2][:], AX.X,
                    apply_absolute_value=True)), deps=[f1_])
                fin[tt] = pb.dma("sp", y2f_out[128 * tt:128 * (tt + 1), :],
                                 y2s[tt % 2][:], deps=[f2_])
            mred = pb.comp("dv", lambda v: v.reduce_max(
                lmx[:, NTT:NTT + 1], lmx[:, 0:NTT], AX.X), deps=[fin[NTT - 1],
                                                                 fin[NTT - 2]])
            d_lm0 = pb.dma("pl", scrD, lmx[:, NTT:NTT + 1], deps=[mred])
            d_lm = pb.dma("pl", lrow[0:1, 0:128],
                          scrD.rearrange("p o -> o p"), deps=[d_lm0])
            gmx = pb.comp("dv", lambda v: v.reduce_max(lmx[0:1, NTT + 1:NTT + 2],
                                                       lrow[0:1, 0:128], AX.X),
                          deps=[d_lm])
            st_lm = pb.dma("sp", lmax_out, lmx[0:1, NTT + 1:NTT + 2], deps=[gmx])
            pb.barrier()

        pb.finalize(block)
    return dram


def build_final(nc: bass.Bass):
    dram = {}
    din = lambda n, s, d=F32: _dram(nc, dram, n, s, d, "ExternalInput")
    dout = lambda n, s, d=F32: _dram(nc, dram, n, s, d, "ExternalOutput")
    x_in = din("x", (T, DM))
    y2f = din("y2f", (T, DM))
    s_raw = din("s_raw", (T, DM), BF16)
    rstd = din("rstd_all", (128, NTT))
    maxv = din("maxv", (1, 8))
    out = dout("out", (T, DM))

    ctx = contextlib.ExitStack()
    E = ctx.enter_context
    with ctx:
        pb = PB(nc, E)
        yt = [E(nc.sbuf_tensor(f"by{i}", [128, DM], F32)) for i in range(2)]
        st_ = [E(nc.sbuf_tensor(f"bs{i}", [128, DM], BF16)) for i in range(2)]
        xt = [E(nc.sbuf_tensor(f"bx{i}", [128, DM], F32)) for i in range(2)]
        ot = [E(nc.sbuf_tensor(f"bo{i}", [128, DM], F32)) for i in range(2)]
        gt = E(nc.sbuf_tensor("bg", [128, DM], BF16))
        rst = E(nc.sbuf_tensor("br", [128, NTT], F32))
        mrow = E(nc.sbuf_tensor("bm", [1, 8], F32))
        scol = E(nc.sbuf_tensor("bc", [128, 4], F32))
        pq = E(nc.psum_tensor("bp", [128, 2], F32))
        block = E(nc.Block())
        ones = E(nc.sbuf_tensor("bones", [1, 128], F32))

        oc = pb.comp("dv", lambda v: v.memset(ones[:, :], 1.0))
        l_m = pb.dma("sp", mrow[:, :], maxv)
        l_r = pb.dma("sp", rst[:, :], rstd)
        g1 = pb.comp("dv", lambda v: v.reduce_max(scol[0:1, 0:1], mrow[:, :], AX.X),
                     deps=[l_m])
        g2 = pb.comp("dv", lambda v: v.tensor_scalar(
            scol[0:1, 1:2], scol[0:1, 0:1], 1.0 / QMAX, QEPS, ALU.mult, ALU.add),
            deps=[g1])
        g3 = pb.comp("dv", lambda v: v.reciprocal(scol[0:1, 2:3], scol[0:1, 1:2]),
                     deps=[g2])
        m1 = pb.comp("pe", lambda t: t.matmul(pq[:, 0:1], ones[0:1, 0:128],
                                              scol[0:1, 1:2], start=True, stop=True),
                     deps=[g2, oc])
        m2 = pb.comp("pe", lambda t: t.matmul(pq[:, 1:2], ones[0:1, 0:128],
                                              scol[0:1, 2:3], start=True, stop=True),
                     deps=[g3, m1])
        e1 = pb.comp("dv", lambda v: v.tensor_copy(scol[:, 0:1], pq[:, 0:1]),
                     deps=[m1])
        e2 = pb.comp("dv", lambda v: v.tensor_copy(scol[:, 1:2], pq[:, 1:2]),
                     deps=[m2, e1])
        done = {}
        for tt in range(NTT):
            b = tt % 2
            r0 = 128 * tt
            l1 = pb.dma("sp", yt[b][:], y2f[r0:r0 + 128, :], deps=[done.get(tt - 2)])
            l2 = pb.dma("sp", st_[b][:], s_raw[r0:r0 + 128, :], deps=[done.get(tt - 2)])
            l3 = pb.dma("pl", xt[b][:], x_in[r0:r0 + 128, :], deps=[done.get(tt - 2)])
            q1 = pb.comp("dv", (lambda v, b=b: v.tensor_scalar(
                yt[b][:], yt[b][:], scol[:, 1:2], MROUND, ALU.mult, ALU.add)),
                deps=[l1, e2])
            q2 = pb.comp("dv", (lambda v, b=b: v.tensor_scalar(
                yt[b][:], yt[b][:], -MROUND, -QMAX, ALU.add, ALU.max)), deps=[q1])
            q3 = pb.comp("dv", (lambda v, b=b: v.tensor_scalar_min(
                yt[b][:], yt[b][:], QMAX)), deps=[q2])
            sg = pb.comp("ac", (lambda s, b=b: s.activation(
                gt[:], yt[b][:], AF.Silu, scale=scol[:, 0:1])), deps=[q3, e1,
                                                                     done.get(tt - 1)])
            s2 = pb.comp("dv", (lambda v, b=b, tt=tt: v.tensor_scalar(
                st_[b][:], st_[b][:], rst[:, tt:tt + 1], None, ALU.mult)),
                deps=[l2, l_r])
            o1 = pb.comp("dv", (lambda v, b=b: v.tensor_mul(
                ot[b][:], st_[b][:], gt[:])), deps=[sg, s2])
            o2 = pb.comp("dv", (lambda v, b=b: v.tensor_add(
                ot[b][:], ot[b][:], xt[b][:])), deps=[o1, l3])
            done[tt] = pb.dma("sp", out[r0:r0 + 128, :], ot[b][:], deps=[o2])
        pb.barrier()
        pb.finalize(block)
    return dram


def _perm_ph():
    j = np.arange(DI)
    return (j % NH) * HD + j // NH


def _consts():
    import ml_dtypes
    bf = ml_dtypes.bfloat16
    ident = np.eye(128, dtype=np.float32).astype(bf)
    hh, jj = np.meshgrid(np.arange(NH), np.arange(NH * CH), indexing="ij")
    onehot_hl = (jj // CH == hh).astype(np.float32)
    ll = np.meshgrid(np.arange(CH), np.arange(NH * CH), indexing="ij")
    onehot_l = (ll[1] % CH == ll[0]).astype(np.float32).astype(bf)
    lp, sd = np.meshgrid(np.arange(CH), np.arange(128), indexing="ij")
    maskB = np.where(lp < (sd % 64), MASKNEG, 0.0).astype(np.float32).astype(bf)
    s_, l_ = np.meshgrid(np.arange(CH), np.arange(CH), indexing="ij")
    tril64 = (l_ >= s_).astype(np.float32).astype(bf)
    k_, t_ = np.meshgrid(np.arange(CH), np.arange(CH), indexing="ij")
    triu_inc = (k_ <= t_).astype(np.float32)
    k2, c2 = np.meshgrid(np.arange(NCK), np.arange(NCK), indexing="ij")
    triu_exc = (k2 < c2).astype(np.float32)
    ones_r = np.ones((1, 128), dtype=np.float32)
    return dict(ident=ident, identf=np.eye(128, dtype=np.float32),
                onehot_hl=onehot_hl, onehot_l=onehot_l, maskB=maskB,
                tril64=tril64, triu_inc=triu_inc, triu_exc32=triu_exc, ones_r=ones_r)


def make_inputs_a(core, inputs):
    perm = _perm_ph()
    W_in = inputs["L0_attn_W_in"]
    WinT = np.ascontiguousarray(np.asarray(W_in).T)
    xs_cols = WinT[:, 4096 + perm]
    bc_cols = WinT[:, 8192:8640]
    z_cols = WinT[:, perm]
    winT = np.ascontiguousarray(np.concatenate([xs_cols, bc_cols, z_cols], axis=1))
    WoutT = np.ascontiguousarray(np.asarray(inputs["L0_attn_W_out"]).T)
    woutT = np.ascontiguousarray(WoutT[perm, :])
    w1T = np.ascontiguousarray(np.asarray(inputs["L0_ffn_W1"]).T)
    w2T = np.ascontiguousarray(np.asarray(inputs["L0_ffn_W2"]).T)
    b1 = np.asarray(inputs["L0_ffn_b1"])
    b1c = np.ascontiguousarray(b1.reshape(HF // 128, 128).T)
    b2row = np.asarray(inputs["L0_ffn_b2"]).reshape(1, DM)
    wrow = np.asarray(inputs["L0_attn_out_norm_w"])[perm].reshape(1, DI)
    drow = np.tile(np.asarray(inputs["L0_attn_D"]), HD).reshape(1, DI)
    dtb = np.tile(np.asarray(inputs["L0_attn_dt_bias"]), NCK).reshape(1, NCK * NH)
    alog = np.tile(np.asarray(inputs["L0_attn_A_log"]), NCK).reshape(1, NCK * NH)
    bw = np.asarray(inputs["L0_attn_B_norm_w"])
    cw = np.asarray(inputs["L0_attn_C_norm_w"])
    bwe = np.tile(bw[0::2], NCK).reshape(1, NCK * HD)
    bwo = np.tile(bw[1::2], NCK).reshape(1, NCK * HD)
    cwe = np.tile(cw[0::2], NCK).reshape(1, NCK * HD)
    cwo = np.tile(cw[1::2], NCK).reshape(1, NCK * HD)
    im = dict(x=np.ascontiguousarray(np.asarray(inputs["L0_attn_u"])[core]),
              winT=winT, woutT=woutT, w1T=w1T, w2T=w2T, b1c=b1c, b2row=b2row,
              wrow=wrow, drow=drow, dtb_row=dtb, alog_row=alog,
              bwe_row=bwe, bwo_row=bwo, cwe_row=cwe, cwo_row=cwo)
    im.update(_consts())
    return {k: np.ascontiguousarray(v) for k, v in im.items()}


def kernel(**inputs):
    ncA = bass.Bass("TRN2", target_bir_lowering=False, debug=False)
    build_main(ncA)
    in_maps = [make_inputs_a(c, inputs) for c in range(8)]
    resA = run_bass_kernel_spmd(ncA, in_maps, core_ids=list(range(8)))

    maxv = np.concatenate([np.asarray(resA.results[c]["lmax"]).reshape(1, 1)
                           for c in range(8)], axis=1)
    ncB = bass.Bass("TRN2", target_bir_lowering=False, debug=False)
    build_final(ncB)
    in_maps_b = []
    for c in range(8):
        in_maps_b.append(dict(
            x=np.ascontiguousarray(np.asarray(inputs["L0_attn_u"])[c]),
            y2f=np.asarray(resA.results[c]["y2f"]),
            s_raw=np.asarray(resA.results[c]["s_raw"]),
            rstd_all=np.asarray(resA.results[c]["rstd_all"]),
            maxv=maxv))
    resB = run_bass_kernel_spmd(ncB, in_maps_b, core_ids=list(range(8)))
    out = np.stack([np.asarray(resB.results[c]["out"]) for c in range(8)], axis=0)
    return out.astype(np.float32)


# version 12
# speedup vs baseline: 1.0101x; 1.0015x over previous
import contextlib
import itertools
import numpy as np
import concourse.bass as bass
import concourse.mybir as mybir
from concourse.bass_utils import run_bass_kernel_spmd

F32 = mybir.dt.float32
BF16 = mybir.dt.bfloat16
AF = mybir.ActivationFunctionType
ALU = mybir.AluOpType
AX = mybir.AxisListType

T = 2048
DM = 2048
DI = 4096
NH = 64
HD = 64
NS = 128
CH = 64
NCK = T // CH
HF = 8192
NTT = T // 128
DIP_SSD = 4544
DIP = DIP_SSD + DI

MROUND = 12582912.0
MASKNEG = -28672.0
LN_EPS = 1e-5
RMS_EPS = 1e-5
QMAX = 7.0
QEPS = 1e-8
THRESH = 0.05

DEBUG_TAPS = ()
BUILD_PHASES = 99

ENGN = ("pe", "dv", "ac", "pl", "sp")
_DECN = {"pe": "tensor", "dv": "vector", "ac": "scalar", "pl": "gpsimd", "sp": "sync"}


class _Step:
    __slots__ = ("eng", "emit", "waits", "sem", "val", "isdma", "sedv")


class PB:
    NQSEM = 16

    def __init__(self, nc, E):
        self.nc = nc
        self.eng = {"pe": nc.tensor, "dv": nc.vector, "ac": nc.scalar,
                    "pl": nc.gpsimd, "sp": nc.sync}
        self.sem = {e: E(nc.semaphore(f"s_{e}")) for e in ENGN}
        self.qsem = {e: [E(nc.semaphore(f"q_{e}{i}")) for i in range(self.NQSEM)]
                     for e in ("sp", "pl", "ac")}
        self.cnt = {e: 0 for e in ENGN}
        self.qcnt = {e: 0 for e in ("sp", "pl", "ac")}
        self.steps = {e: [] for e in ENGN}
        self.last = {e: None for e in ENGN}

    def _mkwaits(self, eng, deps):
        waits = {}
        for d in deps:
            if d is None:
                continue
            if (not d.isdma) and d.eng == eng:
                continue
            key = id(d.sem)
            if key not in waits or waits[key][1] < d.val:
                waits[key] = (d.sem, d.val)
        return list(waits.values())

    def comp(self, eng, emit, deps=()):
        st = _Step()
        st.eng = eng
        st.emit = emit
        st.isdma = False
        st.waits = self._mkwaits(eng, deps)
        st.sedv = max((d.val for d in deps
                       if d is not None and (not d.isdma) and d.eng == eng),
                      default=0)
        self.cnt[eng] += 1
        st.sem = self.sem[eng]
        st.val = self.cnt[eng]
        self.steps[eng].append(st)
        self.last[eng] = st
        return st

    def dma(self, eng, out, in_, deps=(), **kw):
        assert eng in ("sp", "pl", "ac")
        st = _Step()
        st.eng = eng
        st.isdma = True
        j = self.qcnt[eng]
        self.qcnt[eng] = j + 1
        qs = self.qsem[eng][j % self.NQSEM]
        uses = j // self.NQSEM
        st.sem = qs
        st.val = 16 * (uses + 1)

        def emit(e, out=out, in_=in_, kw=kw, qs=qs, uses=uses):
            if uses:
                e.wait_ge(qs, 16 * uses)
            e.dma_start(out=out, in_=in_, **kw).then_inc(qs, 16)
        st.emit = emit
        st.waits = self._mkwaits(eng, deps)
        self.steps[eng].append(st)
        self.last[eng] = st
        return st

    def _qsem_state(self):
        ws = []
        for qe in ("sp", "pl", "ac"):
            n = self.qcnt[qe]
            for i in range(self.NQSEM):
                uses = (n - i + self.NQSEM - 1) // self.NQSEM if n > i else 0
                if uses > 0:
                    ws.append((self.qsem[qe][i], 16 * uses))
        return ws

    def barrier(self):
        lasts = [self.last[e] for e in ENGN if self.last[e] is not None]
        qws = self._qsem_state()
        steps = [self.comp(e, lambda _: None, deps=lasts) for e in ENGN]
        for st in steps:
            seen = {id(s) for (s, _) in st.waits}
            for (s, v) in qws:
                if id(s) not in seen:
                    st.waits.append((s, v))
        return steps

    def finalize(self, block):
        finals = []
        for qe in ("sp", "pl", "ac"):
            n = self.qcnt[qe]
            for i in range(self.NQSEM):
                uses = (n - i + self.NQSEM - 1) // self.NQSEM if n > i else 0
                if uses > 0:
                    finals.append((self.qsem[qe][i], 16 * uses))
        for ename in ENGN:
            steps = self.steps[ename]
            esem = self.sem[ename]

            def mk(steps=steps, esem=esem, ename=ename):
                def f(e):
                    wm = {}
                    cover = 0
                    lval = 0
                    for st in steps:
                        for (s, v) in st.waits:
                            k = id(s)
                            if wm.get(k, -1) >= v:
                                continue
                            e.wait_ge(s, v)
                            wm[k] = v
                        if (not st.isdma) and ename != "pe" \
                                and st.sedv > cover:
                            e.drain()
                            cover = lval
                        ret = st.emit(e)
                        if not st.isdma:
                            if ret is not None:
                                ret.then_inc(esem, 1)
                            else:
                                e.drain().then_inc(esem, 1)
                                cover = st.val
                            lval = st.val
                    if ename == "sp":
                        for (s, v) in finals:
                            if wm.get(id(s), -1) < v:
                                e.wait_ge(s, v)
                return f
            getattr(block, _DECN[ename])(mk())


def _dram(nc, dram, name, shape, dtype, kind):
    dram[name] = nc.dram_tensor(name, list(shape), dtype, kind=kind).ap()
    return dram[name]


def build_main(nc: bass.Bass):
    dram = {}
    din = lambda n, s, d=F32: _dram(nc, dram, n, s, d, "ExternalInput")
    dout = lambda n, s, d=F32: _dram(nc, dram, n, s, d, "ExternalOutput")
    dint = lambda n, s, d=F32: _dram(nc, dram, n, s, d, "Internal")

    x_in = din("x", (T, DM))
    winT = din("winT", (DM, DIP))
    woutT = din("woutT", (DI, DM))
    w1T = din("w1T", (DM, HF))
    w2T = din("w2T", (HF, DM))
    b1c_in = din("b1c", (128, HF // 128))
    b2row = din("b2row", (1, DM))
    wrow = din("wrow", (1, DI))
    drow_in = din("drow", (1, DI))
    dtb_row = din("dtb_row", (1, NCK * NH))
    alog_row = din("alog_row", (1, NCK * NH))
    bwe_row = din("bwe_row", (1, NCK * HD))
    bwo_row = din("bwo_row", (1, NCK * HD))
    cwe_row = din("cwe_row", (1, NCK * HD))
    cwo_row = din("cwo_row", (1, NCK * HD))
    ident_in = din("ident", (128, 128), BF16)
    identf_in = din("identf", (128, 128), F32)
    oh_hl_in = din("onehot_hl", (NH, NH * CH))
    oh_l_in = din("onehot_l", (CH, NH * CH), BF16)
    maskB_in = din("maskB", (CH, 128), BF16)
    tril_in = din("tril64", (CH, CH), BF16)
    triui_in = din("triu_inc", (CH, CH))
    triux_in = din("triu_exc32", (NCK, NCK))
    ones_in = din("ones_r", (1, 128))

    y2f_out = dout("y2f", (T, DM))
    lmax_out = dout("lmax", (1, 1))
    rstd_out = dout("rstd_all", (128, NTT))
    s_out = dout("s_raw", (T, DM), BF16)

    xs_hbm = dint("xs_hbm", (T, DI), BF16)
    xnT_hbm = dint("xnT_hbm", (128, 16 * T), BF16)
    bc_hbm = dint("bc_hbm", (T, 448), F32)
    scrA = dint("scrA", (1, 2048), F32)
    scrB = dint("scrB", (NCK, HD), F32)
    scrC = dint("scrC", (CH, CH), F32)
    scrD = dint("scrD", (128, 1), F32)
    scrE = dint("scrE", (1, 8), F32)
    z_hbm = dint("z_hbm", (T, DI), BF16)
    yd_hbm = dint("yd_hbm", (T, DI), BF16)
    yo_hbm = dint("yo_hbm", (T, DI), BF16)
    t1_hbm = dint("t1_hbm", (DM, HF), BF16)
    t2_hbm = dint("t2_hbm", (HF, DM), BF16)
    h_hbm = dint("h_hbm", (HF, T), BF16)
    ynT_hbm = dint("ynT_hbm", (NTT, 128, DI), BF16)

    ctx = contextlib.ExitStack()
    E = ctx.enter_context
    with ctx:
        pb = PB(nc, E)
        taps = []

        def tap(name, ap_, deps):
            if name in DEBUG_TAPS:
                o = dout("tap_" + name, list(ap_.shape), ap_.dtype)
                taps.append(pb.dma("sp", o, ap_, deps=deps))

        identsb = E(nc.sbuf_tensor("identsb", [128, 128], BF16))
        consts = E(nc.sbuf_tensor("consts", [128, 512], F32))
        trilb = E(nc.sbuf_tensor("trilb", [CH, CH], BF16))
        maskBb = E(nc.sbuf_tensor("maskBb", [CH, 128], BF16))
        mcol = E(nc.sbuf_tensor("mcol", [128, 64], F32))
        b2rep = E(nc.sbuf_tensor("b2rep", [128, DM], F32))
        wscr_sc = E(contextlib.ExitStack())
        WE = wscr_sc.enter_context
        wmt = {m: [WE(nc.sbuf_tensor(f"{m}t{i}", [128, 2048], F32))
                   for i in range(2)] for m in ("w1", "w2")}
        wtb = {m: [WE(nc.sbuf_tensor(f"{m}b{i}", [128, 2048], BF16))
                   for i in range(1)] for m in ("w1", "w2")}
        wqs = WE(nc.sbuf_tensor("wqs", [128, 2048], F32))
        wtr = WE(nc.sbuf_tensor("wtr", [128, 2048], F32))
        wprt = WE(nc.sbuf_tensor("wprt", [128, 520], F32))
        wrr = WE(nc.sbuf_tensor("wrr", [1, 512], F32))
        wones = WE(nc.sbuf_tensor("wones", [128, 128], F32))
        xnt_sc = E(contextlib.ExitStack())
        xnT = xnt_sc.enter_context(nc.sbuf_tensor("xnT", [128, 16 * T], BF16))
        block = E(nc.Block())

        c_id = pb.dma("sp", identsb[:], ident_in)
        c_tril = pb.dma("sp", trilb[:], tril_in)
        c_mb = pb.dma("sp", maskBb[:], maskB_in)
        c_tri = pb.dma("sp", consts[0:CH, 64:128], triui_in)
        c_trx = pb.dma("sp", consts[0:NCK, 128:160], triux_in)
        c_on = pb.dma("sp", consts[0:1, 160:288], ones_in)
        c_on32 = pb.dma("sp", consts[32:33, 160:288], ones_in)
        c_on64 = pb.dma("sp", consts[64:65, 160:288], ones_in)
        s_non = pb.comp("dv", lambda v: v.tensor_scalar(
            consts[0:1, 288:416], consts[0:1, 160:288], -1.0, None, ALU.mult),
            deps=[c_on])
        c_eps = pb.comp("dv", lambda v: v.memset(consts[:, 416:417], LN_EPS))
        EPSC = consts[:, 416:417]
        ONES = consts[0:1, 160:288]
        ONES32 = consts[32:33, 160:288]
        ONES64 = consts[64:65, 160:288]
        NEGONES = consts[0:1, 288:416]
        TRIUI = consts[0:CH, 64:128]
        TRIUX = consts[0:NCK, 128:160]
        c_onall = [c_on, c_on32, c_on64]

        WQ = "ac"
        WCFG = {
            "w1": dict(wT=w1T, thbm=t1_hbm, icol=0, acol=2,
                       tc=0, mc0=8, rc1=136, rc2=200, cc0=264, s0=0, e0=0,
                       a0=16, aj=0),
            "w2": dict(wT=w2T, thbm=t2_hbm, icol=1, acol=3,
                       tc=1, mc0=72, rc1=328, rc2=392, cc0=456, s0=8, e0=2,
                       a0=24, aj=1),
        }
        wmx = {"w1": {}, "w2": {}}
        wcons = {"w1": {}, "w2": {}}
        wst = {"w1": {}, "w2": {}}
        wdelta = {}

        WQSEL = ["ac"]

        def wmax_gen(m):
            c = WCFG[m]
            wT = c["wT"]
            nn = wT.shape[1] // 2048
            for i in range(64):
                kt, nt = divmod(i, nn)
                b = i % 2
                ld = pb.dma(WQSEL[0], wmt[m][b][:],
                            wT[128 * kt:128 * (kt + 1), 2048 * nt:2048 * (nt + 1)],
                            deps=[wmx[m].get(i - 2)])
                wmx[m][i] = pb.comp("dv", (lambda v, b=b, co=c["mc0"] + i, m=m:
                    v.reduce_max(wprt[:, co:co + 1], wmt[m][b][:], AX.X,
                                 apply_absolute_value=True)), deps=[ld])
                yield

        def wdfin_gen():
            prev_rd = None
            prev_gm = None
            for m in ("w1", "w2"):
                c = WCFG[m]
                s0 = c["s0"]
                red = pb.comp("dv", (lambda v, c=c: v.reduce_max(
                    wprt[:, c["tc"]:c["tc"] + 1], wprt[:, c["mc0"]:c["mc0"] + 64],
                    AX.X)), deps=[wmx[m][63]])
                d0 = pb.dma(WQ, scrD, wprt[:, c["tc"]:c["tc"] + 1],
                            deps=[red, prev_rd])
                d1 = pb.dma(WQ, wrr[0:1, 128:256], scrD.rearrange("p o -> o p"),
                            deps=[d0, prev_gm])
                prev_rd = d1
                gm = pb.comp("dv", (lambda v, s0=s0: v.reduce_max(
                    wrr[0:1, s0:s0 + 1], wrr[0:1, 128:256], AX.X)), deps=[d1])
                prev_gm = gm
                dl = pb.comp("dv", (lambda v, s0=s0: v.tensor_scalar(
                    wrr[0:1, s0 + 1:s0 + 2], wrr[0:1, s0:s0 + 1], THRESH, None,
                    ALU.mult)), deps=[gm])
                v2 = pb.comp("dv", (lambda v, s0=s0: v.tensor_scalar(
                    wrr[0:1, s0 + 2:s0 + 3], wrr[0:1, s0 + 1:s0 + 2], 2.0, None,
                    ALU.mult)), deps=[dl])
                r2_ = pb.comp("dv", (lambda v, s0=s0: v.reciprocal(
                    wrr[0:1, s0 + 2:s0 + 3], wrr[0:1, s0 + 2:s0 + 3])), deps=[v2])
                ngd = pb.comp("dv", (lambda v, s0=s0: v.tensor_scalar(
                    wrr[0:1, s0 + 3:s0 + 4], wrr[0:1, s0 + 1:s0 + 2], -1.0, None,
                    ALU.mult)), deps=[dl])
                d2 = pb.dma(WQ, scrE[0:1, c["e0"]:c["e0"] + 2],
                            wrr[0:1, s0 + 2:s0 + 4], deps=[r2_, ngd])
                b_iv = pb.dma(WQ, mcol[:, c["icol"]:c["icol"] + 1],
                              scrE[0:1, c["e0"]:c["e0"] + 1].broadcast_to([128, 1]),
                              deps=[d2])
                b_ng = pb.dma(WQ, mcol[:, c["icol"] + 4:c["icol"] + 5],
                              scrE[0:1, c["e0"] + 1:c["e0"] + 2].broadcast_to([128, 1]),
                              deps=[d2])
                wdelta[m] = (b_iv, b_ng)
                yield

        def wbuild_gen(m):
            c = WCFG[m]
            wT = c["wT"]
            thbm = c["thbm"]
            nn = wT.shape[1] // 2048
            jmap = {0: 62, 1: 63}
            for i in range(64):
                kt, nt = divmod(i, nn)
                b = i % 2
                ld = pb.dma(WQSEL[0], wmt[m][b][:],
                            wT[128 * kt:128 * (kt + 1), 2048 * nt:2048 * (nt + 1)],
                            deps=(wcons[m][i - 2] if i >= 2
                                  else [wmx[m][jmap[b]]]))
                u1 = pb.comp("dv", (lambda v, b=b, m=m, ic=c["icol"]:
                    v.tensor_scalar(wqs[:], wmt[m][b][:], mcol[:, ic:ic + 1],
                                    MROUND, ALU.mult, ALU.add)),
                    deps=[ld, wdelta[m][0]])
                u2 = pb.comp("dv", lambda v: v.tensor_scalar(
                    wqs[:], wqs[:], -MROUND, -1.0, ALU.add, ALU.max), deps=[u1])
                tb_ = wtb[m][0]
                u3 = pb.comp("dv", (lambda v, tb_=tb_: v.tensor_scalar_min(
                    tb_[:], wqs[:], 1.0)), deps=[u2, wst[m].get(i - 1)])
                u4 = pb.comp("dv", (lambda v, tb_=tb_, co=c["cc0"] + i:
                    v.reduce_sum(wprt[:, co:co + 1], tb_[:], AX.X,
                                 apply_absolute_value=True)), deps=[u3])
                r1 = pb.comp("ac", (lambda s, b=b, m=m, co=c["rc1"] + i,
                                    ic=c["icol"]: s.activation(
                    wtr[:], wmt[m][b][:], AF.Relu, bias=mcol[:, ic + 4:ic + 5],
                    scale=1.0, accum_out=wprt[:, co:co + 1])),
                    deps=[ld, wdelta[m][1]])
                r2 = pb.comp("ac", (lambda s, b=b, m=m, co=c["rc2"] + i,
                                    ic=c["icol"]: s.activation(
                    wtr[:], wmt[m][b][:], AF.Relu, bias=mcol[:, ic + 4:ic + 5],
                    scale=-1.0, accum_out=wprt[:, co:co + 1])), deps=[r1])
                wcons[m][i] = [u1, r2]
                dst = thbm[128 * kt:128 * (kt + 1), 2048 * nt:2048 * (nt + 1)]
                wst[m][i] = pb.dma("pl", dst, tb_[:], deps=[u3])
                yield

        def _ilv(*gens):
            gens = list(gens)
            while gens:
                for g in gens[:]:
                    try:
                        next(g)
                    except StopIteration:
                        gens.remove(g)
                        continue
                    yield

        wchain = itertools.chain(
            _ilv(wmax_gen("w1"), wmax_gen("w2")), wdfin_gen(),
            _ilv(wbuild_gen("w1"), wbuild_gen("w2")))

        _SENT = object()

        def wpump(n=1, q="ac"):
            WQSEL[0] = q
            for _ in range(n):
                if next(wchain, _SENT) is _SENT:
                    return

        def walpha_emit(psW):
            oc = pb.comp("dv", lambda v: v.memset(wones[:, :], 1.0))
            prev_pe = []
            for m in ("w1", "w2"):
                c = WCFG[m]
                aj, ic = c["aj"], c["icol"]
                lastac = wcons[m][63][1]
                s1 = pb.comp("dv", (lambda v, c=c: v.reduce_sum(
                    wprt[:, 2:3], wprt[:, c["rc1"]:c["rc1"] + 64], AX.X)),
                    deps=[lastac] + prev_pe)
                s2 = pb.comp("dv", (lambda v, c=c: v.reduce_sum(
                    wprt[:, 3:4], wprt[:, c["rc2"]:c["rc2"] + 64], AX.X)),
                    deps=[lastac, s1])
                s3 = pb.comp("dv", (lambda v, c=c: v.reduce_sum(
                    wprt[:, 4:5], wprt[:, c["cc0"]:c["cc0"] + 64], AX.X)),
                    deps=[s2])
                a12 = pb.comp("dv", lambda v: v.tensor_add(
                    wprt[:, 2:3], wprt[:, 2:3], wprt[:, 3:4]), deps=[s1, s2])
                f_r = pb.comp("pe", (lambda t, aj=aj: t.matmul(
                    psW[:, 4 * aj:4 * aj + 1], wones[:, :], wprt[:, 2:3],
                    start=True, stop=True)), deps=[a12, oc])
                f_c = pb.comp("pe", (lambda t, aj=aj: t.matmul(
                    psW[:, 4 * aj + 1:4 * aj + 2], wones[:, :], wprt[:, 4:5],
                    start=True, stop=True)), deps=[s3, oc, f_r])
                prev_pe = [f_r, f_c]
                t_ = pb.comp("dv", (lambda v, aj=aj, ic=ic: v.tensor_scalar(
                    wprt[:, 5:6], psW[:, 4 * aj + 1:4 * aj + 2],
                    mcol[:, ic + 4:ic + 5], None, ALU.mult)),
                    deps=[f_c, wdelta[m][1]])
                ms_ = pb.comp("dv", (lambda v, aj=aj: v.tensor_sub(
                    wprt[:, 5:6], psW[:, 4 * aj:4 * aj + 1], wprt[:, 5:6])),
                    deps=[t_, f_r])
                cn_ = pb.comp("dv", (lambda v, aj=aj: v.tensor_scalar(
                    wprt[:, 6:7], psW[:, 4 * aj + 1:4 * aj + 2],
                    1.0, None, ALU.max)), deps=[f_c])
                rc_ = pb.comp("dv", lambda v: v.reciprocal(
                    wprt[:, 6:7], wprt[:, 6:7]), deps=[cn_])
                al_ = pb.comp("dv", lambda v: v.tensor_mul(
                    wprt[:, 7:8], wprt[:, 5:6], wprt[:, 6:7]), deps=[ms_, rc_])
                cp_ = pb.comp("dv", (lambda v, c=c: v.tensor_copy(
                    mcol[:, c["acol"]:c["acol"] + 1], wprt[:, 7:8])),
                    deps=[al_])

        wpump(2)

        wsl_sc = E(contextlib.ExitStack())
        wsl = [wsl_sc.enter_context(nc.sbuf_tensor(f"p1w{i}", [128, 16 * 512], BF16))
               for i in range(2)]
        fbounds = []
        off = 0
        while off < DIP:
            wdt = min(512, DIP_SSD - off) if off < DIP_SSD else min(512, DIP - off)
            fbounds.append((off, wdt))
            off += wdt
        wld = [None, None]
        for fc in range(2):
            off, wdt = fbounds[fc]
            wld[fc] = pb.dma("pl", wsl[fc][:, 0:16 * wdt].rearrange(
                "p (kt f) -> p kt f", kt=16),
                winT[:, off:off + wdt].rearrange("(kt p) f -> p kt f", p=128))

        with contextlib.ExitStack() as S:
            xt = [S.enter_context(nc.sbuf_tensor(f"p0x{i}", [128, DM], F32)) for i in range(2)]
            xnb = [S.enter_context(nc.sbuf_tensor(f"p0n{i}", [128, DM], BF16)) for i in range(2)]
            cols = S.enter_context(nc.sbuf_tensor("p0c", [128, 16], F32))
            trash = wtr
            pt = [S.enter_context(nc.psum_tensor(f"p0p{i}", [128, 128], BF16)) for i in range(2)]

            done_tr = {}
            xn_done = [None, None]
            for tt in range(NTT):
                b = tt % 2
                co = 8 * (tt % 2)
                ld = pb.dma("sp", xt[b][:], x_in[128 * tt:128 * (tt + 1), :],
                            deps=[xn_done[b]])
                sq = pb.comp("ac", (lambda s, b=b, co=co: s.activation(
                    trash[:], xt[b][:], AF.Square, accum_out=cols[:, co + 1:co + 2])),
                    deps=[ld])
                sm = pb.comp("dv", (lambda v, b=b, co=co: v.reduce_sum(
                    cols[:, co:co + 1], xt[b][:], AX.X)), deps=[ld])
                nm = pb.comp("dv", (lambda v, co=co: v.tensor_scalar(
                    cols[:, co + 2:co + 3], cols[:, co:co + 1], -1.0 / DM, None, ALU.mult)),
                    deps=[sm])
                ss = pb.comp("dv", (lambda v, co=co: v.tensor_scalar(
                    cols[:, co + 1:co + 2], cols[:, co + 1:co + 2], 1.0 / DM, None, ALU.mult)),
                    deps=[sq])
                nv = pb.comp("dv", (lambda v, co=co: v.scalar_tensor_tensor(
                    cols[:, co + 3:co + 4], cols[:, co + 2:co + 3], cols[:, co + 2:co + 3],
                    cols[:, co + 1:co + 2], ALU.mult, ALU.subtract)), deps=[nm, ss])
                ln = pb.comp("ac", (lambda s, co=co: s.activation(
                    cols[:, co + 3:co + 4], cols[:, co + 3:co + 4], AF.Ln,
                    bias=EPSC, scale=-1.0)), deps=[nv, c_eps])
                rs = pb.comp("ac", (lambda s, co=co: s.activation(
                    cols[:, co + 4:co + 5], cols[:, co + 3:co + 4], AF.Exp, scale=-0.5)),
                    deps=[ln])
                xn_done[b] = pb.comp("dv", (lambda v, b=b, co=co: v.tensor_scalar(
                    xnb[b][:], xt[b][:], cols[:, co + 2:co + 3], cols[:, co + 4:co + 5],
                    ALU.add, ALU.mult)),
                    deps=[rs, nm] + ([done_tr[(tt - 2, 15)]] if tt >= 2 else [c_id]))
                for k in range(16):
                    tr = pb.comp("pe", (lambda t, b=b, k=k: t.transpose(
                        pt[k % 2][:, :], xnb[b][:, 128 * k:128 * (k + 1)], identsb[:, :])),
                        deps=[xn_done[b], done_tr.get((tt, k - 2)),
                              done_tr.get((tt - 1, 14 + k))])
                    done_tr[(tt, k)] = pb.comp("ac", (lambda s, k=k, tt=tt:
                        s.copy(xnT[:, 2048 * k + 128 * tt: 2048 * k + 128 * (tt + 1)],
                               pt[k % 2][:, :])), deps=[tr])
                wpump(1, "pl")
                wpump(1, "ac")
            p0_done = pb.barrier()
            tap("xnT", xnT[:, :], p0_done)

        if BUILD_PHASES < 1:
            pb.finalize(block)
            return dram

        with contextlib.ExitStack() as S:
            ost = [S.enter_context(nc.sbuf_tensor(f"p1o{i}", [128, 512], BF16))
                   for i in range(2)]
            obc = [S.enter_context(nc.sbuf_tensor(f"p1b{i}", [128, 448], F32))
                   for i in range(2)]
            psA = [S.enter_context(nc.psum_tensor(f"p1p{i}", [128, 512], F32))
                   for i in range(2)]
            psW = S.enter_context(nc.psum_tensor("p1wps", [128, 8], F32))
            mmd = {}
            evd = {}
            std = {}
            for fc, (off, wdt) in enumerate(fbounds):
                if fc >= 2:
                    wld[fc % 2] = pb.dma("pl", wsl[fc % 2][:, 0:16 * wdt].rearrange(
                        "p (kt f) -> p kt f", kt=16),
                        winT[:, off:off + wdt].rearrange("(kt p) f -> p kt f", p=128),
                        deps=[mmd.get((fc - 2, NTT - 1))])
                for tt in range(NTT):
                    def mm(t, fc=fc, tt=tt, wdt=wdt):
                        ps = psA[tt % 2]
                        r = None
                        for kt in range(16):
                            r = t.matmul(ps[:, 0:wdt],
                                     xnT[:, 2048 * kt + 128 * tt: 2048 * kt + 128 * (tt + 1)],
                                     wsl[fc % 2][:, wdt * kt: wdt * (kt + 1)],
                                     start=(kt == 0), stop=(kt == 15))
                        return r
                    mmd[(fc, tt)] = pb.comp("pe", mm, deps=[
                        wld[fc % 2], evd.get((fc, tt - 2)), evd.get((fc - 1, NTT - 2 + tt))]
                        + (p0_done if fc == 0 and tt == 0 else []))
                    is_bc = (off == 4096)
                    war = (std.get((fc, tt - 2)) or []) + \
                          (std.get((fc - 1, NTT - 2 + tt)) or [])
                    if not is_bc:
                        evd[(fc, tt)] = pb.comp("ac", (lambda s, tt=tt, wdt=wdt:
                            s.copy(ost[tt % 2][:, 0:wdt], psA[tt % 2][:, 0:wdt])),
                            deps=[mmd[(fc, tt)]] + war)
                        if off + wdt <= DI:
                            dst = xs_hbm[128 * tt:128 * (tt + 1), off:off + wdt]
                        else:
                            dst = z_hbm[128 * tt:128 * (tt + 1),
                                        off - DIP_SSD:off - DIP_SSD + wdt]
                        std[(fc, tt)] = [pb.dma("sp", dst, ost[tt % 2][:, 0:wdt],
                                                deps=[evd[(fc, tt)]])]
                    else:
                        ebc = pb.comp("ac", (lambda s, tt=tt:
                            s.copy(obc[tt % 2][:, 0:448], psA[tt % 2][:, 0:448])),
                            deps=[mmd[(fc, tt)]] + war)
                        evd[(fc, tt)] = ebc
                        st1 = pb.dma("sp", bc_hbm[128 * tt:128 * (tt + 1), :],
                                     obc[tt % 2][:, 0:448], deps=[ebc])
                        std[(fc, tt)] = [st1]
                    wpump(1, "sp" if (16 * fc + tt) % 3 == 0 else "ac")
            while next(wchain, _SENT) is not _SENT:
                pass
            walpha_emit(psW)
            p1_done = pb.barrier()
            sp_xnt = pb.dma("sp", xnT_hbm, xnT[:, :], deps=p1_done)
            p1_done = pb.barrier()

        wsl_sc.close()
        xnt_sc.close()
        wscr_sc.close()

        if BUILD_PHASES < 2:
            pb.finalize(block)
            return dram

        ssd_sc = E(contextlib.ExitStack())
        SE = ssd_sc.enter_context
        expacsD = SE(nc.sbuf_tensor("expacsD", [128, NCK * NH], BF16))
        dful = SE(nc.sbuf_tensor("dful", [128, DI], BF16))
        wrep = SE(nc.sbuf_tensor("wrep", [128, DI], BF16))
        ssd2_sc = E(contextlib.ExitStack())
        SE2 = ssd2_sc.enter_context
        acs = SE2(nc.sbuf_tensor("acs", [CH, NCK * NH], F32))
        nacsT = SE2(nc.sbuf_tensor("nacsT", [CH, NCK * NH], F32))
        dsstk = SE2(nc.sbuf_tensor("dsstk", [128, NCK * NH], BF16))
        dtlstk = SE2(nc.sbuf_tensor("dtlstk", [128, NCK * NH], BF16))
        dbc = SE2(nc.sbuf_tensor("dbc", [128, NCK * NH], F32))
        bstack = SE2(nc.sbuf_tensor("bstack", [128, NCK * NS], BF16))
        bnT = SE2(nc.sbuf_tensor("bnT", [128, T + 64], BF16))
        cnT = SE2(nc.sbuf_tensor("cnT", [128, T], BF16))
        oh_l = SE2(nc.sbuf_tensor("oh_l", [CH, NH * CH], BF16))
        identf = SE2(nc.sbuf_tensor("identf_sb", [CH, CH], F32))

        with contextlib.ExitStack() as S:
            tmp = [S.enter_context(nc.sbuf_tensor(f"p2t{i}", [CH, NCK * NH], F32))
                   for i in range(4)]
            sinb = S.enter_context(nc.sbuf_tensor("p2sin", [CH, NCK * HD], F32))
            cosb = S.enter_context(nc.sbuf_tensor("p2cos", [CH, NCK * HD], F32))
            ang = S.enter_context(nc.sbuf_tensor("p2ang", [CH, NCK * HD], F32))
            bcscr = S.enter_context(nc.sbuf_tensor("p2bc", [CH, NCK * NS], F32))
            bn = bstack[0:CH, :]
            cn = S.enter_context(nc.sbuf_tensor("p2cn", [CH, NCK * NS], BF16))
            msb = S.enter_context(nc.sbuf_tensor("p2mb", [CH, NCK], F32))
            msc = S.enter_context(nc.sbuf_tensor("p2mc", [CH, NCK], F32))
            wet = S.enter_context(nc.sbuf_tensor("p2we", [CH, 2048], F32))
            wot = S.enter_context(nc.sbuf_tensor("p2wo", [CH, 2048], F32))
            rows = S.enter_context(nc.sbuf_tensor("p2r", [65, 4096], F32))
            crow = S.enter_context(nc.sbuf_tensor("p2cr", [NCK, 2 * HD], F32))
            shsc = S.enter_context(nc.sbuf_tensor("p2sh", [CH, NCK * NH], BF16))
            ps4 = [S.enter_context(nc.psum_tensor(f"p2p{i}", [CH, 512], F32))
                   for i in range(4)]
            ps4b_ = [S.enter_context(nc.psum_tensor(f"p2q{i}", [CH, 512], F32))
                     for i in range(2)]
            ps4b = [ps4b_[0], ps4b_[1], ps4b_[0], ps4b_[1]]
            ps128 = S.enter_context(nc.psum_tensor("p2d", [128, 512], F32))
            pmisc = S.enter_context(nc.psum_tensor("p2s", [128, 512], F32))
            pst = pmisc[:, 0:128]
            pstb = pmisc[:, 128:192].bitcast(BF16)
            pstc = pmisc[:, 192:256].bitcast(BF16)
            ps32 = pmisc[0:NCK, 256:256 + HD]

            _SLOT = {"A": (0, 0), "B": (0, 2048), "C": (32, 0), "D": (32, 2048),
                     "E": (64, 0), "F": (64, 2048)}

            def RO(s):
                p, c = _SLOT[s]
                return rows[p:p + 1, c:c + 2048]

            def bcload(col0, wdt, dst, deps):
                return pb.dma("sp", dst.rearrange("p (c w) -> p c w", w=wdt),
                              bc_hbm[:, col0:col0 + wdt].rearrange(
                                  "(c p) w -> p c w", p=CH), deps=deps)

            c_ohl = pb.dma("sp", oh_l[:], oh_l_in, deps=p1_done)
            c_idf = pb.dma("sp", identf[:], identf_in[0:CH, 0:CH], deps=p1_done)
            l_dtb = pb.dma("sp", RO("A"), dtb_row)
            l_alog = pb.dma("sp", RO("B"), alog_row)
            l_bwe = pb.dma("sp", RO("E"), bwe_row)
            l_bwo = pb.dma("sp", RO("F"), bwo_row)

            BCH0 = bcscr[0:CH, 0:2048].rearrange("p (c w) -> p c w", w=HD)
            BCH1 = bcscr[0:CH, 2048:4096].rearrange("p (c w) -> p c w", w=HD)
            l_th = bcload(384, HD, bcscr[0:CH, 0:2048], p1_done)
            l_dtr = bcload(256, HD, bcscr[0:CH, 2048:4096], p1_done)

            mm_th = [pb.comp("pe", (lambda t, q=q: t.matmul(
                ps4[q][:, :], TRIUI, BCH0[:, 8 * q:8 * (q + 1), :],
                start=True, stop=True)), deps=[l_th, c_tri]) for q in range(4)]
            ev_ang = [pb.comp("ac", (lambda s, q=q: s.copy(
                ang[:, 512 * q:512 * (q + 1)], ps4[q][:, :])), deps=[mm_th[q]])
                for q in range(4)]
            d_sums0 = pb.dma("pl", scrA, ang[63:64, :], deps=ev_ang)
            d_sums = pb.dma("pl", crow[0:NCK, 0:HD],
                            scrA.rearrange("o (c w) -> (o c) w", w=HD),
                            deps=[d_sums0])
            mm_cs = pb.comp("pe", lambda t: t.matmul(
                ps32[:, :], TRIUX, crow[0:NCK, 0:HD], start=True, stop=True),
                deps=[d_sums, c_trx])
            ev_cs = pb.comp("dv", lambda v: v.tensor_copy(crow[0:NCK, HD:2 * HD], ps32[:, :]),
                            deps=[mm_cs])
            d_flat0 = pb.dma("pl", scrB, crow[0:NCK, HD:2 * HD], deps=[ev_cs])
            d_flat = pb.dma("pl", RO("D"), scrB.rearrange("c w -> () (c w)"),
                            deps=[d_flat0])
            mm_cb = [pb.comp("pe", (lambda t, q=q: t.matmul(
                ps4[q][:, :], ONES32[:, 0:CH], RO("D")[:, 512 * q:512 * (q + 1)],
                start=True, stop=True)), deps=[d_flat, ev_ang[q], c_on32]) for q in range(4)]
            ad_ang = [pb.comp("dv", (lambda v, q=q: v.tensor_add(
                ang[:, 512 * q:512 * (q + 1)], ang[:, 512 * q:512 * (q + 1)],
                ps4[q][:, :])), deps=[mm_cb[q]]) for q in range(4)]

            sc_done = []
            for dst, shift in ((sinb, 0.0), (cosb, np.pi / 2)):
                s1 = pb.comp("dv", (lambda v, sh=shift: v.tensor_scalar(
                    tmp[0][:, 0:2048], ang[:], sh, 1.0 / (2 * np.pi), ALU.add, ALU.mult)),
                    deps=ad_ang + sc_done)
                s2 = pb.comp("dv", lambda v: v.tensor_scalar(
                    tmp[0][:, 0:2048], tmp[0][:, 0:2048], MROUND, -MROUND, ALU.add, ALU.add),
                    deps=[s1])
                s3 = pb.comp("dv", (lambda v, sh=shift: v.tensor_scalar(
                    tmp[1][:, 0:2048], ang[:], sh, None, ALU.add)), deps=[s2])
                s4 = pb.comp("dv", (lambda v, d=dst: v.scalar_tensor_tensor(
                    d[:], tmp[0][:, 0:2048], -2 * np.pi, tmp[1][:, 0:2048],
                    ALU.mult, ALU.add)), deps=[s3])
                s4b = pb.comp("dv", (lambda v, d=dst: v.tensor_scalar(
                    d[:], d[:], -3.1415925, 3.1415925, ALU.max, ALU.min)), deps=[s4])
                s5 = pb.comp("ac", (lambda s, d=dst: s.activation(d[:], d[:], AF.Sin)),
                             deps=[s4b])
                sc_done.append(s5)

            e_al = pb.comp("ac", lambda s: s.activation(RO("B"), RO("B"), AF.Exp),
                           deps=[l_alog])
            mm_dtb = []
            v_v = []
            for q in range(4):
                mm_dtb.append(pb.comp("pe", (lambda t, q=q: t.matmul(
                    ps4b[q][:, :], ONES[:, 0:CH], RO("A")[:, 512 * q:512 * (q + 1)],
                    start=True, stop=True)),
                    deps=[l_dtb, c_on, (v_v[q - 2] if q >= 2 else None)]))
                v_v.append(pb.comp("dv", (lambda v, q=q: v.tensor_add(
                    tmp[0][:, 512 * q:512 * (q + 1)],
                    BCH1[:, 8 * q:8 * (q + 1), :].rearrange("p c w -> p (c w)"),
                    ps4b[q][:, :])), deps=[mm_dtb[q], l_dtr]))
            a_ab = pb.comp("ac", lambda s: s.activation(
                tmp[1][:, 0:2048], tmp[0][:, 0:2048], AF.Abs, scale=-1.0),
                deps=v_v + sc_done)
            a_ex = pb.comp("ac", lambda s: s.activation(
                tmp[1][:, 0:2048], tmp[1][:, 0:2048], AF.Exp, scale=-1.0), deps=[a_ab])
            a_ln = pb.comp("ac", lambda s: s.activation(
                tmp[1][:, 0:2048], tmp[1][:, 0:2048], AF.Ln, bias=1.0), deps=[a_ex])
            v_dt = pb.comp("dv", lambda v: v.scalar_tensor_tensor(
                tmp[0][:, 0:2048], tmp[0][:, 0:2048], 0.0, tmp[1][:, 0:2048],
                ALU.max, ALU.add), deps=[a_ln])
            l_lam = bcload(320, HD, bcscr[0:CH, 0:2048], mm_th)
            a_lam = pb.comp("ac", lambda s: s.activation(
                tmp[2][:, 0:2048], bcscr[0:CH, 0:2048],
                AF.Sigmoid), deps=[l_lam, a_ln])
            mm_na = []
            v_A = []
            for q in range(4):
                mm_na.append(pb.comp("pe", (lambda t, q=q: t.matmul(
                    ps4b[q][:, :], NEGONES[:, 0:CH], RO("B")[:, 512 * q:512 * (q + 1)],
                    start=True, stop=True)),
                    deps=[e_al, s_non, v_v[q], v_v[(q + 2) % 4],
                          (v_A[q - 2] if q >= 2 else None)]))
                v_A.append(pb.comp("dv", (lambda v, q=q: v.tensor_mul(
                    tmp[3][:, 512 * q:512 * (q + 1)], tmp[0][:, 512 * q:512 * (q + 1)],
                    ps4b[q][:, :])), deps=[mm_na[q], v_dt]))
            v_d1 = pb.comp("dv", lambda v: v.tensor_mul(
                tmp[1][:, 0:2048], tmp[0][:, 0:2048], tmp[2][:, 0:2048]),
                deps=[v_dt, a_lam])
            v_d1b = pb.comp("dv", lambda v: v.tensor_copy(dtlstk[0:CH, :], tmp[1][:, 0:2048]),
                            deps=[v_d1])
            v_d2 = pb.comp("dv", lambda v: v.tensor_sub(
                tmp[1][:, 0:2048], tmp[0][:, 0:2048], tmp[1][:, 0:2048]), deps=[v_d1b])
            v_d2b = pb.comp("dv", lambda v: v.tensor_copy(shsc[0:CH, :], tmp[1][:, 0:2048]),
                            deps=[v_d2])
            d_shA = pb.dma("pl", dtlstk[65:128, :], shsc[0:CH - 1, :], deps=[v_d2b])
            d_shB = pb.dma("pl",
                           dtlstk[64:65, :].rearrange("o (c w) -> o c w", w=NH)[:, 1:NCK, :],
                           shsc[CH - 1:CH, :].rearrange("o (c w) -> o c w", w=NH)[:, 0:NCK - 1, :],
                           deps=[v_d2b])
            z_sh = pb.comp("dv", lambda v: v.memset(dtlstk[64:65, 0:NH], 0.0),
                           deps=[v_d2b])

            mm_ac = [pb.comp("pe", (lambda t, q=q: t.matmul(
                ps4[q][:, :], TRIUI, tmp[3][:, 512 * q:512 * (q + 1)],
                start=True, stop=True)), deps=[v_A[q], ad_ang[q]]) for q in range(4)]
            ev_ac = [pb.comp("ac", (lambda s, q=q: s.copy(
                acs[:, 512 * q:512 * (q + 1)], ps4[q][:, :])), deps=[mm_ac[q]])
                for q in range(4)]
            ev_ea = [pb.comp("ac", (lambda s, q=q: s.activation(
                expacsD[0:CH, 512 * q:512 * (q + 1)], ps4[q][:, :], AF.Exp)),
                deps=[mm_ac[q]]) for q in range(4)]
            d_ea = pb.dma("pl", expacsD[64:128, :], expacsD[0:CH, :], deps=ev_ea)
            tr_ac = {}
            ev_na = {}
            for c in range(NCK):
                tr_ac[c] = pb.comp("pe", (lambda t, c=c: t.transpose(
                    pst[0:CH, 0:CH], acs[:, NH * c:NH * (c + 1)], identf[:, :])),
                    deps=[ev_ac[c // 8], ev_na.get(c - 1), c_idf])
                ev_na[c] = pb.comp("dv", (lambda v, c=c: v.tensor_scalar(
                    nacsT[:, NH * c:NH * (c + 1)], pst[0:CH, 0:CH], -1.0, None, ALU.mult)),
                    deps=[tr_ac[c]])
            d_end = pb.dma("pl", RO("A"), acs[63:64, :], deps=ev_ac + [mm_dtb[3]])
            e_dec = pb.comp("ac", lambda s: s.activation(RO("B"), RO("A"), AF.Exp),
                            deps=[d_end, a_lam, mm_na[3]])
            mm_en = [pb.comp("pe", (lambda t, q=q: t.matmul(
                ps4[q][:, :], ONES[:, 0:CH], RO("A")[:, 512 * q:512 * (q + 1)],
                start=True, stop=True)), deps=[d_end, ev_ea[q], ev_ac[q]])
                for q in range(4)]
            v_ds = [pb.comp("dv", (lambda v, q=q: v.tensor_sub(
                tmp[1][:, 512 * q:512 * (q + 1)], ps4[q][:, :],
                acs[:, 512 * q:512 * (q + 1)])), deps=[mm_en[q], v_d2b]) for q in range(4)]
            a_ds = pb.comp("ac", lambda s: s.activation(
                dsstk[0:CH, :], tmp[1][:, 0:2048], AF.Exp), deps=v_ds)
            d_ds2 = pb.dma("pl", dsstk[64:128, :], dsstk[0:CH, :], deps=[a_ds])
            mm_db = {}
            ev_db = {}
            for q in range(4):
                mm_db[q] = pb.comp("pe", (lambda t, q=q: t.matmul(
                    ps128[:, :], ONES[0:1, 0:128], RO("B")[:, 512 * q:512 * (q + 1)],
                    start=True, stop=True)), deps=[e_dec, ev_db.get(q - 1)])
                ev_db[q] = pb.comp("dv", (lambda v, q=q: v.tensor_copy(
                    dbc[:, 512 * q:512 * (q + 1)], ps128[:, :])), deps=[mm_db[q]])

            BCF = bcscr[0:CH, :].rearrange("p (c w) -> p c w", w=NS)

            def rope_build(ms, werow_ap, worow_ap, dst, extra):
                sqd = pb.comp("ac", (lambda s: s.activation(
                    tmp[1][:, 0:2048].rearrange("p (c w) -> p c w", w=HD),
                    BCF[:, :, 0::2], AF.Square)), deps=extra)
                sqd2 = pb.comp("ac", (lambda s: s.activation(
                    tmp[3][:, 0:2048].rearrange("p (c w) -> p c w", w=HD),
                    BCF[:, :, 1::2], AF.Square)), deps=[sqd])
                ad = pb.comp("dv", lambda v: v.tensor_add(
                    tmp[1][:, 0:2048], tmp[1][:, 0:2048], tmp[3][:, 0:2048]), deps=[sqd2])
                rd = pb.comp("dv", (lambda v, m=ms: v.tensor_reduce(
                    m[:, :].rearrange("p c -> p c ()"),
                    tmp[1][:, 0:2048].rearrange("p (c w) -> p c w", w=HD),
                    AX.X, ALU.add)), deps=[ad])
                r1 = pb.comp("ac", (lambda s, m=ms: s.activation(
                    m[:, :], m[:, :], AF.Ln, bias=EPSC[0:CH, :], scale=1.0 / 128)),
                    deps=[rd, c_eps])
                r2 = pb.comp("ac", (lambda s, m=ms: s.activation(
                    m[:, :], m[:, :], AF.Exp, scale=-0.5)), deps=[r1])
                mwA = []
                mwB = []
                ew = []
                ow = []
                for q in range(4):
                    mwA.append(pb.comp("pe", (lambda t, q=q, w=werow_ap: t.matmul(
                        ps4[q][:, :], ONES64[:, 0:CH], w[:, 512 * q:512 * (q + 1)],
                        start=True, stop=True)), deps=[v_ds[q], r2, c_on64] + extra))
                    mwB.append(pb.comp("pe", (lambda t, q=q, w=worow_ap: t.matmul(
                        ps4b[q][:, :], ONES64[:, 0:CH], w[:, 512 * q:512 * (q + 1)],
                        start=True, stop=True)),
                        deps=[v_A[q], v_A[(q + 2) % 4], r2, c_on64,
                              (ow[q - 2] if q >= 2 else None)] + extra))
                    ew.append(pb.comp("dv", (lambda v, q=q: v.tensor_copy(
                        wet[:, 512 * q:512 * (q + 1)], ps4[q][:, :])), deps=[mwA[q]]))
                    ow.append(pb.comp("dv", (lambda v, q=q: v.tensor_copy(
                        wot[:, 512 * q:512 * (q + 1)], ps4b[q][:, :])), deps=[mwB[q]]))
                ms3 = ms[:, :].rearrange("p c -> p c ()").broadcast_to([CH, NCK, HD])
                x1 = pb.comp("dv", (lambda v, m3=ms3: v.tensor_tensor(
                    tmp[1][:, 0:2048].rearrange("p (c w) -> p c w", w=HD),
                    BCF[:, :, 0::2], m3, ALU.mult)), deps=[r2, rd])
                x1w = pb.comp("dv", lambda v: v.tensor_mul(
                    tmp[1][:, 0:2048], tmp[1][:, 0:2048], wet[:, :]), deps=[x1] + ew)
                x2 = pb.comp("dv", (lambda v, m3=ms3: v.tensor_tensor(
                    tmp[3][:, 0:2048].rearrange("p (c w) -> p c w", w=HD),
                    BCF[:, :, 1::2], m3, ALU.mult)), deps=[r2, ad])
                x2w = pb.comp("dv", lambda v: v.tensor_mul(
                    tmp[3][:, 0:2048], tmp[3][:, 0:2048], wot[:, :]), deps=[x2] + ow)
                dst3 = dst[0:CH, :].rearrange("p (c w) -> p c w", w=NS)
                t1 = pb.comp("dv", lambda v: v.tensor_mul(
                    tmp[0][:, 0:2048], cosb[:], tmp[1][:, 0:2048]),
                    deps=[x1w] + sc_done + [v_dt, v_d1])
                tA = pb.comp("dv", lambda v: v.tensor_mul(
                    tmp[2][:, 0:2048], sinb[:], tmp[3][:, 0:2048]),
                    deps=[x2w] + sc_done + [a_lam, v_d1])
                evn = pb.comp("dv", (lambda v, d=dst3: v.tensor_tensor(
                    d[:, :, 0:NS:2], tmp[0][:, 0:2048].rearrange("p (c w) -> p c w", w=HD),
                    tmp[2][:, 0:2048].rearrange("p (c w) -> p c w", w=HD), ALU.subtract)),
                    deps=[t1, tA])
                t3 = pb.comp("dv", lambda v: v.tensor_mul(
                    tmp[0][:, 0:2048], sinb[:], tmp[1][:, 0:2048]), deps=[evn])
                t4 = pb.comp("dv", lambda v: v.tensor_mul(
                    tmp[2][:, 0:2048], cosb[:], tmp[3][:, 0:2048]), deps=[evn])
                odd = pb.comp("dv", (lambda v, d=dst3: v.tensor_tensor(
                    d[:, :, 1:NS:2], tmp[0][:, 0:2048].rearrange("p (c w) -> p c w", w=HD),
                    tmp[2][:, 0:2048].rearrange("p (c w) -> p c w", w=HD), ALU.add)),
                    deps=[t3, t4])
                return odd

            l_B = bcload(0, NS, bcscr[0:CH, :],
                         [a_lam, v_ds[3], mm_ac[3], v_v[3], mm_th[3]])
            bn_done = rope_build(msb, RO("E"), RO("F"), bn,
                                 [l_B, v_ds[3], a_ds, z_sh, d_shA, d_shB, l_bwe,
                                  l_bwo, v_d2b, ev_na[NCK - 1]])
            l_cwe = pb.dma("sp", RO("E"), cwe_row, deps=[bn_done])
            l_cwo = pb.dma("sp", RO("F"), cwo_row, deps=[bn_done])
            l_C = bcload(128, NS, bcscr[0:CH, :], [bn_done])
            cn_done = rope_build(msc, RO("E"), RO("F"), cn,
                                 [l_C, bn_done, l_cwe, l_cwo])

            d_bs1 = pb.dma("pl", bstack[65:128, :], bn[0:CH - 1, :], deps=[bn_done])
            d_bs2 = pb.dma("pl",
                           bstack[64:65, :].rearrange("o (c w) -> o c w", w=NS)[:, 1:NCK, :],
                           bn[CH - 1:CH, :].rearrange("o (c w) -> o c w", w=NS)[:, 0:NCK - 1, :],
                           deps=[bn_done])
            z_bs = pb.comp("dv", lambda v: v.memset(bstack[64:65, 0:NS], 0.0),
                           deps=[bn_done])
            z_bnT = pb.comp("dv", lambda v: v.memset(bnT[:, 0:1], 0.0), deps=[c_id])
            tr_b = {}
            for c in range(NCK):
                trb = pb.comp("pe", (lambda t, c=c: t.transpose(
                    pstb[:, 0:CH], bn[0:CH, NS * c:NS * (c + 1)], identsb[0:CH, 0:CH])),
                    deps=[bn_done, tr_b.get(c - 1)])
                tr_b[c] = pb.comp("ac", (lambda s, c=c: s.copy(
                    bnT[:, 1 + CH * c:1 + CH * (c + 1)], pstb[:, 0:CH])), deps=[trb])
            tr_c = {}
            for c in range(NCK):
                trc = pb.comp("pe", (lambda t, c=c: t.transpose(
                    pstc[:, 0:CH], cn[0:CH, NS * c:NS * (c + 1)], identsb[0:CH, 0:CH])),
                    deps=[cn_done, tr_c.get(c - 1), tr_b[NCK - 1]])
                tr_c[c] = pb.comp("ac", (lambda s, c=c: s.copy(
                    cnT[:, CH * c:CH * (c + 1)], pstc[:, 0:CH])), deps=[trc])

            l_dr = pb.dma("sp", RO("D"), drow_in[:, 0:2048], deps=[d_flat, mm_cb[3]])
            l_dr2 = pb.dma("sp", RO("C"), drow_in[:, 2048:4096],
                           deps=[e_dec, mm_na[3], mm_db[3]])
            mm_df = {}
            ev_df = {}
            for q in range(8):
                src = (RO("D") if q < 4 else RO("C"))
                mm_df[q] = pb.comp("pe", (lambda t, q=q, src=src: t.matmul(
                    ps128[:, :], ONES32[:, 0:128], src[:, 512 * (q % 4):512 * (q % 4 + 1)],
                    start=True, stop=True)),
                    deps=[l_dr, l_dr2, ev_db[3], ev_df.get(q - 1), c_on32])
                ev_df[q] = pb.comp("dv", (lambda v, q=q: v.tensor_copy(
                    dful[:, 512 * q:512 * (q + 1)], ps128[:, :])), deps=[mm_df[q]])
            l_b2 = pb.dma("sp", RO("A"), b2row, deps=[e_dec, mm_en[3], mm_db[3]])
            l_wr = pb.dma("sp", RO("B"), wrow[:, 0:2048], deps=[mm_db[3]])
            l_wr2 = pb.dma("sp", RO("E"), wrow[:, 2048:4096], deps=[cn_done])
            mm_b2 = {}
            ev_b2 = {}
            for q in range(4):
                mm_b2[q] = pb.comp("pe", (lambda t, q=q: t.matmul(
                    ps128[:, :], ONES[0:1, 0:128], RO("A")[:, 512 * q:512 * (q + 1)],
                    start=True, stop=True)), deps=[l_b2, ev_df[7], mm_b2.get(q - 1),
                                                   ev_b2.get(q - 1)])
                ev_b2[q] = pb.comp("dv", (lambda v, q=q: v.tensor_copy(
                    b2rep[:, 512 * q:512 * (q + 1)], ps128[:, :])), deps=[mm_b2[q]])
            mm_wr = {}
            ev_wr = {}
            for q in range(8):
                src = (RO("B") if q < 4 else RO("E"))
                lhs1 = (ONES if q < 4 else ONES64)
                mm_wr[q] = pb.comp("pe", (lambda t, q=q, src=src, lhs1=lhs1: t.matmul(
                    ps128[:, :], lhs1[:, 0:128], src[:, 512 * (q % 4):512 * (q % 4 + 1)],
                    start=True, stop=True)),
                    deps=[l_wr, l_wr2, ev_b2[3], ev_wr.get(q - 1), c_on64])
                ev_wr[q] = pb.comp("dv", (lambda v, q=q: v.tensor_copy(
                    wrep[:, 512 * q:512 * (q + 1)], ps128[:, :])), deps=[mm_wr[q]])

            p2a_done = pb.barrier()
            for nm, ap_ in (("acs", acs[:, :]), ("ang", ang[:, :]), ("bn", bn[:, :]),
                            ("cn", cn[:, :]),
                            ("dsstk", dsstk[:, :]), ("dtlstk", dtlstk[:, :]),
                            ("dbc", dbc[:, :]),
                            ("bstack", bstack[:, :]), ("bnT", bnT[:, 0:T + 1]),
                            ("cnT", cnT[:, :]),
                            ("expacsD", expacsD[:, :]), ("nacsT", nacsT[:, :]),
                            ("dful", dful[:, :]),
                            ("wrep", wrep[:, :]), ("b2rep", b2rep[:, :])):
                tap(nm, ap_, p2a_done)
            tap("dt", tmp[0][:, :], p2a_done)

        if BUILD_PHASES < 3:
            pb.finalize(block)
            return dram

        with contextlib.ExitStack() as S:
            xsal = [S.enter_context(nc.sbuf_tensor(f"sxa{i}", [128, DI], BF16))
                    for i in range(2)]
            statebf = S.enter_context(nc.sbuf_tensor("sstbf", [128, DI], BF16))
            mpre = S.enter_context(nc.sbuf_tensor("smp", [128, 2 * 512], BF16))
            statbA = [S.enter_context(nc.sbuf_tensor(f"sstb{i}", [128, DI], BF16))
                      for i in range(2)]
            gmst = S.enter_context(nc.sbuf_tensor("sgm", [128, 2 * CH], BF16))
            gmtmp = S.enter_context(nc.sbuf_tensor("sgt", [CH, CH], BF16))
            lhsA = S.enter_context(nc.sbuf_tensor("sla", [128, 128], F32))
            rhsA = S.enter_context(nc.sbuf_tensor("sra", [128, NH * CH], F32))
            state = S.enter_context(nc.sbuf_tensor("sste", [128, DI], F32))
            ych = [S.enter_context(nc.sbuf_tensor(f"syc{i}", [CH, 2 * DI], BF16))
                   for i in range(2)]
            ps_d = [S.enter_context(nc.psum_tensor(f"qd{i}", [128, 512], F32))
                    for i in range(2)]
            ps_g = S.enter_context(nc.psum_tensor("qg", [CH, 128], F32))
            ps_yd = [S.enter_context(nc.psum_tensor(f"qyd{i}", [CH, 512], F32))
                     for i in range(2)]
            ps_yo = S.enter_context(nc.psum_tensor("qyo", [CH, 512], F32))
            ps_st = [S.enter_context(nc.psum_tensor(f"qst{i}", [128, 512], F32))
                     for i in range(2)]

            l_oh = pb.dma("sp", rhsA[0:NH, :], oh_hl_in, deps=p2a_done)
            onesm1 = pb.comp("dv", lambda v: v.memset(lhsA[64:65, :], -1.0),
                             deps=p2a_done)
            zro_st = [pb.comp("dv", (lambda v, q=q: v.memset(
                state[:, 512 * q:512 * (q + 1)], 0.0)), deps=p2a_done) for q in range(8)]
            zro_st += [pb.comp("dv", lambda v: v.memset(statebf[:, :], 0.0),
                               deps=p2a_done)]

            P = {}

            def prep_head(c):
                cp = c % 2
                buf = c % 2
                tok0 = CH * c
                lda = pb.dma("sp", xsal[buf][0:CH, :], xs_hbm[tok0:tok0 + CH, :],
                             deps=[P.get(("smm", c - 2, 7))])
                if c > 0:
                    lds = pb.dma("sp", xsal[buf][64:128, :],
                                 xs_hbm[tok0 - 1:tok0 + CH - 1, :],
                                 deps=[P.get(("smm", c - 2, 7))])
                    shz = None
                else:
                    lds = pb.dma("sp", xsal[0][65:128, :], xs_hbm[0:CH - 1, :],
                                 deps=p2a_done)
                    shz = pb.comp("dv", lambda v: v.memset(xsal[0][64:65, :], 0.0),
                                  deps=p2a_done)
                dtls = dtlstk[:, NH * c:NH * (c + 1)].rearrange(
                    "p w -> p () w").broadcast_to([128, HD, NH])
                P[("xd", c)] = pb.comp("dv", (lambda v, b=buf, d=dtls: v.tensor_tensor(
                    xsal[b][:].rearrange("p (a w) -> p a w", w=NH),
                    xsal[b][:].rearrange("p (a w) -> p a w", w=NH), d, ALU.mult)),
                    deps=[lda, lds, shz])
                g_mm1 = pb.comp("pe", (lambda t, tok0=tok0: t.matmul(
                    ps_g[:, 0:CH], bnT[:, 1 + tok0:1 + tok0 + CH],
                    cnT[:, tok0:tok0 + CH], start=True, stop=False)),
                    deps=[P.get(("gm1", c - 1)), P.get(("gm2t", c - 1)), l_oh, onesm1])
                g_mm2 = pb.comp("pe", (lambda t, tok0=tok0: t.matmul(
                    ps_g[:, CH:128], bnT[:, tok0:tok0 + CH],
                    cnT[:, tok0:tok0 + CH], start=False, stop=True)), deps=[g_mm1])
                gm1 = pb.comp("dv", (lambda v, cp=cp: v.tensor_tensor(
                    gmst[0:CH, CH * cp:CH * (cp + 1)], ps_g[:, 0:CH], trilb[:, :],
                    ALU.mult)), deps=[g_mm2, c_tril, P.get(("stb", c - 2, 7))])
                gm2t = pb.comp("dv", lambda v: v.tensor_tensor(
                    gmtmp[:, :], ps_g[:, CH:128], trilb[:, :], ALU.mult),
                    deps=[g_mm2, P.get(("gmdn", c - 1))])
                gmdn = pb.dma("pl", gmst[64:128, CH * cp:CH * (cp + 1)], gmtmp[:, :],
                              deps=[gm2t, P.get(("stb", c - 2, 7))])
                P[("gm1", c)], P[("gm2t", c)], P[("gmdn", c)] = gm1, gm2t, gmdn
                la1 = pb.comp("dv", (lambda v, c=c: v.tensor_copy(
                    lhsA[0:CH, 0:CH], nacsT[:, NH * c:NH * (c + 1)])),
                    deps=[P.get(("dmm", c - 1, 7)), onesm1])
                la2 = pb.comp("dv", (lambda v, c=c: v.tensor_copy(
                    lhsA[0:CH, CH:128], nacsT[:, NH * c:NH * (c + 1)])), deps=[la1])
                flt0 = pb.dma("pl", scrC, nacsT[:, NH * c:NH * (c + 1)],
                              deps=[P.get(("flt", c - 1))])
                flt = pb.dma("pl", rhsA[64:65, :],
                             scrC.rearrange("h s -> () (h s)"),
                             deps=[flt0, P.get(("dmm", c - 1, 7))])
                P[("flt", c)] = flt
                P[("la", c)] = la2

            def prep_g(c, g):
                cp = c % 2
                pd = ps_d[g % 2]
                war_mp = P.get(("mp", c, g - 2)) or P.get(("mp", c - 1, g + 6))
                dmm = pb.comp("pe", (lambda t, g=g, pd=pd: (
                    t.matmul(pd[:, :], lhsA[0:65, :],
                             rhsA[0:65, 512 * g:512 * (g + 1)], start=True, stop=False),
                    t.matmul(pd[:, :], maskBb[:, :],
                             oh_l[:, 512 * g:512 * (g + 1)], start=False, stop=True))[-1]),
                    deps=[P[("la", c)], P[("flt", c)], l_oh, c_mb, c_ohl, war_mp])
                P[("dmm", c, g)] = dmm
                war_stb = P.get(("stb", c, g - 2)) or P.get(("stb", c - 1, g + 6))
                mp = pb.comp("ac", (lambda s, g=g, pd=pd: s.activation(
                    mpre[:, 512 * (g % 2):512 * (g % 2 + 1)], pd[:, :], AF.Exp)),
                    deps=[dmm, war_stb])
                P[("mp", c, g)] = mp

                def stbf(v, g=g, cp=cp):
                    r = None
                    for i in range(8):
                        r = v.tensor_tensor(
                            statbA[cp][:, 512 * g + 64 * i:512 * g + 64 * (i + 1)],
                            mpre[:, 512 * (g % 2) + 64 * i:
                                 512 * (g % 2) + 64 * (i + 1)],
                            gmst[:, CH * cp:CH * (cp + 1)], ALU.mult)
                    return r
                P[("stb", c, g)] = pb.comp("dv", stbf, deps=[
                    mp, P[("gmdn", c)], P[("gm1", c)], P.get(("ym", c - 2, g))])

            def comp_g(c, g):
                buf = c % 2
                cp = c % 2
                tok0 = CH * c

                def ydmm(t, g=g, b=buf, cp=cp):
                    r = None
                    for i in range(8):
                        h = 8 * g + i
                        mv = xsal[b][:].rearrange("p (a w) -> p a w", w=NH)[:, :, h]
                        r = t.matmul(ps_yd[g % 2][:, 64 * i:64 * (i + 1)],
                                     statbA[cp][:, 512 * g + 64 * i:
                                                512 * g + 64 * (i + 1)], mv,
                                     start=(i == 0), stop=(i == 7))
                    return r
                ym = pb.comp("pe", ydmm, deps=[
                    P[("stb", c, g)], P[("xd", c)],
                    P.get(("yde", c, g - 2)) or P.get(("yde", c - 1, g + 6))])
                P[("ym", c, g)] = ym

                def yomm(t, g=g, tok0=tok0):
                    r = None
                    for i in range(8):
                        h = 8 * g + i
                        mv = statebf[:, :].rearrange("p (a w) -> p a w", w=NH)[:, :, h]
                        r = t.matmul(ps_yo[:, 64 * i:64 * (i + 1)],
                                     cnT[:, tok0:tok0 + CH], mv,
                                     start=(i == 0), stop=(i == 7))
                    return r
                yo = pb.comp("pe", yomm, deps=[
                    ym, P.get(("yoe", c, g - 1)) or P.get(("yoe", c - 1, 7)),
                    P.get(("sbf", c - 1))] + ([] if c > 0 else zro_st))
                P[("yo", c, g)] = yo

                war_ych = P.get(("ychst", buf)) or []
                yde = pb.comp("ac", (lambda s, g=g, b=buf: s.copy(
                    ych[b][0:CH, 0:DI].rearrange(
                        "p (a w) -> p a w", w=NH)[:, :, 8 * g:8 * (g + 1)],
                    ps_yd[g % 2][:, :].rearrange("p (i w) -> p w i", i=8))),
                    deps=[ym] + war_ych)
                P[("yde", c, g)] = yde
                yoe = pb.comp("ac", (lambda s, g=g, b=buf: s.copy(
                    ych[b][0:CH, DI:2 * DI].rearrange(
                        "p (a w) -> p a w", w=NH)[:, :, 8 * g:8 * (g + 1)],
                    ps_yo[:, :].rearrange("p (i w) -> p w i", i=8))),
                    deps=[yo] + war_ych)
                P[("yoe", c, g)] = yoe

            def comp_tail(c):
                buf = c % 2
                dss = dsstk[:, NH * c:NH * (c + 1)].rearrange(
                    "p w -> p () w").broadcast_to([128, HD, NH])
                dx = pb.comp("dv", (lambda v, b=buf, d=dss: v.tensor_tensor(
                    xsal[b][:].rearrange("p (a w) -> p a w", w=NH),
                    xsal[b][:].rearrange("p (a w) -> p a w", w=NH), d, ALU.mult)),
                    deps=[P[("ym", c, 7)], P[("xd", c)]])
                P[("dx", c)] = dx
                for g in range(8):
                    pq = ps_st[g % 2]
                    war_upd = P.get(("upd", c, g - 2)) or P.get(("upd", c - 1, g + 6))
                    smm = pb.comp("pe", (lambda t, g=g, pq=pq, b=buf, c=c: t.matmul(
                        pq[:, :], bstack[:, NS * c:NS * (c + 1)],
                        xsal[b][:].rearrange("p (a w) -> p a w", w=NH)[:, :, 8 * g:8 * (g + 1)],
                        start=True, stop=True)),
                        deps=[dx, war_upd, P[("yo", c, 7)]])
                    P[("smm", c, g)] = smm

                    def updf(v, g=g, pq=pq, c=c):
                        r = None
                        for i in range(8):
                            h = 8 * g + i
                            sl = state[:, :].rearrange("p (a w) -> p a w", w=NH)[:, :, h]
                            pv = pq[:, :].rearrange("p (a w) -> p a w", w=8)[:, :, i]
                            r = v.scalar_tensor_tensor(
                                sl, sl, dbc[:, NH * c + h:NH * c + h + 1], pv,
                                ALU.mult, ALU.add)
                        return r
                    P[("upd", c, g)] = pb.comp("dv", updf, deps=[smm]
                                               + ([] if c > 0 else zro_st))
                sbf = pb.comp("dv", lambda v: v.tensor_copy(statebf[:, :], state[:, :]),
                              deps=[P[("upd", c, 7)], P[("yo", c, 7)]])
                P[("sbf", c)] = sbf
                evs = [P[("yde", c, g)] for g in range(8)] + \
                      [P[("yoe", c, g)] for g in range(8)]
                tok0 = CH * c
                sd = pb.dma("sp", yd_hbm[tok0:tok0 + CH, :], ych[buf][0:CH, 0:DI],
                            deps=evs)
                so = pb.dma("sp", yo_hbm[tok0:tok0 + CH, :], ych[buf][0:CH, DI:2 * DI],
                            deps=evs)
                P[("ychst", buf)] = [sd, so]

            prep_head(0)
            for g in range(8):
                prep_g(0, g)
            for c in range(NCK):
                if c + 1 < NCK:
                    prep_head(c + 1)
                for g in range(8):
                    if c + 1 < NCK:
                        prep_g(c + 1, g)
                    comp_g(c, g)
                comp_tail(c)
            p2b_done = pb.barrier()

        if BUILD_PHASES < 4:
            pb.finalize(block)
            return dram

        ssd2_sc.close()

        f1_sc = E(contextlib.ExitStack())
        FE = f1_sc.enter_context
        xnT2 = FE(nc.sbuf_tensor("xnT2", [128, 16 * T], BF16))
        t1b = [FE(nc.sbuf_tensor(f"f1t{i}", [128, 16 * 256], BF16))
               for i in range(2)]
        hst = [FE(nc.sbuf_tensor(f"f1h{i}", [128, 2048], BF16))
               for i in range(2)]
        b1sb = FE(nc.sbuf_tensor("f1b", [128, HF // 128], F32))
        psF = [FE(nc.psum_tensor(f"f1p{i}", [128, 512], F32)) for i in range(4)]
        l_b1 = pb.dma("sp", b1sb[:], b1c_in, deps=p2b_done)
        l_xnt = pb.dma("pl", xnT2[:, :], xnT_hbm, deps=p2b_done)

        def f1_gen():
            ldg = {}
            mmq = {}
            evq_ = {}
            stf = {}
            for hft in range(HF // 128):
                g, hi = divmod(hft, 2)
                if hi == 0:
                    ldg[g] = pb.dma("pl", t1b[g % 2][:].rearrange(
                        "p (k q) -> p k q", k=16),
                        t1_hbm[:, 256 * g:256 * (g + 1)].rearrange(
                            "(k p) q -> p k q", p=128),
                        deps=[mmq.get((2 * g - 3, 3))]
                        + (p2b_done if g < 2 else []))
                for q in range(4):
                    def mmf_(t, g=g, hi=hi, q=q):
                        r = None
                        for kt in range(16):
                            r = t.matmul(
                                psF[q][:, :],
                                t1b[g % 2][:, 256 * kt + 128 * hi:
                                           256 * kt + 128 * (hi + 1)],
                                xnT2[:, 2048 * kt + 512 * q:
                                     2048 * kt + 512 * (q + 1)],
                                start=(kt == 0), stop=(kt == 15))
                        return r
                    mmq[(hft, q)] = pb.comp("pe", mmf_, deps=[
                        ldg[g], l_xnt, evq_.get((hft - 1, q))])
                    evq_[(hft, q)] = pb.comp("ac", (lambda s, hft=hft, q=q:
                        s.activation(hst[hft % 2][:, 512 * q:512 * (q + 1)],
                                     psF[q][:, :], AF.Gelu,
                                     bias=b1sb[:, hft:hft + 1],
                                     scale=mcol[:, 2:3])),
                        deps=[mmq[(hft, q)], l_b1, stf.get(hft - 2)])
                stf[hft] = pb.dma("sp", h_hbm[128 * hft:128 * (hft + 1), :],
                                  hst[hft % 2][:],
                                  deps=[evq_[(hft, q)] for q in range(4)])
                yield

        f1c = f1_gen()

        def f1pump(n=1):
            for _ in range(n):
                if next(f1c, _SENT) is _SENT:
                    return

        with contextlib.ExitStack() as S:
            ydt = [S.enter_context(nc.sbuf_tensor(f"g0_{i}", [128, DI], BF16)) for i in range(2)]
            yot = [S.enter_context(nc.sbuf_tensor(f"g1_{i}", [128, DI], BF16)) for i in range(2)]
            xst = [S.enter_context(nc.sbuf_tensor(f"g2_{i}", [128, DI], BF16)) for i in range(2)]
            zt = [S.enter_context(nc.sbuf_tensor(f"g3_{i}", [128, DI], BF16)) for i in range(2)]
            ybuf = S.enter_context(nc.sbuf_tensor("g4", [128, DI], BF16))
            zsb = S.enter_context(nc.sbuf_tensor("g5", [128, DI], BF16))
            ynstg = [S.enter_context(nc.sbuf_tensor("g7_0", [128, DI], BF16))]
            msc_ = S.enter_context(nc.sbuf_tensor("g8", [128, NTT], F32))
            ptr = [S.enter_context(nc.psum_tensor(f"g9_{i}", [128, 128], BF16)) for i in range(2)]

            stg_done = [None, None]
            cons = [None, None]
            prev_sq = None
            prev_tr_last = None
            sq2 = None
            for tt in range(NTT):
                b = tt % 2
                r0 = 128 * tt
                l1 = pb.dma("sp", ydt[b][:], yd_hbm[r0:r0 + 128, :],
                            deps=[cons[b]] + (p2b_done if tt < 2 else []))
                l2 = pb.dma("sp", yot[b][:], yo_hbm[r0:r0 + 128, :], deps=[cons[b]])
                l3 = pb.dma("sp", xst[b][:], xs_hbm[r0:r0 + 128, :],
                            deps=[cons[b], stg_done[b]])
                l4 = pb.dma("pl", zt[b][:], z_hbm[r0:r0 + 128, :], deps=[cons[b]])
                e0 = expacsD[0:CH, NH * 2 * tt:NH * (2 * tt + 1)].rearrange(
                    "p w -> p () w").broadcast_to([CH, HD, NH])
                e1 = expacsD[64:128, NH * (2 * tt + 1):NH * (2 * tt + 2)].rearrange(
                    "p w -> p () w").broadcast_to([CH, HD, NH])
                ya = pb.comp("dv", (lambda v, b=b, e=e0: v.tensor_tensor(
                    ybuf[0:CH, :].rearrange("p (a w) -> p a w", w=NH),
                    yot[b][0:CH, :].rearrange("p (a w) -> p a w", w=NH), e, ALU.mult)),
                    deps=[l2, stg_done[b], prev_sq])
                yb = pb.comp("dv", (lambda v, b=b, e=e1: v.tensor_tensor(
                    ybuf[64:128, :].rearrange("p (a w) -> p a w", w=NH),
                    yot[b][64:128, :].rearrange("p (a w) -> p a w", w=NH), e, ALU.mult)),
                    deps=[l2, stg_done[b], prev_sq])
                yc = pb.comp("dv", (lambda v, b=b: v.tensor_add(
                    ybuf[:, :], ybuf[:, :], ydt[b][:, :])), deps=[ya, yb, l1])
                yd2 = pb.comp("dv", (lambda v, b=b: v.tensor_mul(
                    zsb[:, :], xst[b][:, :], dful[:, :])), deps=[l3, yc, prev_sq])
                ye = pb.comp("dv", lambda v: v.tensor_add(ybuf[:, :], ybuf[:, :],
                                                          zsb[:, :]), deps=[yd2])
                zs = pb.comp("ac", (lambda s, b=b: s.activation(
                    zsb[:, :], zt[b][:, :], AF.Silu)), deps=[l4, ye])
                g2 = pb.comp("dv", lambda v: v.tensor_mul(ybuf[:, :], ybuf[:, :],
                                                          zsb[:, :]), deps=[zs])
                cons[b] = g2
                sq2 = pb.comp("ac", (lambda s, tt=tt: s.activation(
                    zsb[:, :], ybuf[:, :], AF.Square,
                    accum_out=msc_[:, tt:tt + 1])), deps=[g2])
                prev_sq = sq2
                yn = pb.comp("dv", (lambda v, b=b: v.tensor_mul(
                    xst[b][:, :], ybuf[:, :], wrep[:, :])),
                    deps=[sq2, yd2, prev_tr_last])
                trs = []
                for k in range(32):
                    tr = pb.comp("pe", (lambda t, k=k, b=b: t.transpose(
                        ptr[k % 2][:, :], xst[b][:, 128 * k:128 * (k + 1)],
                        identsb[:, :])),
                        deps=[yn, trs[k - 2] if k >= 2 else stg_done[1 - b]])
                    ev = pb.comp("dv", (lambda v, k=k: v.tensor_copy(
                        ynstg[0][:, 128 * k:128 * (k + 1)], ptr[k % 2][:, :])), deps=[tr])
                    trs.append(ev)
                prev_tr_last = trs[-1]
                stg_done[b] = pb.dma("sp", ynT_hbm[tt, :, :], ynstg[0][:, :],
                                     deps=trs)
                f1pump(4)
            while next(f1c, _SENT) is not _SENT:
                pass
            r1 = pb.comp("dv", lambda v: v.tensor_scalar(
                msc_[:, :], msc_[:, :], 1.0 / DI, RMS_EPS,
                ALU.mult, ALU.add), deps=[sq2, stg_done[0], stg_done[1]])
            r2 = pb.comp("dv", lambda v: v.reciprocal(msc_[:, :], msc_[:, :]), deps=[r1])
            r3 = pb.comp("ac", lambda s: s.activation(msc_[:, :], msc_[:, :], AF.Sqrt),
                         deps=[r2])
            st_r = pb.dma("sp", rstd_out, msc_[:, :], deps=[r3])
            p3a_done = pb.barrier()
        f1_sc.close()
        ssd_sc.close()

        if BUILD_PHASES < 5:
            pb.finalize(block)
            return dram

        with contextlib.ExitStack() as S:
            wo = S.enter_context(nc.sbuf_tensor("p3w", [128, 32 * 2048], BF16))
            ynb_ = [S.enter_context(nc.sbuf_tensor(f"p3y{i}", [128, 32 * 128], BF16))
                    for i in range(2)]
            sst = [S.enter_context(nc.sbuf_tensor(f"p3s{i}", [128, 2048], BF16))
                   for i in range(2)]
            psS = [S.enter_context(nc.psum_tensor(f"p3p{i}", [128, 512], F32))
                   for i in range(8)]
            l_wo = [pb.dma("pl",
                           wo[:, 8 * 2048 * j:8 * 2048 * (j + 1)].rearrange(
                               "p (kt f) -> p kt f", kt=8),
                           woutT[1024 * j:1024 * (j + 1), :].rearrange(
                               "(kt p) f -> p kt f", p=128), deps=p3a_done)
                    for j in range(4)]
            ldy = {}
            mms = {}
            evs = {}
            sts = {}
            for tt in range(NTT):
                ldy[tt] = pb.dma("sp", ynb_[tt % 2][:], ynT_hbm[tt, :, :],
                                 deps=[mms.get(tt - 2)] + (p3a_done if tt < 2 else []))

                def mall(t, tt=tt):
                    r = None
                    for kt in range(32):
                        for q in range(4):
                            pf = psS[(tt % 2) * 4 + q]
                            r = t.matmul(pf[:, :], ynb_[tt % 2][:, 128 * kt:128 * (kt + 1)],
                                     wo[:, 2048 * kt + 512 * q: 2048 * kt + 512 * (q + 1)],
                                     start=(kt == 0), stop=(kt == 31))
                    return r
                mms[tt] = pb.comp("pe", mall, deps=l_wo + [ldy[tt], evs.get(tt - 2)])

                def evq(s, tt=tt):
                    r = None
                    for q in range(4):
                        pf = psS[(tt % 2) * 4 + q]
                        r = s.copy(sst[tt % 2][:, 512 * q:512 * (q + 1)], pf[:, :])
                    return r
                evs[tt] = pb.comp("ac", evq, deps=[mms[tt], sts.get(tt - 2)])
                sts[tt] = pb.dma("sp", s_out[128 * tt:128 * (tt + 1), :],
                                 sst[tt % 2][:], deps=[evs[tt]])
            p3b_done = pb.barrier()

        if BUILD_PHASES < 8:
            pb.finalize(block)
            return dram

        with contextlib.ExitStack() as S:
            t2b = [S.enter_context(nc.sbuf_tensor("f2t0", [128, 8 * 2048], BF16))]
            hti = [S.enter_context(nc.sbuf_tensor(f"f2h{i}", [128, 8 * 512], BF16))
                   for i in range(2)]
            y2a = S.enter_context(nc.sbuf_tensor("f2y", [128, 16 * 2048], F32))
            y2s = [S.enter_context(nc.sbuf_tensor(f"f2s{i}", [128, 2048], F32))
                   for i in range(2)]
            lmx = S.enter_context(nc.sbuf_tensor("f2m", [128, NTT + 2], F32))
            lrow = S.enter_context(nc.sbuf_tensor("f2r", [1, 128], F32))
            psY = [S.enter_context(nc.psum_tensor(f"f2p{i}", [128, 512], F32))
                   for i in range(8)]
            NBLK = 8
            NKB = HF // 128 // NBLK
            lb = None
            lh = {}
            mmy = {}
            evy = {}
            for blk in range(NBLK):
                lb = pb.dma("sp", t2b[0][:].rearrange(
                    "p (kt f) -> p kt f", kt=NKB),
                    t2_hbm[1024 * blk:1024 * (blk + 1), :].rearrange(
                        "(kt p) f -> p kt f", p=128),
                    deps=[mmy.get((blk - 1, NTT - 1))] + (p3b_done if blk < 1 else []))
                for tt in range(NTT):
                    ts = tt // 4
                    tq = tt % 4
                    if tq == 0:
                        lh[(blk, ts)] = pb.dma("pl" if ts % 2 else "sp",
                            hti[ts % 2][:].rearrange("p (k q) -> p k q", k=NKB),
                            h_hbm[1024 * blk:1024 * (blk + 1),
                                  512 * ts:512 * (ts + 1)].rearrange(
                                "(k p) q -> p k q", p=128),
                            deps=[mmy.get((blk, 4 * ts - 5)),
                                  mmy.get((blk - 1, 4 * ts + 11))]
                            + (p3b_done if blk == 0 and ts < 2 else []))

                    def mall(t, blk=blk, tt=tt, ts=ts, tq=tq):
                        r = None
                        for kt in range(NKB):
                            for q in range(4):
                                pf = psY[(tt % 2) * 4 + q]
                                r = t.matmul(pf[:, :],
                                         hti[ts % 2][:, 512 * kt + 128 * tq:
                                                      512 * kt + 128 * (tq + 1)],
                                         t2b[0][:, 2048 * kt + 512 * q:
                                                2048 * kt + 512 * (q + 1)],
                                         start=(kt == 0), stop=(kt == NKB - 1))
                        return r
                    mmy[(blk, tt)] = pb.comp("pe", mall, deps=[
                        lb, lh[(blk, ts)],
                        evy.get((blk, tt - 2)), evy.get((blk - 1, NTT - 2 + tt))])

                    def evq(v, blk=blk, tt=tt):
                        r = None
                        for q in range(4):
                            pf = psY[(tt % 2) * 4 + q]
                            dst = y2a[:, 2048 * tt + 512 * q: 2048 * tt + 512 * (q + 1)]
                            if blk == 0:
                                r = v.tensor_copy(dst, pf[:, :])
                            else:
                                r = v.tensor_add(dst, dst, pf[:, :])
                        return r
                    evy[(blk, tt)] = pb.comp("dv", evq, deps=[mmy[(blk, tt)],
                                                              evy.get((blk - 1, tt))])
            fin = {}
            for tt in range(NTT):
                f1_ = pb.comp("dv", (lambda v, tt=tt: v.scalar_tensor_tensor(
                    y2s[tt % 2][:], y2a[:, 2048 * tt:2048 * (tt + 1)], mcol[:, 3:4],
                    b2rep[:, :], ALU.mult, ALU.add)),
                    deps=[evy[(NBLK - 1, tt)], fin.get(tt - 2)])
                f2_ = pb.comp("dv", (lambda v, tt=tt: v.reduce_max(
                    lmx[:, tt:tt + 1], y2s[tt % 2][:], AX.X,
                    apply_absolute_value=True)), deps=[f1_])
                fin[tt] = pb.dma("sp", y2f_out[128 * tt:128 * (tt + 1), :],
                                 y2s[tt % 2][:], deps=[f2_])
            mred = pb.comp("dv", lambda v: v.reduce_max(
                lmx[:, NTT:NTT + 1], lmx[:, 0:NTT], AX.X), deps=[fin[NTT - 1],
                                                                 fin[NTT - 2]])
            d_lm0 = pb.dma("pl", scrD, lmx[:, NTT:NTT + 1], deps=[mred])
            d_lm = pb.dma("pl", lrow[0:1, 0:128],
                          scrD.rearrange("p o -> o p"), deps=[d_lm0])
            gmx = pb.comp("dv", lambda v: v.reduce_max(lmx[0:1, NTT + 1:NTT + 2],
                                                       lrow[0:1, 0:128], AX.X),
                          deps=[d_lm])
            st_lm = pb.dma("sp", lmax_out, lmx[0:1, NTT + 1:NTT + 2], deps=[gmx])
            pb.barrier()

        pb.finalize(block)
    return dram


def build_final(nc: bass.Bass):
    dram = {}
    din = lambda n, s, d=F32: _dram(nc, dram, n, s, d, "ExternalInput")
    dout = lambda n, s, d=F32: _dram(nc, dram, n, s, d, "ExternalOutput")
    x_in = din("x", (T, DM))
    y2f = din("y2f", (T, DM))
    s_raw = din("s_raw", (T, DM), BF16)
    rstd = din("rstd_all", (128, NTT))
    maxv = din("maxv", (1, 8))
    out = dout("out", (T, DM))

    ctx = contextlib.ExitStack()
    E = ctx.enter_context
    with ctx:
        pb = PB(nc, E)
        yt = [E(nc.sbuf_tensor(f"by{i}", [128, DM], F32)) for i in range(2)]
        st_ = [E(nc.sbuf_tensor(f"bs{i}", [128, DM], BF16)) for i in range(2)]
        xt = [E(nc.sbuf_tensor(f"bx{i}", [128, DM], F32)) for i in range(2)]
        ot = [E(nc.sbuf_tensor(f"bo{i}", [128, DM], F32)) for i in range(2)]
        gt = E(nc.sbuf_tensor("bg", [128, DM], BF16))
        rst = E(nc.sbuf_tensor("br", [128, NTT], F32))
        mrow = E(nc.sbuf_tensor("bm", [1, 8], F32))
        scol = E(nc.sbuf_tensor("bc", [128, 4], F32))
        pq = E(nc.psum_tensor("bp", [128, 2], F32))
        block = E(nc.Block())
        ones = E(nc.sbuf_tensor("bones", [1, 128], F32))

        oc = pb.comp("dv", lambda v: v.memset(ones[:, :], 1.0))
        l_m = pb.dma("sp", mrow[:, :], maxv)
        l_r = pb.dma("sp", rst[:, :], rstd)
        g1 = pb.comp("dv", lambda v: v.reduce_max(scol[0:1, 0:1], mrow[:, :], AX.X),
                     deps=[l_m])
        g2 = pb.comp("dv", lambda v: v.tensor_scalar(
            scol[0:1, 1:2], scol[0:1, 0:1], 1.0 / QMAX, QEPS, ALU.mult, ALU.add),
            deps=[g1])
        g3 = pb.comp("dv", lambda v: v.reciprocal(scol[0:1, 2:3], scol[0:1, 1:2]),
                     deps=[g2])
        m1 = pb.comp("pe", lambda t: t.matmul(pq[:, 0:1], ones[0:1, 0:128],
                                              scol[0:1, 1:2], start=True, stop=True),
                     deps=[g2, oc])
        m2 = pb.comp("pe", lambda t: t.matmul(pq[:, 1:2], ones[0:1, 0:128],
                                              scol[0:1, 2:3], start=True, stop=True),
                     deps=[g3, m1])
        e1 = pb.comp("dv", lambda v: v.tensor_copy(scol[:, 0:1], pq[:, 0:1]),
                     deps=[m1])
        e2 = pb.comp("dv", lambda v: v.tensor_copy(scol[:, 1:2], pq[:, 1:2]),
                     deps=[m2, e1])
        done = {}
        for tt in range(NTT):
            b = tt % 2
            r0 = 128 * tt
            l1 = pb.dma("sp", yt[b][:], y2f[r0:r0 + 128, :], deps=[done.get(tt - 2)])
            l2 = pb.dma("sp", st_[b][:], s_raw[r0:r0 + 128, :], deps=[done.get(tt - 2)])
            l3 = pb.dma("pl", xt[b][:], x_in[r0:r0 + 128, :], deps=[done.get(tt - 2)])
            q1 = pb.comp("dv", (lambda v, b=b: v.tensor_scalar(
                yt[b][:], yt[b][:], scol[:, 1:2], MROUND, ALU.mult, ALU.add)),
                deps=[l1, e2])
            q2 = pb.comp("dv", (lambda v, b=b: v.tensor_scalar(
                yt[b][:], yt[b][:], -MROUND, -QMAX, ALU.add, ALU.max)), deps=[q1])
            q3 = pb.comp("dv", (lambda v, b=b: v.tensor_scalar_min(
                yt[b][:], yt[b][:], QMAX)), deps=[q2])
            sg = pb.comp("ac", (lambda s, b=b: s.activation(
                gt[:], yt[b][:], AF.Silu, scale=scol[:, 0:1])), deps=[q3, e1,
                                                                     done.get(tt - 1)])
            s2 = pb.comp("dv", (lambda v, b=b, tt=tt: v.tensor_scalar(
                st_[b][:], st_[b][:], rst[:, tt:tt + 1], None, ALU.mult)),
                deps=[l2, l_r])
            o1 = pb.comp("dv", (lambda v, b=b: v.tensor_mul(
                ot[b][:], st_[b][:], gt[:])), deps=[sg, s2])
            o2 = pb.comp("dv", (lambda v, b=b: v.tensor_add(
                ot[b][:], ot[b][:], xt[b][:])), deps=[o1, l3])
            done[tt] = pb.dma("sp", out[r0:r0 + 128, :], ot[b][:], deps=[o2])
        pb.barrier()
        pb.finalize(block)
    return dram


def _perm_ph():
    j = np.arange(DI)
    return (j % NH) * HD + j // NH


def _consts():
    import ml_dtypes
    bf = ml_dtypes.bfloat16
    ident = np.eye(128, dtype=np.float32).astype(bf)
    hh, jj = np.meshgrid(np.arange(NH), np.arange(NH * CH), indexing="ij")
    onehot_hl = (jj // CH == hh).astype(np.float32)
    ll = np.meshgrid(np.arange(CH), np.arange(NH * CH), indexing="ij")
    onehot_l = (ll[1] % CH == ll[0]).astype(np.float32).astype(bf)
    lp, sd = np.meshgrid(np.arange(CH), np.arange(128), indexing="ij")
    maskB = np.where(lp < (sd % 64), MASKNEG, 0.0).astype(np.float32).astype(bf)
    s_, l_ = np.meshgrid(np.arange(CH), np.arange(CH), indexing="ij")
    tril64 = (l_ >= s_).astype(np.float32).astype(bf)
    k_, t_ = np.meshgrid(np.arange(CH), np.arange(CH), indexing="ij")
    triu_inc = (k_ <= t_).astype(np.float32)
    k2, c2 = np.meshgrid(np.arange(NCK), np.arange(NCK), indexing="ij")
    triu_exc = (k2 < c2).astype(np.float32)
    ones_r = np.ones((1, 128), dtype=np.float32)
    return dict(ident=ident, identf=np.eye(128, dtype=np.float32),
                onehot_hl=onehot_hl, onehot_l=onehot_l, maskB=maskB,
                tril64=tril64, triu_inc=triu_inc, triu_exc32=triu_exc, ones_r=ones_r)


def make_inputs_a(core, inputs):
    perm = _perm_ph()
    W_in = inputs["L0_attn_W_in"]
    WinT = np.ascontiguousarray(np.asarray(W_in).T)
    xs_cols = WinT[:, 4096 + perm]
    bc_cols = WinT[:, 8192:8640]
    z_cols = WinT[:, perm]
    winT = np.ascontiguousarray(np.concatenate([xs_cols, bc_cols, z_cols], axis=1))
    WoutT = np.ascontiguousarray(np.asarray(inputs["L0_attn_W_out"]).T)
    woutT = np.ascontiguousarray(WoutT[perm, :])
    w1T = np.ascontiguousarray(np.asarray(inputs["L0_ffn_W1"]).T)
    w2T = np.ascontiguousarray(np.asarray(inputs["L0_ffn_W2"]).T)
    b1 = np.asarray(inputs["L0_ffn_b1"])
    b1c = np.ascontiguousarray(b1.reshape(HF // 128, 128).T)
    b2row = np.asarray(inputs["L0_ffn_b2"]).reshape(1, DM)
    wrow = np.asarray(inputs["L0_attn_out_norm_w"])[perm].reshape(1, DI)
    drow = np.tile(np.asarray(inputs["L0_attn_D"]), HD).reshape(1, DI)
    dtb = np.tile(np.asarray(inputs["L0_attn_dt_bias"]), NCK).reshape(1, NCK * NH)
    alog = np.tile(np.asarray(inputs["L0_attn_A_log"]), NCK).reshape(1, NCK * NH)
    bw = np.asarray(inputs["L0_attn_B_norm_w"])
    cw = np.asarray(inputs["L0_attn_C_norm_w"])
    bwe = np.tile(bw[0::2], NCK).reshape(1, NCK * HD)
    bwo = np.tile(bw[1::2], NCK).reshape(1, NCK * HD)
    cwe = np.tile(cw[0::2], NCK).reshape(1, NCK * HD)
    cwo = np.tile(cw[1::2], NCK).reshape(1, NCK * HD)
    im = dict(x=np.ascontiguousarray(np.asarray(inputs["L0_attn_u"])[core]),
              winT=winT, woutT=woutT, w1T=w1T, w2T=w2T, b1c=b1c, b2row=b2row,
              wrow=wrow, drow=drow, dtb_row=dtb, alog_row=alog,
              bwe_row=bwe, bwo_row=bwo, cwe_row=cwe, cwo_row=cwo)
    im.update(_consts())
    return {k: np.ascontiguousarray(v) for k, v in im.items()}


def kernel(**inputs):
    ncA = bass.Bass("TRN2", target_bir_lowering=False, debug=False)
    build_main(ncA)
    in_maps = [make_inputs_a(c, inputs) for c in range(8)]
    resA = run_bass_kernel_spmd(ncA, in_maps, core_ids=list(range(8)))

    maxv = np.concatenate([np.asarray(resA.results[c]["lmax"]).reshape(1, 1)
                           for c in range(8)], axis=1)
    ncB = bass.Bass("TRN2", target_bir_lowering=False, debug=False)
    build_final(ncB)
    in_maps_b = []
    for c in range(8):
        in_maps_b.append(dict(
            x=np.ascontiguousarray(np.asarray(inputs["L0_attn_u"])[c]),
            y2f=np.asarray(resA.results[c]["y2f"]),
            s_raw=np.asarray(resA.results[c]["s_raw"]),
            rstd_all=np.asarray(resA.results[c]["rstd_all"]),
            maxv=maxv))
    resB = run_bass_kernel_spmd(ncB, in_maps_b, core_ids=list(range(8)))
    out = np.stack([np.asarray(resB.results[c]["out"]) for c in range(8)], axis=0)
    return out.astype(np.float32)


# version 13
# speedup vs baseline: 1.1997x; 1.1877x over previous
import contextlib
import itertools
import numpy as np
import concourse.bass as bass
import concourse.mybir as mybir
from concourse.bass_utils import run_bass_kernel_spmd

F32 = mybir.dt.float32
BF16 = mybir.dt.bfloat16
AF = mybir.ActivationFunctionType
ALU = mybir.AluOpType
AX = mybir.AxisListType

T = 2048
DM = 2048
DI = 4096
NH = 64
HD = 64
NS = 128
CH = 64
NCK = T // CH
HF = 8192
NTT = T // 128
DIP_SSD = 4544
DIP = DIP_SSD + DI

MROUND = 12582912.0
MASKNEG = -28672.0
LN_EPS = 1e-5
RMS_EPS = 1e-5
QMAX = 7.0
QEPS = 1e-8
THRESH = 0.05

DEBUG_TAPS = ()
BUILD_PHASES = 99

ENGN = ("pe", "dv", "ac", "pl", "sp")
_DECN = {"pe": "tensor", "dv": "vector", "ac": "scalar", "pl": "gpsimd", "sp": "sync"}


class _Step:
    __slots__ = ("eng", "emit", "waits", "sem", "val", "isdma", "sedv")


class PB:
    NQSEM = 16

    def __init__(self, nc, E):
        self.nc = nc
        self.eng = {"pe": nc.tensor, "dv": nc.vector, "ac": nc.scalar,
                    "pl": nc.gpsimd, "sp": nc.sync}
        self.sem = {e: E(nc.semaphore(f"s_{e}")) for e in ENGN}
        self.qsem = {e: [E(nc.semaphore(f"q_{e}{i}")) for i in range(self.NQSEM)]
                     for e in ("sp", "pl", "ac")}
        self.cnt = {e: 0 for e in ENGN}
        self.qcnt = {e: 0 for e in ("sp", "pl", "ac")}
        self.steps = {e: [] for e in ENGN}
        self.last = {e: None for e in ENGN}

    def _mkwaits(self, eng, deps):
        waits = {}
        for d in deps:
            if d is None:
                continue
            if (not d.isdma) and d.eng == eng:
                continue
            key = id(d.sem)
            if key not in waits or waits[key][1] < d.val:
                waits[key] = (d.sem, d.val)
        return list(waits.values())

    def comp(self, eng, emit, deps=()):
        st = _Step()
        st.eng = eng
        st.emit = emit
        st.isdma = False
        st.waits = self._mkwaits(eng, deps)
        st.sedv = max((d.val for d in deps
                       if d is not None and (not d.isdma) and d.eng == eng),
                      default=0)
        self.cnt[eng] += 1
        st.sem = self.sem[eng]
        st.val = self.cnt[eng]
        self.steps[eng].append(st)
        self.last[eng] = st
        return st

    def dma(self, eng, out, in_, deps=(), **kw):
        assert eng in ("sp", "pl", "ac")
        st = _Step()
        st.eng = eng
        st.isdma = True
        j = self.qcnt[eng]
        self.qcnt[eng] = j + 1
        qs = self.qsem[eng][j % self.NQSEM]
        uses = j // self.NQSEM
        st.sem = qs
        st.val = 16 * (uses + 1)

        def emit(e, out=out, in_=in_, kw=kw, qs=qs, uses=uses):
            if uses:
                e.wait_ge(qs, 16 * uses)
            e.dma_start(out=out, in_=in_, **kw).then_inc(qs, 16)
        st.emit = emit
        st.waits = self._mkwaits(eng, deps)
        self.steps[eng].append(st)
        self.last[eng] = st
        return st

    def _qsem_state(self):
        ws = []
        for qe in ("sp", "pl", "ac"):
            n = self.qcnt[qe]
            for i in range(self.NQSEM):
                uses = (n - i + self.NQSEM - 1) // self.NQSEM if n > i else 0
                if uses > 0:
                    ws.append((self.qsem[qe][i], 16 * uses))
        return ws

    def barrier(self):
        lasts = [self.last[e] for e in ENGN if self.last[e] is not None]
        qws = self._qsem_state()
        steps = [self.comp(e, lambda _: None, deps=lasts) for e in ENGN]
        for st in steps:
            seen = {id(s) for (s, _) in st.waits}
            for (s, v) in qws:
                if id(s) not in seen:
                    st.waits.append((s, v))
        return steps

    def finalize(self, block):
        finals = []
        for qe in ("sp", "pl", "ac"):
            n = self.qcnt[qe]
            for i in range(self.NQSEM):
                uses = (n - i + self.NQSEM - 1) // self.NQSEM if n > i else 0
                if uses > 0:
                    finals.append((self.qsem[qe][i], 16 * uses))
        for ename in ENGN:
            steps = self.steps[ename]
            esem = self.sem[ename]

            def mk(steps=steps, esem=esem, ename=ename):
                def f(e):
                    wm = {}
                    cover = 0
                    lval = 0
                    for st in steps:
                        for (s, v) in st.waits:
                            k = id(s)
                            if wm.get(k, -1) >= v:
                                continue
                            e.wait_ge(s, v)
                            wm[k] = v
                        if (not st.isdma) and ename != "pe" \
                                and st.sedv > cover:
                            e.drain()
                            cover = lval
                        ret = st.emit(e)
                        if not st.isdma:
                            if ret is not None:
                                ret.then_inc(esem, 1)
                            else:
                                e.drain().then_inc(esem, 1)
                                cover = st.val
                            lval = st.val
                    if ename == "sp":
                        for (s, v) in finals:
                            if wm.get(id(s), -1) < v:
                                e.wait_ge(s, v)
                return f
            getattr(block, _DECN[ename])(mk())


def _dram(nc, dram, name, shape, dtype, kind):
    dram[name] = nc.dram_tensor(name, list(shape), dtype, kind=kind).ap()
    return dram[name]


def build_main(nc: bass.Bass):
    dram = {}
    din = lambda n, s, d=F32: _dram(nc, dram, n, s, d, "ExternalInput")
    dout = lambda n, s, d=F32: _dram(nc, dram, n, s, d, "ExternalOutput")
    dint = lambda n, s, d=F32: _dram(nc, dram, n, s, d, "Internal")

    x_in = din("x", (T, DM))
    winT = din("winT", (DM, DIP))
    woutT = din("woutT", (DI, DM))
    w1T = din("w1T", (DM, HF))
    w2T = din("w2T", (HF, DM))
    b1c_in = din("b1c", (128, HF // 128))
    b2row = din("b2row", (1, DM))
    wrow = din("wrow", (1, DI))
    drow_in = din("drow", (1, DI))
    dtb_row = din("dtb_row", (1, NCK * NH))
    alog_row = din("alog_row", (1, NCK * NH))
    bwe_row = din("bwe_row", (1, NCK * HD))
    bwo_row = din("bwo_row", (1, NCK * HD))
    cwe_row = din("cwe_row", (1, NCK * HD))
    cwo_row = din("cwo_row", (1, NCK * HD))
    ident_in = din("ident", (128, 128), BF16)
    identf_in = din("identf", (128, 128), F32)
    oh_hl_in = din("onehot_hl", (NH, NH * CH))
    oh_l_in = din("onehot_l", (CH, NH * CH), BF16)
    maskB_in = din("maskB", (CH, 128), BF16)
    tril_in = din("tril64", (CH, CH), BF16)
    triui_in = din("triu_inc", (CH, CH))
    triux_in = din("triu_exc32", (NCK, NCK))
    ones_in = din("ones_r", (1, 128))

    y2f_out = dout("y2f", (T, DM))
    lmax_out = dout("lmax", (1, 1))
    rstd_out = dout("rstd_all", (128, NTT))
    s_out = dout("s_raw", (T, DM), BF16)

    xs_hbm = dint("xs_hbm", (T, DI), BF16)
    xnT_hbm = dint("xnT_hbm", (128, 16 * T), BF16)
    bc_hbm = dint("bc_hbm", (T, 448), F32)
    scrA = dint("scrA", (1, 2048), F32)
    scrB = dint("scrB", (NCK, HD), F32)
    scrC = dint("scrC", (CH, CH), F32)
    scrD = dint("scrD", (128, 1), F32)
    scrE = dint("scrE", (1, 8), F32)
    z_hbm = dint("z_hbm", (T, DI), BF16)
    yd_hbm = dint("yd_hbm", (T, DI), BF16)
    yo_hbm = dint("yo_hbm", (T, DI), BF16)
    t1_hbm = dint("t1_hbm", (DM, HF), BF16)
    t2_hbm = dint("t2_hbm", (HF, DM), BF16)
    h_hbm = dint("h_hbm", (HF, T), BF16)
    ynT_hbm = dint("ynT_hbm", (NTT, 128, DI), BF16)

    ctx = contextlib.ExitStack()
    E = ctx.enter_context
    with ctx:
        pb = PB(nc, E)
        taps = []

        def tap(name, ap_, deps):
            if name in DEBUG_TAPS:
                o = dout("tap_" + name, list(ap_.shape), ap_.dtype)
                taps.append(pb.dma("sp", o, ap_, deps=deps))

        identsb = E(nc.sbuf_tensor("identsb", [128, 128], BF16))
        consts = E(nc.sbuf_tensor("consts", [128, 512], F32))
        trilb = E(nc.sbuf_tensor("trilb", [CH, CH], BF16))
        maskBb = E(nc.sbuf_tensor("maskBb", [CH, 128], BF16))
        mcol = E(nc.sbuf_tensor("mcol", [128, 64], F32))
        b2rep = E(nc.sbuf_tensor("b2rep", [128, DM], F32))
        wscr_sc = E(contextlib.ExitStack())
        WE = wscr_sc.enter_context
        wmt = {m: [WE(nc.sbuf_tensor(f"{m}t{i}", [128, 2048], F32))
                   for i in range(2)] for m in ("w1", "w2")}
        wtb = {m: [WE(nc.sbuf_tensor(f"{m}b{i}", [128, 2048], BF16))
                   for i in range(1)] for m in ("w1", "w2")}
        wqs = WE(nc.sbuf_tensor("wqs", [128, 2048], F32))
        wtr = WE(nc.sbuf_tensor("wtr", [128, 2048], F32))
        wprt = WE(nc.sbuf_tensor("wprt", [128, 520], F32))
        wrr = WE(nc.sbuf_tensor("wrr", [1, 512], F32))
        wones = WE(nc.sbuf_tensor("wones", [128, 128], F32))
        xnt_sc = E(contextlib.ExitStack())
        xnT = xnt_sc.enter_context(nc.sbuf_tensor("xnT", [128, 16 * T], BF16))
        block = E(nc.Block())

        c_id = pb.dma("sp", identsb[:], ident_in)
        c_tril = pb.dma("sp", trilb[:], tril_in)
        c_mb = pb.dma("sp", maskBb[:], maskB_in)
        c_tri = pb.dma("sp", consts[0:CH, 64:128], triui_in)
        c_trx = pb.dma("sp", consts[0:NCK, 128:160], triux_in)
        c_on = pb.dma("sp", consts[0:1, 160:288], ones_in)
        c_on32 = pb.dma("sp", consts[32:33, 160:288], ones_in)
        c_on64 = pb.dma("sp", consts[64:65, 160:288], ones_in)
        s_non = pb.comp("dv", lambda v: v.tensor_scalar(
            consts[0:1, 288:416], consts[0:1, 160:288], -1.0, None, ALU.mult),
            deps=[c_on])
        c_eps = pb.comp("dv", lambda v: v.memset(consts[:, 416:417], LN_EPS))
        EPSC = consts[:, 416:417]
        ONES = consts[0:1, 160:288]
        ONES32 = consts[32:33, 160:288]
        ONES64 = consts[64:65, 160:288]
        NEGONES = consts[0:1, 288:416]
        TRIUI = consts[0:CH, 64:128]
        TRIUX = consts[0:NCK, 128:160]
        c_onall = [c_on, c_on32, c_on64]

        WQ = "ac"
        WCFG = {
            "w1": dict(wT=w1T, thbm=t1_hbm, icol=0, acol=2,
                       tc=0, mc0=8, rc1=136, rc2=200, cc0=264, s0=0, e0=0,
                       a0=16, aj=0),
            "w2": dict(wT=w2T, thbm=t2_hbm, icol=1, acol=3,
                       tc=1, mc0=72, rc1=328, rc2=392, cc0=456, s0=8, e0=2,
                       a0=24, aj=1),
        }
        wmx = {"w1": {}, "w2": {}}
        wcons = {"w1": {}, "w2": {}}
        wst = {"w1": {}, "w2": {}}
        wdelta = {}

        WQSEL = ["ac"]

        def wmax_gen(m):
            c = WCFG[m]
            wT = c["wT"]
            nn = wT.shape[1] // 2048
            for i in range(64):
                kt, nt = divmod(i, nn)
                b = i % 2
                ld = pb.dma(WQSEL[0], wmt[m][b][:],
                            wT[128 * kt:128 * (kt + 1), 2048 * nt:2048 * (nt + 1)],
                            deps=[wmx[m].get(i - 2)])
                wmx[m][i] = pb.comp("dv", (lambda v, b=b, co=c["mc0"] + i, m=m:
                    v.reduce_max(wprt[:, co:co + 1], wmt[m][b][:], AX.X,
                                 apply_absolute_value=True)), deps=[ld])
                yield

        def wdfin_gen():
            prev_rd = None
            prev_gm = None
            for m in ("w1", "w2"):
                c = WCFG[m]
                s0 = c["s0"]
                red = pb.comp("dv", (lambda v, c=c: v.reduce_max(
                    wprt[:, c["tc"]:c["tc"] + 1], wprt[:, c["mc0"]:c["mc0"] + 64],
                    AX.X)), deps=[wmx[m][63]])
                d0 = pb.dma(WQ, scrD, wprt[:, c["tc"]:c["tc"] + 1],
                            deps=[red, prev_rd])
                d1 = pb.dma(WQ, wrr[0:1, 128:256], scrD.rearrange("p o -> o p"),
                            deps=[d0, prev_gm])
                prev_rd = d1
                gm = pb.comp("dv", (lambda v, s0=s0: v.reduce_max(
                    wrr[0:1, s0:s0 + 1], wrr[0:1, 128:256], AX.X)), deps=[d1])
                prev_gm = gm
                dl = pb.comp("dv", (lambda v, s0=s0: v.tensor_scalar(
                    wrr[0:1, s0 + 1:s0 + 2], wrr[0:1, s0:s0 + 1], THRESH, None,
                    ALU.mult)), deps=[gm])
                v2 = pb.comp("dv", (lambda v, s0=s0: v.tensor_scalar(
                    wrr[0:1, s0 + 2:s0 + 3], wrr[0:1, s0 + 1:s0 + 2], 2.0, None,
                    ALU.mult)), deps=[dl])
                r2_ = pb.comp("dv", (lambda v, s0=s0: v.reciprocal(
                    wrr[0:1, s0 + 2:s0 + 3], wrr[0:1, s0 + 2:s0 + 3])), deps=[v2])
                ngd = pb.comp("dv", (lambda v, s0=s0: v.tensor_scalar(
                    wrr[0:1, s0 + 3:s0 + 4], wrr[0:1, s0 + 1:s0 + 2], -1.0, None,
                    ALU.mult)), deps=[dl])
                d2 = pb.dma(WQ, scrE[0:1, c["e0"]:c["e0"] + 2],
                            wrr[0:1, s0 + 2:s0 + 4], deps=[r2_, ngd])
                b_iv = pb.dma(WQ, mcol[:, c["icol"]:c["icol"] + 1],
                              scrE[0:1, c["e0"]:c["e0"] + 1].broadcast_to([128, 1]),
                              deps=[d2])
                b_ng = pb.dma(WQ, mcol[:, c["icol"] + 4:c["icol"] + 5],
                              scrE[0:1, c["e0"] + 1:c["e0"] + 2].broadcast_to([128, 1]),
                              deps=[d2])
                wdelta[m] = (b_iv, b_ng)
                yield

        def wbuild_gen(m):
            c = WCFG[m]
            wT = c["wT"]
            thbm = c["thbm"]
            nn = wT.shape[1] // 2048
            jmap = {0: 62, 1: 63}
            for i in range(64):
                kt, nt = divmod(i, nn)
                b = i % 2
                ld = pb.dma(WQSEL[0], wmt[m][b][:],
                            wT[128 * kt:128 * (kt + 1), 2048 * nt:2048 * (nt + 1)],
                            deps=(wcons[m][i - 2] if i >= 2
                                  else [wmx[m][jmap[b]]]))
                u1 = pb.comp("dv", (lambda v, b=b, m=m, ic=c["icol"]:
                    v.tensor_scalar(wqs[:], wmt[m][b][:], mcol[:, ic:ic + 1],
                                    MROUND, ALU.mult, ALU.add)),
                    deps=[ld, wdelta[m][0]])
                u2 = pb.comp("dv", lambda v: v.tensor_scalar(
                    wqs[:], wqs[:], -MROUND, -1.0, ALU.add, ALU.max), deps=[u1])
                tb_ = wtb[m][0]
                u3 = pb.comp("dv", (lambda v, tb_=tb_: v.tensor_scalar_min(
                    tb_[:], wqs[:], 1.0)), deps=[u2, wst[m].get(i - 1)])
                u4 = pb.comp("dv", (lambda v, tb_=tb_, co=c["cc0"] + i:
                    v.reduce_sum(wprt[:, co:co + 1], tb_[:], AX.X,
                                 apply_absolute_value=True)), deps=[u3])
                r1 = pb.comp("ac", (lambda s, b=b, m=m, co=c["rc1"] + i,
                                    ic=c["icol"]: s.activation(
                    wtr[:], wmt[m][b][:], AF.Relu, bias=mcol[:, ic + 4:ic + 5],
                    scale=1.0, accum_out=wprt[:, co:co + 1])),
                    deps=[ld, wdelta[m][1]])
                r2 = pb.comp("ac", (lambda s, b=b, m=m, co=c["rc2"] + i,
                                    ic=c["icol"]: s.activation(
                    wtr[:], wmt[m][b][:], AF.Relu, bias=mcol[:, ic + 4:ic + 5],
                    scale=-1.0, accum_out=wprt[:, co:co + 1])), deps=[r1])
                wcons[m][i] = [u1, r2]
                dst = thbm[128 * kt:128 * (kt + 1), 2048 * nt:2048 * (nt + 1)]
                wst[m][i] = pb.dma("pl", dst, tb_[:], deps=[u3])
                yield

        def _ilv(*gens):
            gens = list(gens)
            while gens:
                for g in gens[:]:
                    try:
                        next(g)
                    except StopIteration:
                        gens.remove(g)
                        continue
                    yield

        wchain = itertools.chain(
            _ilv(wmax_gen("w1"), wmax_gen("w2")), wdfin_gen(),
            _ilv(wbuild_gen("w1"), wbuild_gen("w2")))

        _SENT = object()

        def wpump(n=1, q="ac"):
            WQSEL[0] = q
            for _ in range(n):
                if next(wchain, _SENT) is _SENT:
                    return

        def walpha_emit(psW):
            oc = pb.comp("dv", lambda v: v.memset(wones[:, :], 1.0))
            prev_pe = []
            for m in ("w1", "w2"):
                c = WCFG[m]
                aj, ic = c["aj"], c["icol"]
                lastac = wcons[m][63][1]
                s1 = pb.comp("dv", (lambda v, c=c: v.reduce_sum(
                    wprt[:, 2:3], wprt[:, c["rc1"]:c["rc1"] + 64], AX.X)),
                    deps=[lastac] + prev_pe)
                s2 = pb.comp("dv", (lambda v, c=c: v.reduce_sum(
                    wprt[:, 3:4], wprt[:, c["rc2"]:c["rc2"] + 64], AX.X)),
                    deps=[lastac, s1])
                s3 = pb.comp("dv", (lambda v, c=c: v.reduce_sum(
                    wprt[:, 4:5], wprt[:, c["cc0"]:c["cc0"] + 64], AX.X)),
                    deps=[s2])
                a12 = pb.comp("dv", lambda v: v.tensor_add(
                    wprt[:, 2:3], wprt[:, 2:3], wprt[:, 3:4]), deps=[s1, s2])
                f_r = pb.comp("pe", (lambda t, aj=aj: t.matmul(
                    psW[:, 4 * aj:4 * aj + 1], wones[:, :], wprt[:, 2:3],
                    start=True, stop=True)), deps=[a12, oc])
                f_c = pb.comp("pe", (lambda t, aj=aj: t.matmul(
                    psW[:, 4 * aj + 1:4 * aj + 2], wones[:, :], wprt[:, 4:5],
                    start=True, stop=True)), deps=[s3, oc, f_r])
                prev_pe = [f_r, f_c]
                t_ = pb.comp("dv", (lambda v, aj=aj, ic=ic: v.tensor_scalar(
                    wprt[:, 5:6], psW[:, 4 * aj + 1:4 * aj + 2],
                    mcol[:, ic + 4:ic + 5], None, ALU.mult)),
                    deps=[f_c, wdelta[m][1]])
                ms_ = pb.comp("dv", (lambda v, aj=aj: v.tensor_sub(
                    wprt[:, 5:6], psW[:, 4 * aj:4 * aj + 1], wprt[:, 5:6])),
                    deps=[t_, f_r])
                cn_ = pb.comp("dv", (lambda v, aj=aj: v.tensor_scalar(
                    wprt[:, 6:7], psW[:, 4 * aj + 1:4 * aj + 2],
                    1.0, None, ALU.max)), deps=[f_c])
                rc_ = pb.comp("dv", lambda v: v.reciprocal(
                    wprt[:, 6:7], wprt[:, 6:7]), deps=[cn_])
                al_ = pb.comp("dv", lambda v: v.tensor_mul(
                    wprt[:, 7:8], wprt[:, 5:6], wprt[:, 6:7]), deps=[ms_, rc_])
                cp_ = pb.comp("dv", (lambda v, c=c: v.tensor_copy(
                    mcol[:, c["acol"]:c["acol"] + 1], wprt[:, 7:8])),
                    deps=[al_])

        wpump(2)

        wsl_sc = E(contextlib.ExitStack())
        wsl = [wsl_sc.enter_context(nc.sbuf_tensor(f"p1w{i}", [128, 16 * 512], BF16))
               for i in range(2)]
        fbounds = []
        off = 0
        while off < DIP:
            wdt = min(512, DIP_SSD - off) if off < DIP_SSD else min(512, DIP - off)
            fbounds.append((off, wdt))
            off += wdt
        wld = [None, None]
        for fc in range(2):
            off, wdt = fbounds[fc]
            wld[fc] = pb.dma("pl", wsl[fc][:, 0:16 * wdt].rearrange(
                "p (kt f) -> p kt f", kt=16),
                winT[:, off:off + wdt].rearrange("(kt p) f -> p kt f", p=128))

        with contextlib.ExitStack() as S:
            xt = [S.enter_context(nc.sbuf_tensor(f"p0x{i}", [128, DM], F32)) for i in range(2)]
            xnb = [S.enter_context(nc.sbuf_tensor(f"p0n{i}", [128, DM], BF16)) for i in range(2)]
            cols = S.enter_context(nc.sbuf_tensor("p0c", [128, 16], F32))
            trash = wtr
            pt = [S.enter_context(nc.psum_tensor(f"p0p{i}", [128, 128], BF16)) for i in range(2)]

            done_tr = {}
            xn_done = [None, None]
            for tt in range(NTT):
                b = tt % 2
                co = 8 * (tt % 2)
                ld = pb.dma("sp", xt[b][:], x_in[128 * tt:128 * (tt + 1), :],
                            deps=[xn_done[b]])
                sq = pb.comp("ac", (lambda s, b=b, co=co: s.activation(
                    trash[:], xt[b][:], AF.Square, accum_out=cols[:, co + 1:co + 2])),
                    deps=[ld])
                sm = pb.comp("dv", (lambda v, b=b, co=co: v.reduce_sum(
                    cols[:, co:co + 1], xt[b][:], AX.X)), deps=[ld])
                nm = pb.comp("dv", (lambda v, co=co: v.tensor_scalar(
                    cols[:, co + 2:co + 3], cols[:, co:co + 1], -1.0 / DM, None, ALU.mult)),
                    deps=[sm])
                ss = pb.comp("dv", (lambda v, co=co: v.tensor_scalar(
                    cols[:, co + 1:co + 2], cols[:, co + 1:co + 2], 1.0 / DM, None, ALU.mult)),
                    deps=[sq])
                nv = pb.comp("dv", (lambda v, co=co: v.scalar_tensor_tensor(
                    cols[:, co + 3:co + 4], cols[:, co + 2:co + 3], cols[:, co + 2:co + 3],
                    cols[:, co + 1:co + 2], ALU.mult, ALU.subtract)), deps=[nm, ss])
                ln = pb.comp("ac", (lambda s, co=co: s.activation(
                    cols[:, co + 3:co + 4], cols[:, co + 3:co + 4], AF.Ln,
                    bias=EPSC, scale=-1.0)), deps=[nv, c_eps])
                rs = pb.comp("ac", (lambda s, co=co: s.activation(
                    cols[:, co + 4:co + 5], cols[:, co + 3:co + 4], AF.Exp, scale=-0.5)),
                    deps=[ln])
                xn_done[b] = pb.comp("dv", (lambda v, b=b, co=co: v.tensor_scalar(
                    xnb[b][:], xt[b][:], cols[:, co + 2:co + 3], cols[:, co + 4:co + 5],
                    ALU.add, ALU.mult)),
                    deps=[rs, nm] + ([done_tr[(tt - 2, 15)]] if tt >= 2 else [c_id]))
                for k in range(16):
                    tr = pb.comp("pe", (lambda t, b=b, k=k: t.transpose(
                        pt[k % 2][:, :], xnb[b][:, 128 * k:128 * (k + 1)], identsb[:, :])),
                        deps=[xn_done[b], done_tr.get((tt, k - 2)),
                              done_tr.get((tt - 1, 14 + k))])
                    done_tr[(tt, k)] = pb.comp("ac", (lambda s, k=k, tt=tt:
                        s.copy(xnT[:, 2048 * k + 128 * tt: 2048 * k + 128 * (tt + 1)],
                               pt[k % 2][:, :])), deps=[tr])
                wpump(1, "pl")
                wpump(1, "ac")
            p0_done = pb.barrier()
            tap("xnT", xnT[:, :], p0_done)

        if BUILD_PHASES < 1:
            pb.finalize(block)
            return dram

        with contextlib.ExitStack() as S:
            ost = [S.enter_context(nc.sbuf_tensor(f"p1o{i}", [128, 512], BF16))
                   for i in range(2)]
            obc = [S.enter_context(nc.sbuf_tensor(f"p1b{i}", [128, 448], F32))
                   for i in range(2)]
            psA = [S.enter_context(nc.psum_tensor(f"p1p{i}", [128, 512], F32))
                   for i in range(2)]
            psW = S.enter_context(nc.psum_tensor("p1wps", [128, 8], F32))
            mmd = {}
            evd = {}
            std = {}
            for fc, (off, wdt) in enumerate(fbounds):
                if fc >= 2:
                    wld[fc % 2] = pb.dma("pl", wsl[fc % 2][:, 0:16 * wdt].rearrange(
                        "p (kt f) -> p kt f", kt=16),
                        winT[:, off:off + wdt].rearrange("(kt p) f -> p kt f", p=128),
                        deps=[mmd.get((fc - 2, NTT - 1))])
                for tt in range(NTT):
                    def mm(t, fc=fc, tt=tt, wdt=wdt):
                        ps = psA[tt % 2]
                        r = None
                        for kt in range(16):
                            r = t.matmul(ps[:, 0:wdt],
                                     xnT[:, 2048 * kt + 128 * tt: 2048 * kt + 128 * (tt + 1)],
                                     wsl[fc % 2][:, wdt * kt: wdt * (kt + 1)],
                                     start=(kt == 0), stop=(kt == 15))
                        return r
                    mmd[(fc, tt)] = pb.comp("pe", mm, deps=[
                        wld[fc % 2], evd.get((fc, tt - 2)), evd.get((fc - 1, NTT - 2 + tt))]
                        + (p0_done if fc == 0 and tt == 0 else []))
                    is_bc = (off == 4096)
                    war = (std.get((fc, tt - 2)) or []) + \
                          (std.get((fc - 1, NTT - 2 + tt)) or [])
                    if not is_bc:
                        evd[(fc, tt)] = pb.comp("ac", (lambda s, tt=tt, wdt=wdt:
                            s.copy(ost[tt % 2][:, 0:wdt], psA[tt % 2][:, 0:wdt])),
                            deps=[mmd[(fc, tt)]] + war)
                        if off + wdt <= DI:
                            dst = xs_hbm[128 * tt:128 * (tt + 1), off:off + wdt]
                        else:
                            dst = z_hbm[128 * tt:128 * (tt + 1),
                                        off - DIP_SSD:off - DIP_SSD + wdt]
                        std[(fc, tt)] = [pb.dma("sp", dst, ost[tt % 2][:, 0:wdt],
                                                deps=[evd[(fc, tt)]])]
                    else:
                        ebc = pb.comp("ac", (lambda s, tt=tt:
                            s.copy(obc[tt % 2][:, 0:448], psA[tt % 2][:, 0:448])),
                            deps=[mmd[(fc, tt)]] + war)
                        evd[(fc, tt)] = ebc
                        st1 = pb.dma("sp", bc_hbm[128 * tt:128 * (tt + 1), :],
                                     obc[tt % 2][:, 0:448], deps=[ebc])
                        std[(fc, tt)] = [st1]
                    wpump(1, "sp" if (16 * fc + tt) % 3 == 0 else "ac")
            while next(wchain, _SENT) is not _SENT:
                pass
            walpha_emit(psW)
            p1_done = pb.barrier()
            sp_xnt = pb.dma("sp", xnT_hbm, xnT[:, :], deps=p1_done)
            p1_done = pb.barrier()

        wsl_sc.close()
        xnt_sc.close()
        wscr_sc.close()

        if BUILD_PHASES < 2:
            pb.finalize(block)
            return dram

        ssd_sc = E(contextlib.ExitStack())
        SE = ssd_sc.enter_context
        expacsD = SE(nc.sbuf_tensor("expacsD", [128, NCK * NH], BF16))
        dful = SE(nc.sbuf_tensor("dful", [128, DI], BF16))
        wrep = SE(nc.sbuf_tensor("wrep", [128, DI], BF16))
        ssd2_sc = E(contextlib.ExitStack())
        SE2 = ssd2_sc.enter_context
        acs = SE2(nc.sbuf_tensor("acs", [CH, NCK * NH], F32))
        nacsT = SE2(nc.sbuf_tensor("nacsT", [CH, NCK * NH], F32))
        dsstk = SE2(nc.sbuf_tensor("dsstk", [128, NCK * NH], BF16))
        dtlstk = SE2(nc.sbuf_tensor("dtlstk", [128, NCK * NH], BF16))
        dbc = SE2(nc.sbuf_tensor("dbc", [128, NCK * NH], F32))
        bstack = SE2(nc.sbuf_tensor("bstack", [128, NCK * NS], BF16))
        bnT = SE2(nc.sbuf_tensor("bnT", [128, T + 64], BF16))
        cnT = SE2(nc.sbuf_tensor("cnT", [128, T], BF16))
        oh_l = SE2(nc.sbuf_tensor("oh_l", [CH, NH * CH], BF16))
        identf = SE2(nc.sbuf_tensor("identf_sb", [CH, CH], F32))

        with contextlib.ExitStack() as S:
            tmp = [S.enter_context(nc.sbuf_tensor(f"p2t{i}", [CH, NCK * NH], F32))
                   for i in range(4)]
            sinb = S.enter_context(nc.sbuf_tensor("p2sin", [CH, NCK * HD], F32))
            cosb = S.enter_context(nc.sbuf_tensor("p2cos", [CH, NCK * HD], F32))
            ang = S.enter_context(nc.sbuf_tensor("p2ang", [CH, NCK * HD], F32))
            bcscr = S.enter_context(nc.sbuf_tensor("p2bc", [CH, NCK * NS], F32))
            bn = bstack[0:CH, :]
            cn = S.enter_context(nc.sbuf_tensor("p2cn", [CH, NCK * NS], BF16))
            msb = S.enter_context(nc.sbuf_tensor("p2mb", [CH, NCK], F32))
            msc = S.enter_context(nc.sbuf_tensor("p2mc", [CH, NCK], F32))
            wet = S.enter_context(nc.sbuf_tensor("p2we", [CH, 2048], F32))
            wot = S.enter_context(nc.sbuf_tensor("p2wo", [CH, 2048], F32))
            rows = S.enter_context(nc.sbuf_tensor("p2r", [65, 4096], F32))
            crow = S.enter_context(nc.sbuf_tensor("p2cr", [NCK, 2 * HD], F32))
            shsc = S.enter_context(nc.sbuf_tensor("p2sh", [CH, NCK * NH], BF16))
            ps4 = [S.enter_context(nc.psum_tensor(f"p2p{i}", [CH, 512], F32))
                   for i in range(4)]
            ps4b_ = [S.enter_context(nc.psum_tensor(f"p2q{i}", [CH, 512], F32))
                     for i in range(2)]
            ps4b = [ps4b_[0], ps4b_[1], ps4b_[0], ps4b_[1]]
            ps128 = S.enter_context(nc.psum_tensor("p2d", [128, 512], F32))
            pmisc = S.enter_context(nc.psum_tensor("p2s", [128, 512], F32))
            pst = pmisc[:, 0:128]
            pstb = pmisc[:, 128:192].bitcast(BF16)
            pstc = pmisc[:, 192:256].bitcast(BF16)
            ps32 = pmisc[0:NCK, 256:256 + HD]

            _SLOT = {"A": (0, 0), "B": (0, 2048), "C": (32, 0), "D": (32, 2048),
                     "E": (64, 0), "F": (64, 2048)}

            def RO(s):
                p, c = _SLOT[s]
                return rows[p:p + 1, c:c + 2048]

            def bcload(col0, wdt, dst, deps):
                return pb.dma("sp", dst.rearrange("p (c w) -> p c w", w=wdt),
                              bc_hbm[:, col0:col0 + wdt].rearrange(
                                  "(c p) w -> p c w", p=CH), deps=deps)

            c_ohl = pb.dma("sp", oh_l[:], oh_l_in, deps=p1_done)
            c_idf = pb.dma("sp", identf[:], identf_in[0:CH, 0:CH], deps=p1_done)
            l_dtb = pb.dma("sp", RO("A"), dtb_row)
            l_alog = pb.dma("sp", RO("B"), alog_row)
            l_bwe = pb.dma("sp", RO("E"), bwe_row)
            l_bwo = pb.dma("sp", RO("F"), bwo_row)

            BCH0 = bcscr[0:CH, 0:2048].rearrange("p (c w) -> p c w", w=HD)
            BCH1 = bcscr[0:CH, 2048:4096].rearrange("p (c w) -> p c w", w=HD)
            l_th = bcload(384, HD, bcscr[0:CH, 0:2048], p1_done)
            l_dtr = bcload(256, HD, bcscr[0:CH, 2048:4096], p1_done)

            mm_th = [pb.comp("pe", (lambda t, q=q: t.matmul(
                ps4[q][:, :], TRIUI, BCH0[:, 8 * q:8 * (q + 1), :],
                start=True, stop=True)), deps=[l_th, c_tri]) for q in range(4)]
            ev_ang = [pb.comp("ac", (lambda s, q=q: s.copy(
                ang[:, 512 * q:512 * (q + 1)], ps4[q][:, :])), deps=[mm_th[q]])
                for q in range(4)]
            d_sums0 = pb.dma("pl", scrA, ang[63:64, :], deps=ev_ang)
            d_sums = pb.dma("pl", crow[0:NCK, 0:HD],
                            scrA.rearrange("o (c w) -> (o c) w", w=HD),
                            deps=[d_sums0])
            mm_cs = pb.comp("pe", lambda t: t.matmul(
                ps32[:, :], TRIUX, crow[0:NCK, 0:HD], start=True, stop=True),
                deps=[d_sums, c_trx])
            ev_cs = pb.comp("dv", lambda v: v.tensor_copy(crow[0:NCK, HD:2 * HD], ps32[:, :]),
                            deps=[mm_cs])
            d_flat0 = pb.dma("pl", scrB, crow[0:NCK, HD:2 * HD], deps=[ev_cs])
            d_flat = pb.dma("pl", RO("D"), scrB.rearrange("c w -> () (c w)"),
                            deps=[d_flat0])
            mm_cb = [pb.comp("pe", (lambda t, q=q: t.matmul(
                ps4[q][:, :], ONES32[:, 0:CH], RO("D")[:, 512 * q:512 * (q + 1)],
                start=True, stop=True)), deps=[d_flat, ev_ang[q], c_on32]) for q in range(4)]
            ad_ang = [pb.comp("dv", (lambda v, q=q: v.tensor_add(
                ang[:, 512 * q:512 * (q + 1)], ang[:, 512 * q:512 * (q + 1)],
                ps4[q][:, :])), deps=[mm_cb[q]]) for q in range(4)]

            sc_done = []
            for dst, shift in ((sinb, 0.0), (cosb, np.pi / 2)):
                s1 = pb.comp("dv", (lambda v, sh=shift: v.tensor_scalar(
                    tmp[0][:, 0:2048], ang[:], sh, 1.0 / (2 * np.pi), ALU.add, ALU.mult)),
                    deps=ad_ang + sc_done)
                s2 = pb.comp("dv", lambda v: v.tensor_scalar(
                    tmp[0][:, 0:2048], tmp[0][:, 0:2048], MROUND, -MROUND, ALU.add, ALU.add),
                    deps=[s1])
                s3 = pb.comp("dv", (lambda v, sh=shift: v.tensor_scalar(
                    tmp[1][:, 0:2048], ang[:], sh, None, ALU.add)), deps=[s2])
                s4 = pb.comp("dv", (lambda v, d=dst: v.scalar_tensor_tensor(
                    d[:], tmp[0][:, 0:2048], -2 * np.pi, tmp[1][:, 0:2048],
                    ALU.mult, ALU.add)), deps=[s3])
                s4b = pb.comp("dv", (lambda v, d=dst: v.tensor_scalar(
                    d[:], d[:], -3.1415925, 3.1415925, ALU.max, ALU.min)), deps=[s4])
                s5 = pb.comp("ac", (lambda s, d=dst: s.activation(d[:], d[:], AF.Sin)),
                             deps=[s4b])
                sc_done.append(s5)

            e_al = pb.comp("ac", lambda s: s.activation(RO("B"), RO("B"), AF.Exp),
                           deps=[l_alog])
            mm_dtb = []
            v_v = []
            for q in range(4):
                mm_dtb.append(pb.comp("pe", (lambda t, q=q: t.matmul(
                    ps4b[q][:, :], ONES[:, 0:CH], RO("A")[:, 512 * q:512 * (q + 1)],
                    start=True, stop=True)),
                    deps=[l_dtb, c_on, (v_v[q - 2] if q >= 2 else None)]))
                v_v.append(pb.comp("dv", (lambda v, q=q: v.tensor_add(
                    tmp[0][:, 512 * q:512 * (q + 1)],
                    BCH1[:, 8 * q:8 * (q + 1), :].rearrange("p c w -> p (c w)"),
                    ps4b[q][:, :])), deps=[mm_dtb[q], l_dtr]))
            a_ab = pb.comp("ac", lambda s: s.activation(
                tmp[1][:, 0:2048], tmp[0][:, 0:2048], AF.Abs, scale=-1.0),
                deps=v_v + sc_done)
            a_ex = pb.comp("ac", lambda s: s.activation(
                tmp[1][:, 0:2048], tmp[1][:, 0:2048], AF.Exp, scale=-1.0), deps=[a_ab])
            a_ln = pb.comp("ac", lambda s: s.activation(
                tmp[1][:, 0:2048], tmp[1][:, 0:2048], AF.Ln, bias=1.0), deps=[a_ex])
            v_dt = pb.comp("dv", lambda v: v.scalar_tensor_tensor(
                tmp[0][:, 0:2048], tmp[0][:, 0:2048], 0.0, tmp[1][:, 0:2048],
                ALU.max, ALU.add), deps=[a_ln])
            l_lam = bcload(320, HD, bcscr[0:CH, 0:2048], mm_th)
            a_lam = pb.comp("ac", lambda s: s.activation(
                tmp[2][:, 0:2048], bcscr[0:CH, 0:2048],
                AF.Sigmoid), deps=[l_lam, a_ln])
            mm_na = []
            v_A = []
            for q in range(4):
                mm_na.append(pb.comp("pe", (lambda t, q=q: t.matmul(
                    ps4b[q][:, :], NEGONES[:, 0:CH], RO("B")[:, 512 * q:512 * (q + 1)],
                    start=True, stop=True)),
                    deps=[e_al, s_non, v_v[q], v_v[(q + 2) % 4],
                          (v_A[q - 2] if q >= 2 else None)]))
                v_A.append(pb.comp("dv", (lambda v, q=q: v.tensor_mul(
                    tmp[3][:, 512 * q:512 * (q + 1)], tmp[0][:, 512 * q:512 * (q + 1)],
                    ps4b[q][:, :])), deps=[mm_na[q], v_dt]))
            v_d1 = pb.comp("dv", lambda v: v.tensor_mul(
                tmp[1][:, 0:2048], tmp[0][:, 0:2048], tmp[2][:, 0:2048]),
                deps=[v_dt, a_lam])
            v_d1b = pb.comp("dv", lambda v: v.tensor_copy(dtlstk[0:CH, :], tmp[1][:, 0:2048]),
                            deps=[v_d1])
            v_d2 = pb.comp("dv", lambda v: v.tensor_sub(
                tmp[1][:, 0:2048], tmp[0][:, 0:2048], tmp[1][:, 0:2048]), deps=[v_d1b])
            v_d2b = pb.comp("dv", lambda v: v.tensor_copy(shsc[0:CH, :], tmp[1][:, 0:2048]),
                            deps=[v_d2])
            d_shA = pb.dma("pl", dtlstk[65:128, :], shsc[0:CH - 1, :], deps=[v_d2b])
            d_shB = pb.dma("pl",
                           dtlstk[64:65, :].rearrange("o (c w) -> o c w", w=NH)[:, 1:NCK, :],
                           shsc[CH - 1:CH, :].rearrange("o (c w) -> o c w", w=NH)[:, 0:NCK - 1, :],
                           deps=[v_d2b])
            z_sh = pb.comp("dv", lambda v: v.memset(dtlstk[64:65, 0:NH], 0.0),
                           deps=[v_d2b])

            mm_ac = [pb.comp("pe", (lambda t, q=q: t.matmul(
                ps4[q][:, :], TRIUI, tmp[3][:, 512 * q:512 * (q + 1)],
                start=True, stop=True)), deps=[v_A[q], ad_ang[q]]) for q in range(4)]
            ev_ac = [pb.comp("ac", (lambda s, q=q: s.copy(
                acs[:, 512 * q:512 * (q + 1)], ps4[q][:, :])), deps=[mm_ac[q]])
                for q in range(4)]
            ev_ea = [pb.comp("ac", (lambda s, q=q: s.activation(
                expacsD[0:CH, 512 * q:512 * (q + 1)], ps4[q][:, :], AF.Exp)),
                deps=[mm_ac[q]]) for q in range(4)]
            d_ea = pb.dma("pl", expacsD[64:128, :], expacsD[0:CH, :], deps=ev_ea)
            tr_ac = {}
            ev_na = {}
            for c in range(NCK):
                tr_ac[c] = pb.comp("pe", (lambda t, c=c: t.transpose(
                    pst[0:CH, 0:CH], acs[:, NH * c:NH * (c + 1)], identf[:, :])),
                    deps=[ev_ac[c // 8], ev_na.get(c - 1), c_idf])
                ev_na[c] = pb.comp("dv", (lambda v, c=c: v.tensor_scalar(
                    nacsT[:, NH * c:NH * (c + 1)], pst[0:CH, 0:CH], -1.0, None, ALU.mult)),
                    deps=[tr_ac[c]])
            d_end = pb.dma("pl", RO("A"), acs[63:64, :], deps=ev_ac + [mm_dtb[3]])
            e_dec = pb.comp("ac", lambda s: s.activation(RO("B"), RO("A"), AF.Exp),
                            deps=[d_end, a_lam, mm_na[3]])
            mm_en = [pb.comp("pe", (lambda t, q=q: t.matmul(
                ps4[q][:, :], ONES[:, 0:CH], RO("A")[:, 512 * q:512 * (q + 1)],
                start=True, stop=True)), deps=[d_end, ev_ea[q], ev_ac[q]])
                for q in range(4)]
            v_ds = [pb.comp("dv", (lambda v, q=q: v.tensor_sub(
                tmp[1][:, 512 * q:512 * (q + 1)], ps4[q][:, :],
                acs[:, 512 * q:512 * (q + 1)])), deps=[mm_en[q], v_d2b]) for q in range(4)]
            a_ds = pb.comp("ac", lambda s: s.activation(
                dsstk[0:CH, :], tmp[1][:, 0:2048], AF.Exp), deps=v_ds)
            d_ds2 = pb.dma("pl", dsstk[64:128, :], dsstk[0:CH, :], deps=[a_ds])
            mm_db = {}
            ev_db = {}
            for q in range(4):
                mm_db[q] = pb.comp("pe", (lambda t, q=q: t.matmul(
                    ps128[:, :], ONES[0:1, 0:128], RO("B")[:, 512 * q:512 * (q + 1)],
                    start=True, stop=True)), deps=[e_dec, ev_db.get(q - 1)])
                ev_db[q] = pb.comp("dv", (lambda v, q=q: v.tensor_copy(
                    dbc[:, 512 * q:512 * (q + 1)], ps128[:, :])), deps=[mm_db[q]])

            BCF = bcscr[0:CH, :].rearrange("p (c w) -> p c w", w=NS)

            def rope_build(ms, werow_ap, worow_ap, dst, extra):
                sqd = pb.comp("ac", (lambda s: s.activation(
                    tmp[1][:, 0:2048].rearrange("p (c w) -> p c w", w=HD),
                    BCF[:, :, 0::2], AF.Square)), deps=extra)
                sqd2 = pb.comp("ac", (lambda s: s.activation(
                    tmp[3][:, 0:2048].rearrange("p (c w) -> p c w", w=HD),
                    BCF[:, :, 1::2], AF.Square)), deps=[sqd])
                ad = pb.comp("dv", lambda v: v.tensor_add(
                    tmp[1][:, 0:2048], tmp[1][:, 0:2048], tmp[3][:, 0:2048]), deps=[sqd2])
                rd = pb.comp("dv", (lambda v, m=ms: v.tensor_reduce(
                    m[:, :].rearrange("p c -> p c ()"),
                    tmp[1][:, 0:2048].rearrange("p (c w) -> p c w", w=HD),
                    AX.X, ALU.add)), deps=[ad])
                r1 = pb.comp("ac", (lambda s, m=ms: s.activation(
                    m[:, :], m[:, :], AF.Ln, bias=EPSC[0:CH, :], scale=1.0 / 128)),
                    deps=[rd, c_eps])
                r2 = pb.comp("ac", (lambda s, m=ms: s.activation(
                    m[:, :], m[:, :], AF.Exp, scale=-0.5)), deps=[r1])
                mwA = []
                mwB = []
                ew = []
                ow = []
                for q in range(4):
                    mwA.append(pb.comp("pe", (lambda t, q=q, w=werow_ap: t.matmul(
                        ps4[q][:, :], ONES64[:, 0:CH], w[:, 512 * q:512 * (q + 1)],
                        start=True, stop=True)), deps=[v_ds[q], r2, c_on64] + extra))
                    mwB.append(pb.comp("pe", (lambda t, q=q, w=worow_ap: t.matmul(
                        ps4b[q][:, :], ONES64[:, 0:CH], w[:, 512 * q:512 * (q + 1)],
                        start=True, stop=True)),
                        deps=[v_A[q], v_A[(q + 2) % 4], r2, c_on64,
                              (ow[q - 2] if q >= 2 else None)] + extra))
                    ew.append(pb.comp("dv", (lambda v, q=q: v.tensor_copy(
                        wet[:, 512 * q:512 * (q + 1)], ps4[q][:, :])), deps=[mwA[q]]))
                    ow.append(pb.comp("dv", (lambda v, q=q: v.tensor_copy(
                        wot[:, 512 * q:512 * (q + 1)], ps4b[q][:, :])), deps=[mwB[q]]))
                ms3 = ms[:, :].rearrange("p c -> p c ()").broadcast_to([CH, NCK, HD])
                x1 = pb.comp("dv", (lambda v, m3=ms3: v.tensor_tensor(
                    tmp[1][:, 0:2048].rearrange("p (c w) -> p c w", w=HD),
                    BCF[:, :, 0::2], m3, ALU.mult)), deps=[r2, rd])
                x1w = pb.comp("dv", lambda v: v.tensor_mul(
                    tmp[1][:, 0:2048], tmp[1][:, 0:2048], wet[:, :]), deps=[x1] + ew)
                x2 = pb.comp("dv", (lambda v, m3=ms3: v.tensor_tensor(
                    tmp[3][:, 0:2048].rearrange("p (c w) -> p c w", w=HD),
                    BCF[:, :, 1::2], m3, ALU.mult)), deps=[r2, ad])
                x2w = pb.comp("dv", lambda v: v.tensor_mul(
                    tmp[3][:, 0:2048], tmp[3][:, 0:2048], wot[:, :]), deps=[x2] + ow)
                dst3 = dst[0:CH, :].rearrange("p (c w) -> p c w", w=NS)
                t1 = pb.comp("dv", lambda v: v.tensor_mul(
                    tmp[0][:, 0:2048], cosb[:], tmp[1][:, 0:2048]),
                    deps=[x1w] + sc_done + [v_dt, v_d1])
                tA = pb.comp("dv", lambda v: v.tensor_mul(
                    tmp[2][:, 0:2048], sinb[:], tmp[3][:, 0:2048]),
                    deps=[x2w] + sc_done + [a_lam, v_d1])
                evn = pb.comp("dv", (lambda v, d=dst3: v.tensor_tensor(
                    d[:, :, 0:NS:2], tmp[0][:, 0:2048].rearrange("p (c w) -> p c w", w=HD),
                    tmp[2][:, 0:2048].rearrange("p (c w) -> p c w", w=HD), ALU.subtract)),
                    deps=[t1, tA])
                t3 = pb.comp("dv", lambda v: v.tensor_mul(
                    tmp[0][:, 0:2048], sinb[:], tmp[1][:, 0:2048]), deps=[evn])
                t4 = pb.comp("dv", lambda v: v.tensor_mul(
                    tmp[2][:, 0:2048], cosb[:], tmp[3][:, 0:2048]), deps=[evn])
                odd = pb.comp("dv", (lambda v, d=dst3: v.tensor_tensor(
                    d[:, :, 1:NS:2], tmp[0][:, 0:2048].rearrange("p (c w) -> p c w", w=HD),
                    tmp[2][:, 0:2048].rearrange("p (c w) -> p c w", w=HD), ALU.add)),
                    deps=[t3, t4])
                return odd

            l_B = bcload(0, NS, bcscr[0:CH, :],
                         [a_lam, v_ds[3], mm_ac[3], v_v[3], mm_th[3]])
            bn_done = rope_build(msb, RO("E"), RO("F"), bn,
                                 [l_B, v_ds[3], a_ds, z_sh, d_shA, d_shB, l_bwe,
                                  l_bwo, v_d2b, ev_na[NCK - 1]])
            l_cwe = pb.dma("sp", RO("E"), cwe_row, deps=[bn_done])
            l_cwo = pb.dma("sp", RO("F"), cwo_row, deps=[bn_done])
            l_C = bcload(128, NS, bcscr[0:CH, :], [bn_done])
            cn_done = rope_build(msc, RO("E"), RO("F"), cn,
                                 [l_C, bn_done, l_cwe, l_cwo])

            d_bs1 = pb.dma("pl", bstack[65:128, :], bn[0:CH - 1, :], deps=[bn_done])
            d_bs2 = pb.dma("pl",
                           bstack[64:65, :].rearrange("o (c w) -> o c w", w=NS)[:, 1:NCK, :],
                           bn[CH - 1:CH, :].rearrange("o (c w) -> o c w", w=NS)[:, 0:NCK - 1, :],
                           deps=[bn_done])
            z_bs = pb.comp("dv", lambda v: v.memset(bstack[64:65, 0:NS], 0.0),
                           deps=[bn_done])
            z_bnT = pb.comp("dv", lambda v: v.memset(bnT[:, 0:1], 0.0), deps=[c_id])
            tr_b = {}
            for c in range(NCK):
                trb = pb.comp("pe", (lambda t, c=c: t.transpose(
                    pstb[:, 0:CH], bn[0:CH, NS * c:NS * (c + 1)], identsb[0:CH, 0:CH])),
                    deps=[bn_done, tr_b.get(c - 1)])
                tr_b[c] = pb.comp("ac", (lambda s, c=c: s.copy(
                    bnT[:, 1 + CH * c:1 + CH * (c + 1)], pstb[:, 0:CH])), deps=[trb])
            tr_c = {}
            for c in range(NCK):
                trc = pb.comp("pe", (lambda t, c=c: t.transpose(
                    pstc[:, 0:CH], cn[0:CH, NS * c:NS * (c + 1)], identsb[0:CH, 0:CH])),
                    deps=[cn_done, tr_c.get(c - 1), tr_b[NCK - 1]])
                tr_c[c] = pb.comp("ac", (lambda s, c=c: s.copy(
                    cnT[:, CH * c:CH * (c + 1)], pstc[:, 0:CH])), deps=[trc])

            l_dr = pb.dma("sp", RO("D"), drow_in[:, 0:2048], deps=[d_flat, mm_cb[3]])
            l_dr2 = pb.dma("sp", RO("C"), drow_in[:, 2048:4096],
                           deps=[e_dec, mm_na[3], mm_db[3]])
            mm_df = {}
            ev_df = {}
            for q in range(8):
                src = (RO("D") if q < 4 else RO("C"))
                mm_df[q] = pb.comp("pe", (lambda t, q=q, src=src: t.matmul(
                    ps128[:, :], ONES32[:, 0:128], src[:, 512 * (q % 4):512 * (q % 4 + 1)],
                    start=True, stop=True)),
                    deps=[l_dr, l_dr2, ev_db[3], ev_df.get(q - 1), c_on32])
                ev_df[q] = pb.comp("dv", (lambda v, q=q: v.tensor_copy(
                    dful[:, 512 * q:512 * (q + 1)], ps128[:, :])), deps=[mm_df[q]])
            l_b2 = pb.dma("sp", RO("A"), b2row, deps=[e_dec, mm_en[3], mm_db[3]])
            l_wr = pb.dma("sp", RO("B"), wrow[:, 0:2048], deps=[mm_db[3]])
            l_wr2 = pb.dma("sp", RO("E"), wrow[:, 2048:4096], deps=[cn_done])
            mm_b2 = {}
            ev_b2 = {}
            for q in range(4):
                mm_b2[q] = pb.comp("pe", (lambda t, q=q: t.matmul(
                    ps128[:, :], ONES[0:1, 0:128], RO("A")[:, 512 * q:512 * (q + 1)],
                    start=True, stop=True)), deps=[l_b2, ev_df[7], mm_b2.get(q - 1),
                                                   ev_b2.get(q - 1)])
                ev_b2[q] = pb.comp("dv", (lambda v, q=q: v.tensor_copy(
                    b2rep[:, 512 * q:512 * (q + 1)], ps128[:, :])), deps=[mm_b2[q]])
            mm_wr = {}
            ev_wr = {}
            for q in range(8):
                src = (RO("B") if q < 4 else RO("E"))
                lhs1 = (ONES if q < 4 else ONES64)
                mm_wr[q] = pb.comp("pe", (lambda t, q=q, src=src, lhs1=lhs1: t.matmul(
                    ps128[:, :], lhs1[:, 0:128], src[:, 512 * (q % 4):512 * (q % 4 + 1)],
                    start=True, stop=True)),
                    deps=[l_wr, l_wr2, ev_b2[3], ev_wr.get(q - 1), c_on64])
                ev_wr[q] = pb.comp("dv", (lambda v, q=q: v.tensor_copy(
                    wrep[:, 512 * q:512 * (q + 1)], ps128[:, :])), deps=[mm_wr[q]])

            p2a_done = pb.barrier()
            for nm, ap_ in (("acs", acs[:, :]), ("ang", ang[:, :]), ("bn", bn[:, :]),
                            ("cn", cn[:, :]),
                            ("dsstk", dsstk[:, :]), ("dtlstk", dtlstk[:, :]),
                            ("dbc", dbc[:, :]),
                            ("bstack", bstack[:, :]), ("bnT", bnT[:, 0:T + 1]),
                            ("cnT", cnT[:, :]),
                            ("expacsD", expacsD[:, :]), ("nacsT", nacsT[:, :]),
                            ("dful", dful[:, :]),
                            ("wrep", wrep[:, :]), ("b2rep", b2rep[:, :])):
                tap(nm, ap_, p2a_done)
            tap("dt", tmp[0][:, :], p2a_done)

        if BUILD_PHASES < 3:
            pb.finalize(block)
            return dram

        with contextlib.ExitStack() as S:
            xsal = [S.enter_context(nc.sbuf_tensor(f"sxa{i}", [128, DI], BF16))
                    for i in range(2)]
            statebf = S.enter_context(nc.sbuf_tensor("sstbf", [128, DI], BF16))
            mpre = S.enter_context(nc.sbuf_tensor("smp", [128, 2 * 512], BF16))
            statbA = [S.enter_context(nc.sbuf_tensor(f"sstb{i}", [128, DI], BF16))
                      for i in range(2)]
            gmst = S.enter_context(nc.sbuf_tensor("sgm", [128, 2 * CH], BF16))
            gmtmp = S.enter_context(nc.sbuf_tensor("sgt", [CH, CH], BF16))
            lhsA = S.enter_context(nc.sbuf_tensor("sla", [128, 128], F32))
            rhsA = S.enter_context(nc.sbuf_tensor("sra", [128, NH * CH], F32))
            state = S.enter_context(nc.sbuf_tensor("sste", [128, DI], F32))
            ych = [S.enter_context(nc.sbuf_tensor(f"syc{i}", [CH, 2 * DI], BF16))
                   for i in range(2)]
            ps_d = [S.enter_context(nc.psum_tensor(f"qd{i}", [128, 512], F32))
                    for i in range(2)]
            ps_g = S.enter_context(nc.psum_tensor("qg", [CH, 128], F32))
            ps_yd = [S.enter_context(nc.psum_tensor(f"qyd{i}", [CH, 512], F32))
                     for i in range(2)]
            ps_yo = S.enter_context(nc.psum_tensor("qyo", [CH, 512], F32))
            ps_st = [S.enter_context(nc.psum_tensor(f"qst{i}", [128, 512], F32))
                     for i in range(2)]

            l_oh = pb.dma("sp", rhsA[0:NH, :], oh_hl_in, deps=p2a_done)
            onesm1 = pb.comp("dv", lambda v: v.memset(lhsA[64:65, :], -1.0),
                             deps=p2a_done)
            zro_st = [pb.comp("dv", (lambda v, q=q: v.memset(
                state[:, 512 * q:512 * (q + 1)], 0.0)), deps=p2a_done) for q in range(8)]
            zro_st += [pb.comp("dv", lambda v: v.memset(statebf[:, :], 0.0),
                               deps=p2a_done)]

            P = {}

            def prep_head(c):
                cp = c % 2
                buf = c % 2
                tok0 = CH * c
                lda = pb.dma("sp", xsal[buf][0:CH, :], xs_hbm[tok0:tok0 + CH, :],
                             deps=[P.get(("smm", c - 2, 7))])
                if c > 0:
                    lds = pb.dma("sp", xsal[buf][64:128, :],
                                 xs_hbm[tok0 - 1:tok0 + CH - 1, :],
                                 deps=[P.get(("smm", c - 2, 7))])
                    shz = None
                else:
                    lds = pb.dma("sp", xsal[0][65:128, :], xs_hbm[0:CH - 1, :],
                                 deps=p2a_done)
                    shz = pb.comp("dv", lambda v: v.memset(xsal[0][64:65, :], 0.0),
                                  deps=p2a_done)
                dtls = dtlstk[:, NH * c:NH * (c + 1)].rearrange(
                    "p w -> p () w").broadcast_to([128, HD, NH])
                P[("xd", c)] = pb.comp("dv", (lambda v, b=buf, d=dtls: v.tensor_tensor(
                    xsal[b][:].rearrange("p (a w) -> p a w", w=NH),
                    xsal[b][:].rearrange("p (a w) -> p a w", w=NH), d, ALU.mult)),
                    deps=[lda, lds, shz])
                g_mm1 = pb.comp("pe", (lambda t, tok0=tok0: t.matmul(
                    ps_g[:, 0:CH], bnT[:, 1 + tok0:1 + tok0 + CH],
                    cnT[:, tok0:tok0 + CH], start=True, stop=False)),
                    deps=[P.get(("gm1", c - 1)), P.get(("gm2t", c - 1)), l_oh, onesm1])
                g_mm2 = pb.comp("pe", (lambda t, tok0=tok0: t.matmul(
                    ps_g[:, CH:128], bnT[:, tok0:tok0 + CH],
                    cnT[:, tok0:tok0 + CH], start=False, stop=True)), deps=[g_mm1])
                gm1 = pb.comp("dv", (lambda v, cp=cp: v.tensor_tensor(
                    gmst[0:CH, CH * cp:CH * (cp + 1)], ps_g[:, 0:CH], trilb[:, :],
                    ALU.mult)), deps=[g_mm2, c_tril, P.get(("stb", c - 2, 7))])
                gm2t = pb.comp("dv", lambda v: v.tensor_tensor(
                    gmtmp[:, :], ps_g[:, CH:128], trilb[:, :], ALU.mult),
                    deps=[g_mm2, P.get(("gmdn", c - 1))])
                gmdn = pb.dma("pl", gmst[64:128, CH * cp:CH * (cp + 1)], gmtmp[:, :],
                              deps=[gm2t, P.get(("stb", c - 2, 7))])
                P[("gm1", c)], P[("gm2t", c)], P[("gmdn", c)] = gm1, gm2t, gmdn
                la1 = pb.comp("dv", (lambda v, c=c: v.tensor_copy(
                    lhsA[0:CH, 0:CH], nacsT[:, NH * c:NH * (c + 1)])),
                    deps=[P.get(("dmm", c - 1, 7)), onesm1])
                la2 = pb.comp("dv", (lambda v, c=c: v.tensor_copy(
                    lhsA[0:CH, CH:128], nacsT[:, NH * c:NH * (c + 1)])), deps=[la1])
                flt0 = pb.dma("pl", scrC, nacsT[:, NH * c:NH * (c + 1)],
                              deps=[P.get(("flt", c - 1))])
                flt = pb.dma("pl", rhsA[64:65, :],
                             scrC.rearrange("h s -> () (h s)"),
                             deps=[flt0, P.get(("dmm", c - 1, 7))])
                P[("flt", c)] = flt
                P[("la", c)] = la2

            def prep_g(c, g):
                cp = c % 2
                pd = ps_d[g % 2]
                war_mp = P.get(("mp", c, g - 2)) or P.get(("mp", c - 1, g + 6))
                dmm = pb.comp("pe", (lambda t, g=g, pd=pd: (
                    t.matmul(pd[:, :], lhsA[0:65, :],
                             rhsA[0:65, 512 * g:512 * (g + 1)], start=True, stop=False),
                    t.matmul(pd[:, :], maskBb[:, :],
                             oh_l[:, 512 * g:512 * (g + 1)], start=False, stop=True))[-1]),
                    deps=[P[("la", c)], P[("flt", c)], l_oh, c_mb, c_ohl, war_mp])
                P[("dmm", c, g)] = dmm
                war_stb = P.get(("stb", c, g - 2)) or P.get(("stb", c - 1, g + 6))
                mp = pb.comp("ac", (lambda s, g=g, pd=pd: s.activation(
                    mpre[:, 512 * (g % 2):512 * (g % 2 + 1)], pd[:, :], AF.Exp)),
                    deps=[dmm, war_stb])
                P[("mp", c, g)] = mp

                def stbf(v, g=g, cp=cp):
                    r = None
                    for i in range(8):
                        r = v.tensor_tensor(
                            statbA[cp][:, 512 * g + 64 * i:512 * g + 64 * (i + 1)],
                            mpre[:, 512 * (g % 2) + 64 * i:
                                 512 * (g % 2) + 64 * (i + 1)],
                            gmst[:, CH * cp:CH * (cp + 1)], ALU.mult)
                    return r
                P[("stb", c, g)] = pb.comp("dv", stbf, deps=[
                    mp, P[("gmdn", c)], P[("gm1", c)], P.get(("ym", c - 2, g))])

            def comp_g(c, g):
                buf = c % 2
                cp = c % 2
                tok0 = CH * c

                def ydmm(t, g=g, b=buf, cp=cp):
                    r = None
                    for i in range(8):
                        h = 8 * g + i
                        mv = xsal[b][:].rearrange("p (a w) -> p a w", w=NH)[:, :, h]
                        r = t.matmul(ps_yd[g % 2][:, 64 * i:64 * (i + 1)],
                                     statbA[cp][:, 512 * g + 64 * i:
                                                512 * g + 64 * (i + 1)], mv,
                                     start=(i == 0), stop=(i == 7))
                    return r
                ym = pb.comp("pe", ydmm, deps=[
                    P[("stb", c, g)], P[("xd", c)],
                    P.get(("yde", c, g - 2)) or P.get(("yde", c - 1, g + 6))])
                P[("ym", c, g)] = ym

                def yomm(t, g=g, tok0=tok0):
                    r = None
                    for i in range(8):
                        h = 8 * g + i
                        mv = statebf[:, :].rearrange("p (a w) -> p a w", w=NH)[:, :, h]
                        r = t.matmul(ps_yo[:, 64 * i:64 * (i + 1)],
                                     cnT[:, tok0:tok0 + CH], mv,
                                     start=(i == 0), stop=(i == 7))
                    return r
                yo = pb.comp("pe", yomm, deps=[
                    ym, P.get(("yoe", c, g - 1)) or P.get(("yoe", c - 1, 7)),
                    P.get(("sbf", c - 1))] + ([] if c > 0 else zro_st))
                P[("yo", c, g)] = yo

                war_ych = P.get(("ychst", buf)) or []
                yde = pb.comp("ac", (lambda s, g=g, b=buf: s.copy(
                    ych[b][0:CH, 0:DI].rearrange(
                        "p (a w) -> p a w", w=NH)[:, :, 8 * g:8 * (g + 1)],
                    ps_yd[g % 2][:, :].rearrange("p (i w) -> p w i", i=8))),
                    deps=[ym] + war_ych)
                P[("yde", c, g)] = yde
                yoe = pb.comp("ac", (lambda s, g=g, b=buf: s.copy(
                    ych[b][0:CH, DI:2 * DI].rearrange(
                        "p (a w) -> p a w", w=NH)[:, :, 8 * g:8 * (g + 1)],
                    ps_yo[:, :].rearrange("p (i w) -> p w i", i=8))),
                    deps=[yo] + war_ych)
                P[("yoe", c, g)] = yoe

            def comp_tail(c):
                buf = c % 2
                dss = dsstk[:, NH * c:NH * (c + 1)].rearrange(
                    "p w -> p () w").broadcast_to([128, HD, NH])
                dx = pb.comp("dv", (lambda v, b=buf, d=dss: v.tensor_tensor(
                    xsal[b][:].rearrange("p (a w) -> p a w", w=NH),
                    xsal[b][:].rearrange("p (a w) -> p a w", w=NH), d, ALU.mult)),
                    deps=[P[("ym", c, 7)], P[("xd", c)]])
                P[("dx", c)] = dx
                for g in range(8):
                    pq = ps_st[g % 2]
                    war_upd = P.get(("upd", c, g - 2)) or P.get(("upd", c - 1, g + 6))
                    smm = pb.comp("pe", (lambda t, g=g, pq=pq, b=buf, c=c: t.matmul(
                        pq[:, :], bstack[:, NS * c:NS * (c + 1)],
                        xsal[b][:].rearrange("p (a w) -> p a w", w=NH)[:, :, 8 * g:8 * (g + 1)],
                        start=True, stop=True)),
                        deps=[dx, war_upd, P[("yo", c, 7)]])
                    P[("smm", c, g)] = smm

                    def updf(v, g=g, pq=pq, c=c):
                        r = None
                        for i in range(8):
                            h = 8 * g + i
                            sl = state[:, :].rearrange("p (a w) -> p a w", w=NH)[:, :, h]
                            pv = pq[:, :].rearrange("p (a w) -> p a w", w=8)[:, :, i]
                            r = v.scalar_tensor_tensor(
                                sl, sl, dbc[:, NH * c + h:NH * c + h + 1], pv,
                                ALU.mult, ALU.add)
                        return r
                    P[("upd", c, g)] = pb.comp("dv", updf, deps=[smm]
                                               + ([] if c > 0 else zro_st))
                sbf = pb.comp("dv", lambda v: v.tensor_copy(statebf[:, :], state[:, :]),
                              deps=[P[("upd", c, 7)], P[("yo", c, 7)]])
                P[("sbf", c)] = sbf
                evs = [P[("yde", c, g)] for g in range(8)] + \
                      [P[("yoe", c, g)] for g in range(8)]
                tok0 = CH * c
                sd = pb.dma("sp", yd_hbm[tok0:tok0 + CH, :], ych[buf][0:CH, 0:DI],
                            deps=evs)
                so = pb.dma("sp", yo_hbm[tok0:tok0 + CH, :], ych[buf][0:CH, DI:2 * DI],
                            deps=evs)
                P[("ychst", buf)] = [sd, so]

            prep_head(0)
            for g in range(8):
                prep_g(0, g)
            for c in range(NCK):
                if c + 1 < NCK:
                    prep_head(c + 1)
                for g in range(8):
                    if c + 1 < NCK:
                        prep_g(c + 1, g)
                    comp_g(c, g)
                comp_tail(c)
            p2b_done = pb.barrier()

        if BUILD_PHASES < 4:
            pb.finalize(block)
            return dram

        ssd2_sc.close()

        f1_sc = E(contextlib.ExitStack())
        FE = f1_sc.enter_context
        xnT2 = FE(nc.sbuf_tensor("xnT2", [128, 16 * T], BF16))
        t1b = [FE(nc.sbuf_tensor(f"f1t{i}", [128, 16 * 256], BF16))
               for i in range(2)]
        hst = [FE(nc.sbuf_tensor(f"f1h{i}", [128, 2048], BF16))
               for i in range(2)]
        b1sb = FE(nc.sbuf_tensor("f1b", [128, HF // 128], F32))
        psF = [FE(nc.psum_tensor(f"f1p{i}", [128, 512], F32)) for i in range(4)]
        l_b1 = pb.dma("sp", b1sb[:], b1c_in, deps=p2b_done)
        l_xnt = pb.dma("pl", xnT2[:, :], xnT_hbm, deps=p2b_done)

        def f1_gen():
            ldg = {}
            mmq = {}
            evq_ = {}
            stf = {}
            for hft in range(HF // 128):
                g, hi = divmod(hft, 2)
                if hi == 0:
                    ldg[g] = pb.dma("pl", t1b[g % 2][:].rearrange(
                        "p (k q) -> p k q", k=16),
                        t1_hbm[:, 256 * g:256 * (g + 1)].rearrange(
                            "(k p) q -> p k q", p=128),
                        deps=[mmq.get((2 * g - 3, 3))]
                        + (p2b_done if g < 2 else []))
                for q in range(4):
                    def mmf_(t, g=g, hi=hi, q=q):
                        r = None
                        for kt in range(16):
                            r = t.matmul(
                                psF[q][:, :],
                                t1b[g % 2][:, 256 * kt + 128 * hi:
                                           256 * kt + 128 * (hi + 1)],
                                xnT2[:, 2048 * kt + 512 * q:
                                     2048 * kt + 512 * (q + 1)],
                                start=(kt == 0), stop=(kt == 15))
                        return r
                    mmq[(hft, q)] = pb.comp("pe", mmf_, deps=[
                        ldg[g], l_xnt, evq_.get((hft - 1, q))])
                    evq_[(hft, q)] = pb.comp("ac", (lambda s, hft=hft, q=q:
                        s.activation(hst[hft % 2][:, 512 * q:512 * (q + 1)],
                                     psF[q][:, :], AF.Gelu,
                                     bias=b1sb[:, hft:hft + 1],
                                     scale=mcol[:, 2:3])),
                        deps=[mmq[(hft, q)], l_b1, stf.get(hft - 2)])
                stf[hft] = pb.dma("sp", h_hbm[128 * hft:128 * (hft + 1), :],
                                  hst[hft % 2][:],
                                  deps=[evq_[(hft, q)] for q in range(4)])
                yield

        f1c = f1_gen()

        def f1pump(n=1):
            for _ in range(n):
                if next(f1c, _SENT) is _SENT:
                    return

        with contextlib.ExitStack() as S:
            ydt = [S.enter_context(nc.sbuf_tensor(f"g0_{i}", [128, DI], BF16)) for i in range(2)]
            yot = [S.enter_context(nc.sbuf_tensor(f"g1_{i}", [128, DI], BF16)) for i in range(2)]
            xst = [S.enter_context(nc.sbuf_tensor(f"g2_{i}", [128, DI], BF16)) for i in range(2)]
            zt = [S.enter_context(nc.sbuf_tensor(f"g3_{i}", [128, DI], BF16)) for i in range(2)]
            ybuf = S.enter_context(nc.sbuf_tensor("g4", [128, DI], BF16))
            zsb = S.enter_context(nc.sbuf_tensor("g5", [128, DI], BF16))
            ynstg = [S.enter_context(nc.sbuf_tensor("g7_0", [128, DI], BF16))]
            msc_ = S.enter_context(nc.sbuf_tensor("g8", [128, NTT], F32))
            ptr = [S.enter_context(nc.psum_tensor(f"g9_{i}", [128, 128], BF16)) for i in range(2)]

            stg_done = [None, None]
            cons = [None, None]
            prev_sq = None
            prev_tr_last = None
            sq2 = None
            for tt in range(NTT):
                b = tt % 2
                r0 = 128 * tt
                l1 = pb.dma("sp", ydt[b][:], yd_hbm[r0:r0 + 128, :],
                            deps=[cons[b]] + (p2b_done if tt < 2 else []))
                l2 = pb.dma("sp", yot[b][:], yo_hbm[r0:r0 + 128, :], deps=[cons[b]])
                l3 = pb.dma("sp", xst[b][:], xs_hbm[r0:r0 + 128, :],
                            deps=[cons[b], stg_done[b]])
                l4 = pb.dma("pl", zt[b][:], z_hbm[r0:r0 + 128, :], deps=[cons[b]])
                f1pump(4)
                e0 = expacsD[0:CH, NH * 2 * tt:NH * (2 * tt + 1)].rearrange(
                    "p w -> p () w").broadcast_to([CH, HD, NH])
                e1 = expacsD[64:128, NH * (2 * tt + 1):NH * (2 * tt + 2)].rearrange(
                    "p w -> p () w").broadcast_to([CH, HD, NH])
                ya = pb.comp("dv", (lambda v, b=b, e=e0: v.tensor_tensor(
                    ybuf[0:CH, :].rearrange("p (a w) -> p a w", w=NH),
                    yot[b][0:CH, :].rearrange("p (a w) -> p a w", w=NH), e, ALU.mult)),
                    deps=[l2, stg_done[b], prev_sq])
                yb = pb.comp("dv", (lambda v, b=b, e=e1: v.tensor_tensor(
                    ybuf[64:128, :].rearrange("p (a w) -> p a w", w=NH),
                    yot[b][64:128, :].rearrange("p (a w) -> p a w", w=NH), e, ALU.mult)),
                    deps=[l2, stg_done[b], prev_sq])
                yc = pb.comp("dv", (lambda v, b=b: v.tensor_add(
                    ybuf[:, :], ybuf[:, :], ydt[b][:, :])), deps=[ya, yb, l1])
                yd2 = pb.comp("dv", (lambda v, b=b: v.tensor_mul(
                    zsb[:, :], xst[b][:, :], dful[:, :])), deps=[l3, yc, prev_sq])
                ye = pb.comp("dv", lambda v: v.tensor_add(ybuf[:, :], ybuf[:, :],
                                                          zsb[:, :]), deps=[yd2])
                zs = pb.comp("ac", (lambda s, b=b: s.activation(
                    zsb[:, :], zt[b][:, :], AF.Silu)), deps=[l4, ye])
                g2 = pb.comp("dv", lambda v: v.tensor_mul(ybuf[:, :], ybuf[:, :],
                                                          zsb[:, :]), deps=[zs])
                cons[b] = g2
                sq2 = pb.comp("ac", (lambda s, tt=tt: s.activation(
                    zsb[:, :], ybuf[:, :], AF.Square,
                    accum_out=msc_[:, tt:tt + 1])), deps=[g2])
                prev_sq = sq2
                yn = pb.comp("dv", (lambda v, b=b: v.tensor_mul(
                    xst[b][:, :], ybuf[:, :], wrep[:, :])),
                    deps=[sq2, yd2, prev_tr_last])
                trs = []
                for k in range(32):
                    tr = pb.comp("pe", (lambda t, k=k, b=b: t.transpose(
                        ptr[k % 2][:, :], xst[b][:, 128 * k:128 * (k + 1)],
                        identsb[:, :])),
                        deps=[yn, trs[k - 2] if k >= 2 else stg_done[1 - b]])
                    ev = pb.comp("dv", (lambda v, k=k: v.tensor_copy(
                        ynstg[0][:, 128 * k:128 * (k + 1)], ptr[k % 2][:, :])), deps=[tr])
                    trs.append(ev)
                prev_tr_last = trs[-1]
                stg_done[b] = pb.dma("sp", ynT_hbm[tt, :, :], ynstg[0][:, :],
                                     deps=trs)
            while next(f1c, _SENT) is not _SENT:
                pass
            r1 = pb.comp("dv", lambda v: v.tensor_scalar(
                msc_[:, :], msc_[:, :], 1.0 / DI, RMS_EPS,
                ALU.mult, ALU.add), deps=[sq2, stg_done[0], stg_done[1]])
            r2 = pb.comp("dv", lambda v: v.reciprocal(msc_[:, :], msc_[:, :]), deps=[r1])
            r3 = pb.comp("ac", lambda s: s.activation(msc_[:, :], msc_[:, :], AF.Sqrt),
                         deps=[r2])
            st_r = pb.dma("sp", rstd_out, msc_[:, :], deps=[r3])
            p3a_done = pb.barrier()
        f1_sc.close()
        ssd_sc.close()

        if BUILD_PHASES < 5:
            pb.finalize(block)
            return dram

        with contextlib.ExitStack() as S:
            wo = S.enter_context(nc.sbuf_tensor("p3w", [128, 32 * 2048], BF16))
            ynb_ = [S.enter_context(nc.sbuf_tensor(f"p3y{i}", [128, 32 * 128], BF16))
                    for i in range(2)]
            sst = [S.enter_context(nc.sbuf_tensor(f"p3s{i}", [128, 2048], BF16))
                   for i in range(2)]
            psS = [S.enter_context(nc.psum_tensor(f"p3p{i}", [128, 512], F32))
                   for i in range(8)]
            l_wo = [pb.dma("pl",
                           wo[:, 8 * 2048 * j:8 * 2048 * (j + 1)].rearrange(
                               "p (kt f) -> p kt f", kt=8),
                           woutT[1024 * j:1024 * (j + 1), :].rearrange(
                               "(kt p) f -> p kt f", p=128), deps=p3a_done)
                    for j in range(4)]
            ldy = {}
            mms = {}
            evs = {}
            sts = {}
            for tt in range(NTT):
                ldy[tt] = pb.dma("sp", ynb_[tt % 2][:], ynT_hbm[tt, :, :],
                                 deps=[mms.get(tt - 2)] + (p3a_done if tt < 2 else []))

                def mall(t, tt=tt):
                    r = None
                    for kt in range(32):
                        for q in range(4):
                            pf = psS[(tt % 2) * 4 + q]
                            r = t.matmul(pf[:, :], ynb_[tt % 2][:, 128 * kt:128 * (kt + 1)],
                                     wo[:, 2048 * kt + 512 * q: 2048 * kt + 512 * (q + 1)],
                                     start=(kt == 0), stop=(kt == 31))
                    return r
                mms[tt] = pb.comp("pe", mall, deps=l_wo + [ldy[tt], evs.get(tt - 2)])

                def evq(s, tt=tt):
                    r = None
                    for q in range(4):
                        pf = psS[(tt % 2) * 4 + q]
                        r = s.copy(sst[tt % 2][:, 512 * q:512 * (q + 1)], pf[:, :])
                    return r
                evs[tt] = pb.comp("ac", evq, deps=[mms[tt], sts.get(tt - 2)])
                sts[tt] = pb.dma("sp", s_out[128 * tt:128 * (tt + 1), :],
                                 sst[tt % 2][:], deps=[evs[tt]])
            p3b_done = pb.barrier()

        if BUILD_PHASES < 8:
            pb.finalize(block)
            return dram

        with contextlib.ExitStack() as S:
            t2b = [S.enter_context(nc.sbuf_tensor("f2t0", [128, 8 * 2048], BF16))]
            hti = [S.enter_context(nc.sbuf_tensor(f"f2h{i}", [128, 8 * 512], BF16))
                   for i in range(2)]
            y2a = S.enter_context(nc.sbuf_tensor("f2y", [128, 16 * 2048], F32))
            y2s = [S.enter_context(nc.sbuf_tensor(f"f2s{i}", [128, 2048], F32))
                   for i in range(2)]
            lmx = S.enter_context(nc.sbuf_tensor("f2m", [128, NTT + 2], F32))
            lrow = S.enter_context(nc.sbuf_tensor("f2r", [1, 128], F32))
            psY = [S.enter_context(nc.psum_tensor(f"f2p{i}", [128, 512], F32))
                   for i in range(8)]
            NBLK = 8
            NKB = HF // 128 // NBLK
            lb = None
            lh = {}
            mmy = {}
            evy = {}
            for blk in range(NBLK):
                lb = pb.dma("sp", t2b[0][:].rearrange(
                    "p (kt f) -> p kt f", kt=NKB),
                    t2_hbm[1024 * blk:1024 * (blk + 1), :].rearrange(
                        "(kt p) f -> p kt f", p=128),
                    deps=[mmy.get((blk - 1, NTT - 1))] + (p3b_done if blk < 1 else []))
                for tt in range(NTT):
                    ts = tt // 4
                    tq = tt % 4
                    if tq == 0:
                        lh[(blk, ts)] = pb.dma("pl" if ts % 2 else "sp",
                            hti[ts % 2][:].rearrange("p (k q) -> p k q", k=NKB),
                            h_hbm[1024 * blk:1024 * (blk + 1),
                                  512 * ts:512 * (ts + 1)].rearrange(
                                "(k p) q -> p k q", p=128),
                            deps=[mmy.get((blk, 4 * ts - 5)),
                                  mmy.get((blk - 1, 4 * ts + 11))]
                            + (p3b_done if blk == 0 and ts < 2 else []))

                    def mall(t, blk=blk, tt=tt, ts=ts, tq=tq):
                        r = None
                        for kt in range(NKB):
                            for q in range(4):
                                pf = psY[(tt % 2) * 4 + q]
                                r = t.matmul(pf[:, :],
                                         hti[ts % 2][:, 512 * kt + 128 * tq:
                                                      512 * kt + 128 * (tq + 1)],
                                         t2b[0][:, 2048 * kt + 512 * q:
                                                2048 * kt + 512 * (q + 1)],
                                         start=(kt == 0), stop=(kt == NKB - 1))
                        return r
                    mmy[(blk, tt)] = pb.comp("pe", mall, deps=[
                        lb, lh[(blk, ts)],
                        evy.get((blk, tt - 2)), evy.get((blk - 1, NTT - 2 + tt))])

                    def evq(v, blk=blk, tt=tt):
                        r = None
                        for q in range(4):
                            pf = psY[(tt % 2) * 4 + q]
                            dst = y2a[:, 2048 * tt + 512 * q: 2048 * tt + 512 * (q + 1)]
                            if blk == 0:
                                r = v.tensor_copy(dst, pf[:, :])
                            else:
                                r = v.tensor_add(dst, dst, pf[:, :])
                        return r
                    evy[(blk, tt)] = pb.comp("dv", evq, deps=[mmy[(blk, tt)],
                                                              evy.get((blk - 1, tt))])
            fin = {}
            for tt in range(NTT):
                f1_ = pb.comp("dv", (lambda v, tt=tt: v.scalar_tensor_tensor(
                    y2s[tt % 2][:], y2a[:, 2048 * tt:2048 * (tt + 1)], mcol[:, 3:4],
                    b2rep[:, :], ALU.mult, ALU.add)),
                    deps=[evy[(NBLK - 1, tt)], fin.get(tt - 2)])
                f2_ = pb.comp("dv", (lambda v, tt=tt: v.reduce_max(
                    lmx[:, tt:tt + 1], y2s[tt % 2][:], AX.X,
                    apply_absolute_value=True)), deps=[f1_])
                fin[tt] = pb.dma("sp", y2f_out[128 * tt:128 * (tt + 1), :],
                                 y2s[tt % 2][:], deps=[f2_])
            mred = pb.comp("dv", lambda v: v.reduce_max(
                lmx[:, NTT:NTT + 1], lmx[:, 0:NTT], AX.X), deps=[fin[NTT - 1],
                                                                 fin[NTT - 2]])
            d_lm0 = pb.dma("pl", scrD, lmx[:, NTT:NTT + 1], deps=[mred])
            d_lm = pb.dma("pl", lrow[0:1, 0:128],
                          scrD.rearrange("p o -> o p"), deps=[d_lm0])
            gmx = pb.comp("dv", lambda v: v.reduce_max(lmx[0:1, NTT + 1:NTT + 2],
                                                       lrow[0:1, 0:128], AX.X),
                          deps=[d_lm])
            st_lm = pb.dma("sp", lmax_out, lmx[0:1, NTT + 1:NTT + 2], deps=[gmx])
            pb.barrier()

        pb.finalize(block)
    return dram


def build_final(nc: bass.Bass):
    dram = {}
    din = lambda n, s, d=F32: _dram(nc, dram, n, s, d, "ExternalInput")
    dout = lambda n, s, d=F32: _dram(nc, dram, n, s, d, "ExternalOutput")
    x_in = din("x", (T, DM))
    y2f = din("y2f", (T, DM))
    s_raw = din("s_raw", (T, DM), BF16)
    rstd = din("rstd_all", (128, NTT))
    maxv = din("maxv", (1, 8))
    out = dout("out", (T, DM))

    ctx = contextlib.ExitStack()
    E = ctx.enter_context
    with ctx:
        pb = PB(nc, E)
        yt = [E(nc.sbuf_tensor(f"by{i}", [128, DM], F32)) for i in range(2)]
        st_ = [E(nc.sbuf_tensor(f"bs{i}", [128, DM], BF16)) for i in range(2)]
        xt = [E(nc.sbuf_tensor(f"bx{i}", [128, DM], F32)) for i in range(2)]
        ot = [E(nc.sbuf_tensor(f"bo{i}", [128, DM], F32)) for i in range(2)]
        gt = E(nc.sbuf_tensor("bg", [128, DM], BF16))
        rst = E(nc.sbuf_tensor("br", [128, NTT], F32))
        mrow = E(nc.sbuf_tensor("bm", [1, 8], F32))
        scol = E(nc.sbuf_tensor("bc", [128, 4], F32))
        pq = E(nc.psum_tensor("bp", [128, 2], F32))
        block = E(nc.Block())
        ones = E(nc.sbuf_tensor("bones", [1, 128], F32))

        oc = pb.comp("dv", lambda v: v.memset(ones[:, :], 1.0))
        l_m = pb.dma("sp", mrow[:, :], maxv)
        l_r = pb.dma("sp", rst[:, :], rstd)
        g1 = pb.comp("dv", lambda v: v.reduce_max(scol[0:1, 0:1], mrow[:, :], AX.X),
                     deps=[l_m])
        g2 = pb.comp("dv", lambda v: v.tensor_scalar(
            scol[0:1, 1:2], scol[0:1, 0:1], 1.0 / QMAX, QEPS, ALU.mult, ALU.add),
            deps=[g1])
        g3 = pb.comp("dv", lambda v: v.reciprocal(scol[0:1, 2:3], scol[0:1, 1:2]),
                     deps=[g2])
        m1 = pb.comp("pe", lambda t: t.matmul(pq[:, 0:1], ones[0:1, 0:128],
                                              scol[0:1, 1:2], start=True, stop=True),
                     deps=[g2, oc])
        m2 = pb.comp("pe", lambda t: t.matmul(pq[:, 1:2], ones[0:1, 0:128],
                                              scol[0:1, 2:3], start=True, stop=True),
                     deps=[g3, m1])
        e1 = pb.comp("dv", lambda v: v.tensor_copy(scol[:, 0:1], pq[:, 0:1]),
                     deps=[m1])
        e2 = pb.comp("dv", lambda v: v.tensor_copy(scol[:, 1:2], pq[:, 1:2]),
                     deps=[m2, e1])
        done = {}
        for tt in range(NTT):
            b = tt % 2
            r0 = 128 * tt
            l1 = pb.dma("sp", yt[b][:], y2f[r0:r0 + 128, :], deps=[done.get(tt - 2)])
            l2 = pb.dma("sp", st_[b][:], s_raw[r0:r0 + 128, :], deps=[done.get(tt - 2)])
            l3 = pb.dma("pl", xt[b][:], x_in[r0:r0 + 128, :], deps=[done.get(tt - 2)])
            q1 = pb.comp("dv", (lambda v, b=b: v.tensor_scalar(
                yt[b][:], yt[b][:], scol[:, 1:2], MROUND, ALU.mult, ALU.add)),
                deps=[l1, e2])
            q2 = pb.comp("dv", (lambda v, b=b: v.tensor_scalar(
                yt[b][:], yt[b][:], -MROUND, -QMAX, ALU.add, ALU.max)), deps=[q1])
            q3 = pb.comp("dv", (lambda v, b=b: v.tensor_scalar_min(
                yt[b][:], yt[b][:], QMAX)), deps=[q2])
            sg = pb.comp("ac", (lambda s, b=b: s.activation(
                gt[:], yt[b][:], AF.Silu, scale=scol[:, 0:1])), deps=[q3, e1,
                                                                     done.get(tt - 1)])
            s2 = pb.comp("dv", (lambda v, b=b, tt=tt: v.tensor_scalar(
                st_[b][:], st_[b][:], rst[:, tt:tt + 1], None, ALU.mult)),
                deps=[l2, l_r])
            o1 = pb.comp("dv", (lambda v, b=b: v.tensor_mul(
                ot[b][:], st_[b][:], gt[:])), deps=[sg, s2])
            o2 = pb.comp("dv", (lambda v, b=b: v.tensor_add(
                ot[b][:], ot[b][:], xt[b][:])), deps=[o1, l3])
            done[tt] = pb.dma("sp", out[r0:r0 + 128, :], ot[b][:], deps=[o2])
        pb.barrier()
        pb.finalize(block)
    return dram


def _perm_ph():
    j = np.arange(DI)
    return (j % NH) * HD + j // NH


def _consts():
    import ml_dtypes
    bf = ml_dtypes.bfloat16
    ident = np.eye(128, dtype=np.float32).astype(bf)
    hh, jj = np.meshgrid(np.arange(NH), np.arange(NH * CH), indexing="ij")
    onehot_hl = (jj // CH == hh).astype(np.float32)
    ll = np.meshgrid(np.arange(CH), np.arange(NH * CH), indexing="ij")
    onehot_l = (ll[1] % CH == ll[0]).astype(np.float32).astype(bf)
    lp, sd = np.meshgrid(np.arange(CH), np.arange(128), indexing="ij")
    maskB = np.where(lp < (sd % 64), MASKNEG, 0.0).astype(np.float32).astype(bf)
    s_, l_ = np.meshgrid(np.arange(CH), np.arange(CH), indexing="ij")
    tril64 = (l_ >= s_).astype(np.float32).astype(bf)
    k_, t_ = np.meshgrid(np.arange(CH), np.arange(CH), indexing="ij")
    triu_inc = (k_ <= t_).astype(np.float32)
    k2, c2 = np.meshgrid(np.arange(NCK), np.arange(NCK), indexing="ij")
    triu_exc = (k2 < c2).astype(np.float32)
    ones_r = np.ones((1, 128), dtype=np.float32)
    return dict(ident=ident, identf=np.eye(128, dtype=np.float32),
                onehot_hl=onehot_hl, onehot_l=onehot_l, maskB=maskB,
                tril64=tril64, triu_inc=triu_inc, triu_exc32=triu_exc, ones_r=ones_r)


def make_inputs_a(core, inputs):
    perm = _perm_ph()
    W_in = inputs["L0_attn_W_in"]
    WinT = np.ascontiguousarray(np.asarray(W_in).T)
    xs_cols = WinT[:, 4096 + perm]
    bc_cols = WinT[:, 8192:8640]
    z_cols = WinT[:, perm]
    winT = np.ascontiguousarray(np.concatenate([xs_cols, bc_cols, z_cols], axis=1))
    WoutT = np.ascontiguousarray(np.asarray(inputs["L0_attn_W_out"]).T)
    woutT = np.ascontiguousarray(WoutT[perm, :])
    w1T = np.ascontiguousarray(np.asarray(inputs["L0_ffn_W1"]).T)
    w2T = np.ascontiguousarray(np.asarray(inputs["L0_ffn_W2"]).T)
    b1 = np.asarray(inputs["L0_ffn_b1"])
    b1c = np.ascontiguousarray(b1.reshape(HF // 128, 128).T)
    b2row = np.asarray(inputs["L0_ffn_b2"]).reshape(1, DM)
    wrow = np.asarray(inputs["L0_attn_out_norm_w"])[perm].reshape(1, DI)
    drow = np.tile(np.asarray(inputs["L0_attn_D"]), HD).reshape(1, DI)
    dtb = np.tile(np.asarray(inputs["L0_attn_dt_bias"]), NCK).reshape(1, NCK * NH)
    alog = np.tile(np.asarray(inputs["L0_attn_A_log"]), NCK).reshape(1, NCK * NH)
    bw = np.asarray(inputs["L0_attn_B_norm_w"])
    cw = np.asarray(inputs["L0_attn_C_norm_w"])
    bwe = np.tile(bw[0::2], NCK).reshape(1, NCK * HD)
    bwo = np.tile(bw[1::2], NCK).reshape(1, NCK * HD)
    cwe = np.tile(cw[0::2], NCK).reshape(1, NCK * HD)
    cwo = np.tile(cw[1::2], NCK).reshape(1, NCK * HD)
    im = dict(x=np.ascontiguousarray(np.asarray(inputs["L0_attn_u"])[core]),
              winT=winT, woutT=woutT, w1T=w1T, w2T=w2T, b1c=b1c, b2row=b2row,
              wrow=wrow, drow=drow, dtb_row=dtb, alog_row=alog,
              bwe_row=bwe, bwo_row=bwo, cwe_row=cwe, cwo_row=cwo)
    im.update(_consts())
    return {k: np.ascontiguousarray(v) for k, v in im.items()}


def kernel(**inputs):
    ncA = bass.Bass("TRN2", target_bir_lowering=False, debug=False)
    build_main(ncA)
    in_maps = [make_inputs_a(c, inputs) for c in range(8)]
    resA = run_bass_kernel_spmd(ncA, in_maps, core_ids=list(range(8)))

    maxv = np.concatenate([np.asarray(resA.results[c]["lmax"]).reshape(1, 1)
                           for c in range(8)], axis=1)
    ncB = bass.Bass("TRN2", target_bir_lowering=False, debug=False)
    build_final(ncB)
    in_maps_b = []
    for c in range(8):
        in_maps_b.append(dict(
            x=np.ascontiguousarray(np.asarray(inputs["L0_attn_u"])[c]),
            y2f=np.asarray(resA.results[c]["y2f"]),
            s_raw=np.asarray(resA.results[c]["s_raw"]),
            rstd_all=np.asarray(resA.results[c]["rstd_all"]),
            maxv=maxv))
    resB = run_bass_kernel_spmd(ncB, in_maps_b, core_ids=list(range(8)))
    out = np.stack([np.asarray(resB.results[c]["out"]) for c in range(8)], axis=0)
    return out.astype(np.float32)
